# version 17
# baseline (speedup 1.0000x reference)
import contextlib

import numpy as np
import ml_dtypes

import concourse.bass as bass
import concourse.mybir as mybir
from concourse.bass_utils import run_bass_kernel_spmd
from concourse.bass import IndirectOffsetOnAxis, ds

F32 = mybir.dt.float32
BF16 = mybir.dt.bfloat16
U32 = mybir.dt.uint32
I32 = mybir.dt.int32
I16 = mybir.dt.int16
U8 = mybir.dt.uint8
AF = mybir.ActivationFunctionType
ALU = mybir.AluOpType
bfloat16 = ml_dtypes.bfloat16

NCORES = 8
D = 1024
SEQ = 8192
B = 2
BLK = 512
TPC = 2048
NT = TPC // 128
BLOCKS = [sorted(s) for s in ([0, 15, 7, 8], [1, 14, 6, 9], [2, 13, 5, 10], [3, 12, 4, 11])]
LN_EPS = 1e-5

ENGS = ("sync", "tensor", "vector", "scalar", "gpsimd")



class Plan:

    def __init__(self, nc):
        self.nc = nc
        self.ops = {e: [] for e in ENGS}
        self.sem_handles = {}
        self.cnt = {}
        self._stack = contextlib.ExitStack()

    def sem(self, name):
        assert name not in self.sem_handles
        self.sem_handles[name] = self._stack.enter_context(self.nc.semaphore(name))
        self.cnt[name] = 0
        return name

    def emit(self, eng, fn):
        self.ops[eng].append(fn)

    def op(self, eng, make, sem=None, k=1):
        h = self.sem_handles[sem] if sem else None
        if sem:
            self.cnt[sem] += k

        def fn(e):
            i = make(e)
            if h is not None:
                i.then_inc(h, k)
        self.emit(eng, fn)
        if sem:
            return self.cnt[sem]

    def wait(self, eng, sem, value=None):
        v = self.cnt[sem] if value is None else value
        if v <= 0:
            return
        h = self.sem_handles[sem]
        self.emit(eng, lambda e: e.wait_ge(h, v))

    def val(self, sem):
        return self.cnt[sem]

    def barrier(self, sem_suffix, producers_done):
        for eng in ENGS:
            for sem, v in producers_done:
                self.wait(eng, sem, v)

    def run(self):
        with self._stack:
            with self.nc.Block() as block:
                for name in ENGS:
                    def body(eng, _n=name):
                        for fn in self.ops[_n]:
                            fn(eng)
                    getattr(block, name)(body)


class Arena:

    def __init__(self, nc, stack, nbytes=206 * 1024, name="arena"):
        self.t = stack.enter_context(nc.sbuf_tensor(name, [128, nbytes], U8))
        self.off = 0
        self.nbytes = nbytes
        self.marks = []
        self.peak = 0

    def alloc(self, shape, dtype, align=128):
        assert shape[0] <= 128
        esz = {F32: 4, BF16: 2, U32: 4, I32: 4, I16: 2, U8: 1}[dtype]
        n = 1
        for s in shape[1:]:
            n *= s
        self.off = (self.off + align - 1) // align * align
        start = self.off
        self.off += n * esz
        self.peak = max(self.peak, self.off)
        limit = self.nbytes - getattr(self, "reserved", 0)
        assert self.off <= limit, f"arena overflow: {self.off} > {limit}"
        v = self.t[0:shape[0], start:start + n * esz]
        if dtype != U8:
            v = v.bitcast(dtype)
        if len(shape) > 2:
            dims = " ".join(f"d{i}" for i in range(1, len(shape)))
            v = v.rearrange(f"p ({dims}) -> p {dims}",
                            **{f"d{i}": shape[i] for i in range(1, len(shape) - 1)})
        return v

    def mark(self):
        self.marks.append(self.off)

    def release(self):
        self.off = self.marks.pop()


class ArenaTop:

    def __init__(self, ar, size):
        ar.reserved = size
        self.par = ar
        self.base = ar.nbytes - size
        self.off = self.base
        self.peak = self.base
        self.nbytes = ar.nbytes
        self.t = ar.t
        self.marks = []

    alloc = Arena.alloc
    mark = Arena.mark
    release = Arena.release



def shard_rows(x_full):
    shards = []
    for c in range(NCORES):
        b = c // 4
        rows = [x_full[b, blk * BLK:(blk + 1) * BLK] for blk in BLOCKS[c % 4]]
        shards.append(np.ascontiguousarray(np.concatenate(rows, axis=0)))
    return shards


def unshard_rows(shards):
    out = np.zeros((B, SEQ, D), dtype=shards[0].dtype)
    for c in range(NCORES):
        b = c // 4
        for i, blk in enumerate(BLOCKS[c % 4]):
            out[b, blk * BLK:(blk + 1) * BLK] = shards[c][i * BLK:(i + 1) * BLK]
    return out


def const_inputs():
    return {
        "ident_bf": np.eye(128, dtype=bfloat16),
        "ident_f32": np.eye(128, dtype=np.float32),
        "ones_f32": np.ones((1, 128), dtype=np.float32),
    }


CONST_SPECS = [("ident_bf", [128, 128], BF16),
               ("ident_f32", [128, 128], F32),
               ("ones_f32", [1, 128], F32)]


def declare_consts(nc):
    return {name: nc.dram_tensor(name, shape, dt, kind="ExternalInput").ap()
            for name, shape, dt in CONST_SPECS}


def load_consts(p, ar, cin):
    out = {}
    s = p.sem("ldc")
    for name, shape, dt in CONST_SPECS:
        out[name] = ar.alloc(shape, dt)
        p.op("sync", lambda e, d=out[name], src=cin[name]: e.dma_start(out=d, in_=src),
             sem=s, k=16)
    out["eps128"] = ar.alloc([128, 1], F32)
    p.op("vector", lambda e: e.memset(out["eps128"], LN_EPS), sem=s, k=1)
    out["onesc_f32"] = ar.alloc([128, 1], F32)
    p.op("vector", lambda e: e.memset(out["onesc_f32"], 1.0), sem=s, k=1)
    for eng in ("tensor", "vector", "scalar", "gpsimd"):
        p.wait(eng, s)
    return out



def emit_transposes_bf(p, psum, cons, src, dst_fn, rtiles, ctiles, s_tmm, s_tev,
                       psum_byte0=2048 * 4):
    nslot = 4
    for rt in range(rtiles):
        for ct in range(ctiles):
            slot = p.val(s_tmm) % nslot
            ps = psum.t_u8[:, psum_byte0 + slot * 2048: psum_byte0 + slot * 2048 + 256].bitcast(BF16)
            if p.val(s_tmm) >= nslot:
                p.wait("tensor", s_tev, p.val(s_tev) - nslot + 1)
            p.op("tensor", lambda e, ps=ps, rt=rt, ct=ct: e.transpose(
                ps, src[:, rt, ct * 128:(ct + 1) * 128], cons["ident_bf"]), sem=s_tmm)
            p.wait("vector", s_tmm)
            p.op("vector", lambda e, ps=ps, d=dst_fn(rt, ct): e.tensor_copy(d, ps),
                 sem=s_tev)


class Psum:

    def __init__(self, nc, stack):
        self.h = stack.enter_context(nc.psum_tensor("ps", [128, 4096], F32))
        self.f32 = self.h.ap()
        self.t_u8 = self.f32.bitcast(U8)

    def f(self, col0, ncols):
        return self.f32[:, col0:col0 + ncols]


def emit_prenorm_tile(p, ar, x_t, scr, scrb, stats, mr, s_ev, cons=None):
    p.op("vector", lambda e: e.bn_stats(stats[:, 0:6], x_t[:, 0:512]), sem=s_ev)
    p.op("vector", lambda e: e.bn_stats(stats[:, 6:12], x_t[:, 512:1024]), sem=s_ev)
    p.wait("vector", s_ev)
    p.op("vector", lambda e: e.bn_aggr(mr[:, 0:2], stats[:, 0:12]), sem=s_ev)
    p.wait("scalar", s_ev)
    p.op("scalar", lambda e: e.activation(
        stats[:, 12:13], mr[:, 1:2], AF.Sqrt, bias=cons["eps128"], scale=1.0), sem=s_ev)
    p.wait("vector", s_ev)
    p.op("vector", lambda e: e.reciprocal(stats[:, 13:14], stats[:, 12:13]), sem=s_ev)
    p.wait("vector", s_ev)
    p.op("vector", lambda e: e.tensor_scalar(
        scr, x_t, mr[:, 0:1], stats[:, 13:14], ALU.subtract, ALU.mult), sem=s_ev)
    p.wait("scalar", s_ev)
    p.op("scalar", lambda e: e.copy(scrb, scr), sem=s_ev)



def emit_wT_to_hbm(p, ar, psum, cons, w, R, C, hbm, group, sems, psum_byte0=8192,
                   load_eng="gpsimd", store_eng="scalar", half_rt_cap=None,
                   nraw=2):
    if len(sems) == 5:
        s_w, s_tmm, s_tev, s_wst, s_wst2 = sems
        wsts = [s_wst, s_wst2]
    else:
        s_w, s_tmm, s_tev, s_wst = sems
        wsts = [s_wst, s_wst]
    RT, CT = R // 128, C // 128
    half_rt = max(1, min(RT, 16384 // C))
    if half_rt_cap is not None:
        half_rt = min(half_rt, half_rt_cap)
    nhalf = (RT + half_rt - 1) // half_rt
    nb = min(nraw, nhalf)
    ar.mark()
    raw = [ar.alloc([128, half_rt, C], BF16) for _ in range(nb)]
    chunk = [ar.alloc([128, CT * 128 if group == "r" else half_rt * 128], BF16)
             for _ in range(2)]
    nch = 0
    tmm_after_half = {}
    w_t = w.rearrange("(t p) c -> p t c", p=128)
    halves = list(range(0, RT, half_rt))
    s_w_base = p.val(s_w)
    for hidx in range(nb):
        h0 = halves[hidx]
        h1 = min(h0 + half_rt, RT)
        p.op(load_eng, lambda e, h0=h0, h1=h1, r=raw[hidx % nb]: e.dma_start(
            out=r[:, 0:h1 - h0, :], in_=w_t[:, h0:h1, :]), sem=s_w, k=16)
    for hidx, h0 in enumerate(halves):
        h1 = min(h0 + half_rt, RT)
        rbuf = raw[hidx % nb]
        p.wait("tensor", s_w, s_w_base + 16 * (hidx + 1))
        outer = range(h0, h1) if group == "r" else range(CT)
        inner = range(CT) if group == "r" else range(h0, h1)
        for g in outer:
            cb = chunk[nch % 2]
            sw = wsts[nch % 2]
            if nch >= 2:
                p.wait("vector", sw, p.val(sw))
            nch += 1
            for i_, it in enumerate(inner):
                rt, ct = (g, it) if group == "r" else (it, g)
                slot = p.val(s_tmm) % 4
                ps = psum.t_u8[:, psum_byte0 + slot * 2048: psum_byte0 + slot * 2048 + 256].bitcast(BF16)
                if p.val(s_tmm) >= 4:
                    p.wait("tensor", s_tev, p.val(s_tev) - 3)
                p.op("tensor", lambda e, ps=ps, rt=rt, ct=ct, h0=h0, rbuf=rbuf: e.transpose(
                    ps, rbuf[:, rt - h0, ct * 128:(ct + 1) * 128], cons["ident_bf"]),
                    sem=s_tmm)
                p.wait("vector", s_tmm)
                p.op("vector", lambda e, ps=ps, cb=cb, i_=i_: e.tensor_copy(
                    cb[:, i_ * 128:(i_ + 1) * 128], ps), sem=s_tev)
            p.wait(store_eng, s_tev)
            if group == "r":
                dst = hbm[g]
            else:
                dst = hbm[g][:, h0 * 128:h1 * 128]
            p.wait(store_eng, sw, p.val(sw))
            p.op(store_eng, lambda e, cb=cb, dst=dst, n=(len(list(inner))) * 128:
                 e.dma_start(out=dst, in_=cb[:, 0:n]), sem=sw, k=16)
        tmm_after_half[hidx] = p.val(s_tmm)
        if hidx + nb < nhalf:
            nh0 = halves[hidx + nb]
            nh1 = min(nh0 + half_rt, RT)
            p.wait(load_eng, s_tmm, tmm_after_half[hidx])
            p.op(load_eng, lambda e, nh0=nh0, nh1=nh1, rbuf=rbuf: e.dma_start(
                out=rbuf[:, 0:nh1 - nh0, :], in_=w_t[:, nh0:nh1, :]), sem=s_w, k=16)
    for sw in set(wsts):
        p.wait(load_eng, sw)
        p.wait("vector", sw)
        p.wait(store_eng, sw)
    ar.release()


def emit_l1f(p, ar, psum, cons, xin_sb, wp, pref="f1", tbs=4, stage=9):
    nc = p.nc
    s_tmm = p.sem(pref + "tmm")
    s_tev = p.sem(pref + "tev")
    s_mm = p.sem(pref + "mm")
    s_ev = p.sem(pref + "ev")
    s_y = p.sem(pref + "y")
    s_lif = p.sem(pref + "lif")
    s_ws = [p.sem(pref + "ws0"), p.sem(pref + "ws1")]

    wfcT_hbm = wp["wfcT"]
    wprT_hbm = wp["wprT"]
    wfcb = [ar.alloc([128, 8, 128], BF16) for _ in range(2)]
    wprb = [ar.alloc([128, 1024], BF16) for _ in range(2)]
    p.wait("sync", wp["s_wst"])
    p.wait("sync", wp["s_wst2"])
    nstep = 5
    for tb in range(tbs):
        ar.mark()
        xln_fm = ar.alloc([128, 8, 512], BF16)
        h2 = ar.alloc([128, 32, 512], BF16)
        scr = ar.alloc([128, 1024], F32)
        scrb = ar.alloc([128, 1024], BF16)
        stats = ar.alloc([128, 16], F32)
        mr = ar.alloc([128, 2], F32)
        ysb = ar.alloc([128, 1024], F32)
        memt = ar.alloc([128, 1024], F32)
        acct = ar.alloc([128, 1024], F32)
        spkt = ar.alloc([128, 1024], F32)

        for ti in range(4):
            t = tb * 4 + ti
            if ti > 0 or tb > 0:
                p.wait("scalar", s_tmm)
            emit_prenorm_tile(p, ar, xin_sb[:, t, :], scr, scrb, stats, mr, s_ev, cons)
            p.wait("tensor", s_ev)
            if tb > 0 and ti == 0:
                p.wait("tensor", s_y, p.val(s_y))
            for dt in range(8):
                slot = dt % 2
                ps = psum.t_u8[:, 12288 + slot * 2048: 12288 + slot * 2048 + 256].bitcast(BF16)
                if ti * 8 + dt >= 2:
                    p.wait("tensor", s_tev, p.val(s_tev) - 1)
                p.op("tensor", lambda e, ps=ps, dt=dt: e.transpose(
                    ps, scrb[:, dt * 128:(dt + 1) * 128], cons["ident_bf"]), sem=s_tmm)
                p.wait("vector", s_tmm)
                p.op("vector", lambda e, ps=ps, dt=dt, ti=ti: e.tensor_copy(
                    xln_fm[:, dt, ti * 128:(ti + 1) * 128], ps), sem=s_tev)

        p.wait("tensor", s_tev)
        p.wait("sync", s_mm)
        if stage < 2:
            ar.release()
            continue
        for ft in range(32):
            wb = wfcb[ft % 2]
            if ft >= 2:
                p.wait("sync", s_mm, p.val(s_mm) - 1)
            p.op("sync", lambda e, wb=wb, ft=ft: e.dma_start(
                out=wb, in_=wfcT_hbm[ft].rearrange("p (t c) -> p t c", t=8)),
                sem=s_ws[ft % 2], k=16)
            p.wait("tensor", s_ws[ft % 2])
            ps = psum.f((ft % 2) * 512, 512)
            if ft >= 2:
                p.wait("tensor", s_ev, p.val(s_ev) - 3)
            for dt in range(8):
                p.op("tensor", lambda e, ps=ps, wb=wb, dt=dt: e.matmul(
                    ps, wb[:, dt, :], xln_fm[:, dt, :],
                    start=(dt == 0), stop=(dt == 7)),
                    sem=(s_mm if dt == 7 else None))
            p.wait("vector", s_mm)
            p.op("vector", lambda e, ps=ps, ft=ft: e.tensor_scalar(
                h2[:, ft, :], ps, 0.0, None, ALU.max, ALU.bypass), sem=s_ev)
            p.wait("vector", s_ev)
            p.op("vector", lambda e, ft=ft: e.tensor_tensor(
                h2[:, ft, :], h2[:, ft, :], h2[:, ft, :], ALU.mult), sem=s_ev)

        if stage < 3:
            ar.release()
            continue
        p.wait("tensor", s_ev)
        p.wait("sync", s_mm)
        for ft in range(32):
            wb = wprb[ft % 2]
            if ft >= 2:
                p.wait("sync", s_mm, p.val(s_mm) - 8)
            p.op("sync", lambda e, wb=wb, ft=ft: e.dma_start(
                out=wb, in_=wprT_hbm[ft]), sem=s_ws[ft % 2], k=16)
            p.wait("tensor", s_ws[ft % 2])
            for ti in range(4):
                for nchk in range(2):
                    ps = psum.f((ti * 2 + nchk) * 512, 512)
                    p.op("tensor", lambda e, ps=ps, wb=wb, ft=ft, ti=ti, nchk=nchk: e.matmul(
                        ps, h2[:, ft, ti * 128:(ti + 1) * 128],
                        wb[:, nchk * 512:(nchk + 1) * 512],
                        start=(ft == 0), stop=(ft == 31)),
                        sem=s_mm)
        if stage < 4:
            ar.release()
            continue
        for ti in range(4):
            t = tb * 4 + ti
            p.wait("scalar", s_mm)
            p.wait("scalar", s_lif)
            for nchk in range(2):
                ps = psum.f((ti * 2 + nchk) * 512, 512)
                p.op("scalar", lambda e, ps=ps, nchk=nchk: e.copy(
                    ysb[:, nchk * 512:(nchk + 1) * 512], ps), sem=s_y)
            p.wait("vector", s_y)
            p.op("vector", lambda e: e.tensor_scalar(
                acct, ysb, 1.0, None, ALU.is_gt, ALU.bypass), sem=s_lif)
            p.wait("vector", s_lif)
            p.op("vector", lambda e: e.tensor_copy(spkt, acct), sem=s_lif)
            p.op("vector", lambda e: e.tensor_copy(memt, ysb), sem=s_lif)
            for st in range(1, nstep):
                p.wait("vector", s_lif)
                p.op("vector", lambda e: e.scalar_tensor_tensor(
                    memt, memt, 0.9, ysb, ALU.mult, ALU.add), sem=s_lif)
                p.wait("vector", s_lif)
                p.op("vector", lambda e: e.tensor_tensor(
                    memt, memt, spkt, ALU.subtract), sem=s_lif)
                p.wait("vector", s_lif)
                p.op("vector", lambda e: e.tensor_scalar(
                    spkt, memt, 1.0, None, ALU.is_gt, ALU.bypass), sem=s_lif)
                p.wait("vector", s_lif)
                p.op("vector", lambda e: e.tensor_tensor(
                    acct, acct, spkt, ALU.add), sem=s_lif)
            p.wait("vector", s_lif)
            p.op("vector", lambda e, t=t: e.scalar_tensor_tensor(
                xin_sb[:, t, :], acct, 0.2, xin_sb[:, t, :], ALU.mult, ALU.add),
                sem=s_lif)
        ar.release()
    return s_lif



def emit_l0a(p, ar, psum, cons, x_sb, wq, wk, wv, pref="a0", nchunks=4, stage=9,
             prelude=None):
    nc = p.nc
    s_w = p.sem(pref + "w")
    s_tmm = p.sem(pref + "tmm")
    s_tev = p.sem(pref + "tev")
    s_pn = p.sem(pref + "pn")
    s_qkv = p.sem(pref + "qkv")
    s_qev = p.sem(pref + "qev")
    s_sc = p.sem(pref + "sc")
    s_ex = p.sem(pref + "ex")
    s_av = p.sem(pref + "av")
    s_oev = p.sem(pref + "oev")
    s_oc = p.sem(pref + "oc")
    s_ot = p.sem(pref + "ot")
    s_ov = p.sem(pref + "ov")
    s_nrm = p.sem(pref + "nrm")
    s_yst = p.sem(pref + "yst")
    s_cc = p.sem(pref + "cc")
    s_fin = p.sem(pref + "fin")

    y_hbm = nc.dram_tensor(pref + "_y", [TPC, D], F32).ap()
    cc_in = nc.dram_tensor(pref + "_ccin", [1, 16], F32)
    cc_out = nc.dram_tensor(pref + "_ccout", [1, 16], F32)
    groups = [[0, 1, 2, 3], [4, 5, 6, 7]]

    wT = {}
    for name in ("q", "k", "v"):
        wT[name] = ar.alloc([128, 8, 1024], BF16)
    ar.mark()
    raw = ar.alloc([128, 8, 1024], BF16)
    for name, w in (("q", wq), ("k", wk), ("v", wv)):
        if name != "q":
            p.wait("gpsimd", s_tmm)
        p.op("gpsimd", lambda e, w=w: e.dma_start(
            out=raw, in_=w.rearrange("(t p) c -> p t c", p=128)), sem=s_w, k=16)
        p.wait("tensor", s_w)
        emit_transposes_bf(p, psum, cons, raw,
                           lambda rt, ct, name=name: wT[name][:, ct, rt * 128:(rt + 1) * 128],
                           8, 8, s_tmm, s_tev)
    p.wait("gpsimd", s_tmm)
    ar.release()
    if prelude is not None:
        p.wait("tensor", s_tev)
        gates = prelude()
        p.barrier("wp", gates)

    bnstash = ar.alloc([128, 192], F32)
    mv2 = ar.alloc([128, 2], F32)
    msq = ar.alloc([128, 1], F32)
    red_sb = ar.alloc([1, 16], F32)
    bcast_sb = ar.alloc([128, 2], F32)
    onescol_bf = ar.alloc([128, 1], BF16)
    p.op("vector", lambda e: e.memset(onescol_bf, 1.0), sem=s_pn)
    p.op("vector", lambda e: e.memset(red_sb, 0.0), sem=s_pn)

    kbs = [list(range(i, min(i + 3, 64))) for i in range(0, 64, 3)]
    ptb_last_read = {0: 0, 1: 0}
    exp_read = {0: 0, 1: 0}

    for c in range(nchunks):
        ar.mark()
        xln_fm = ar.alloc([128, 8, 512], BF16)
        qc = ar.alloc([128, 8192], BF16)
        kc = ar.alloc([128, 8192], BF16)
        vbuf = ar.alloc([128, 4, 16, 96], BF16)
        ptb = [ar.alloc([128, 1536], BF16) for _ in range(2)]
        stag = ar.alloc([128, 512], F32)
        ych = ar.alloc([128, 4, 1024], F32)
        scr = ar.alloc([128, 1024], F32)
        scrb = ar.alloc([128, 1024], BF16)
        stats = ar.alloc([128, 16], F32)
        mr = ar.alloc([128, 2], F32)
        mrstash = ar.alloc([128, 8], F32)
        sq = ar.alloc([64, 512], BF16)
        umax = ar.alloc([1, 8], F32)
        negu = ar.alloc([128, 1], F32)
        rc = ar.alloc([128, 1], F32)

        p.wait("tensor", s_ex)
        p.wait("tensor", s_ov)
        p.wait("vector", s_qkv)

        for ti in range(4):
            t = c * 4 + ti
            if t > 0:
                p.wait("scalar", s_tmm)
            emit_prenorm_tile(p, ar, x_sb[:, t, :], scr, scrb, stats, mr, s_pn, cons)
            p.wait("vector", s_pn)
            p.op("vector", lambda e, ti=ti: e.tensor_copy(
                mrstash[:, 2 * ti:2 * ti + 1], mr[:, 0:1]), sem=s_pn)
            p.op("vector", lambda e, ti=ti: e.tensor_copy(
                mrstash[:, 2 * ti + 1:2 * ti + 2], stats[:, 13:14]), sem=s_pn)
            p.wait("tensor", s_pn)
            for dt in range(8):
                slot = dt % 2
                ps = psum.t_u8[:, 12288 + slot * 2048: 12288 + slot * 2048 + 256].bitcast(BF16)
                if p.val(s_tmm) >= 2:
                    p.wait("tensor", s_tev, p.val(s_tev) - 1)
                p.op("tensor", lambda e, ps=ps, dt=dt: e.transpose(
                    ps, scrb[:, dt * 128:(dt + 1) * 128], cons["ident_bf"]), sem=s_tmm)
                p.wait("vector", s_tmm)
                p.op("vector", lambda e, ps=ps, dt=dt, ti=ti: e.tensor_copy(
                    xln_fm[:, dt, ti * 128:(ti + 1) * 128], ps), sem=s_tev)

        p.wait("tensor", s_oc)
        p.wait("vector", s_sc)
        p.wait("vector", s_av)
        p.wait("vector", s_ot)
        p.wait("vector", s_ov)
        p.wait("scalar", s_av)
        p.op("vector", lambda e: e.memset(stag, 0.0), sem=s_pn)
        p.op("vector", lambda e: e.memset(vbuf, 1.0), sem=s_pn)
        p.op("vector", lambda e: e.memset(qc[64:128, :], 0.0), sem=s_pn)
        p.op("vector", lambda e: e.memset(kc[64:128, :], 0.0), sem=s_pn)

        p.wait("tensor", s_tev)
        nstrip = 0
        for name, dstbuf in (("q", qc), ("k", kc)):
            for g in range(16):
                ps = psum.f((nstrip % 2) * 512, 512)[0:64, :]
                if nstrip >= 2:
                    p.wait("tensor", s_qev, p.val(s_qev) - 1)
                for dt in range(8):
                    p.op("tensor", lambda e, ps=ps, name=name, g=g, dt=dt: e.matmul(
                        ps, wT[name][:, dt, g * 64:(g + 1) * 64], xln_fm[:, dt, :],
                        start=(dt == 0), stop=(dt == 7)),
                        sem=(s_qkv if dt == 7 else None))
                p.wait("vector", s_qkv)
                p.op("vector", lambda e, ps=ps, dstbuf=dstbuf, g=g: e.tensor_copy(
                    dstbuf[0:64, g * 512:(g + 1) * 512], ps), sem=s_qev)
                nstrip += 1
        for tb in range(4):
            for dvc in range(2):
                ps = psum.f((nstrip % 2) * 512, 512)
                if nstrip >= 2:
                    p.wait("tensor", s_qev, p.val(s_qev) - 1)
                for dt in range(8):
                    p.op("tensor", lambda e, ps=ps, tb=tb, dvc=dvc, dt=dt: e.matmul(
                        ps, xln_fm[:, dt, tb * 128:(tb + 1) * 128],
                        wT["v"][:, dt, dvc * 512:(dvc + 1) * 512],
                        start=(dt == 0), stop=(dt == 7)),
                        sem=(s_qkv if dt == 7 else None))
                p.wait("vector", s_qkv)
                dst = vbuf[:, tb, dvc * 8:(dvc + 1) * 8, 0:64]
                p.op("vector", lambda e, ps=ps, dst=dst: e.tensor_copy(dst, ps), sem=s_qev)
                nstrip += 1

        if stage < 2:
            ar.release()
            continue

        p.op("vector", lambda e: e.memset(umax, 0.0), sem=s_nrm)
        p.wait("vector", s_qev)
        p.op("vector", lambda e: e.memset(kc[64:65, :], 1.0), sem=s_nrm)
        p.wait("scalar", s_qev)
        for blk in range(16):
            p.wait("scalar", s_nrm)
            p.op("scalar", lambda e, blk=blk: e.activation(
                sq, kc[0:64, blk * 512:(blk + 1) * 512], AF.Square,
                bias=0.0, scale=1.0), sem=s_nrm)
            ps = psum.f(1024, 512)[0:1, :]
            p.wait("tensor", s_nrm)
            p.op("tensor", lambda e, ps=ps: e.matmul(
                ps, onescol_bf[0:64, :], sq[0:64, :], start=True, stop=True),
                sem=s_nrm)
            p.wait("vector", s_nrm)
            p.op("vector", lambda e, ps=ps: e.tensor_reduce(
                umax[0:1, 2:3], ps, mybir.AxisListType.X, ALU.max), sem=s_nrm)
            p.wait("vector", s_nrm)
            p.op("vector", lambda e: e.tensor_tensor(
                umax[0:1, 0:1], umax[0:1, 0:1], umax[0:1, 2:3], ALU.max), sem=s_nrm)
        p.wait("scalar", s_nrm)
        p.op("scalar", lambda e: e.activation(
            umax[0:1, 4:5], umax[0:1, 0:1], AF.Sqrt, bias=0.0, scale=1.0), sem=s_nrm)
        ps_nb = psum.f(1024, 128)[:, 0:1]
        p.wait("tensor", s_nrm)
        p.op("tensor", lambda e, ps=ps_nb: e.matmul(
            ps, cons["ones_f32"], umax[0:1, 4:5], start=True, stop=True), sem=s_nrm)
        p.wait("vector", s_nrm)
        p.op("vector", lambda e, ps=ps_nb: e.tensor_copy(negu, ps), sem=s_nrm)
        for g in range(16):
            p.wait("scalar", s_nrm)
            p.op("scalar", lambda e, g=g: e.activation(
                sq, qc[0:64, g * 512:(g + 1) * 512], AF.Square,
                bias=0.0, scale=1.0), sem=s_nrm)
            psq = psum.f(1536, 512)[64:65, :]
            p.wait("tensor", s_nrm)
            p.op("tensor", lambda e, psq=psq: e.matmul(
                psq, onescol_bf[0:64, :], sq[0:64, :], start=True, stop=True,
                tile_position=(0, 64)), sem=s_nrm)
            p.wait("scalar", s_nrm)
            p.op("scalar", lambda e, psq=psq: e.activation(
                scr[64:65, 0:512], psq, AF.Sqrt, bias=0.0, scale=1.0), sem=s_nrm)
            p.wait("vector", s_nrm)
            p.op("vector", lambda e, g=g: e.tensor_scalar(
                qc[64:65, g * 512:(g + 1) * 512], scr[64:65, 0:512],
                negu[64:65, 0:1], -1.0, ALU.mult, ALU.subtract), sem=s_nrm)
        p.wait("vector", s_nrm)
        p.op("vector", lambda e: e.tensor_scalar(
            qc[64:65, :], qc[64:65, :], -1.0, None, ALU.mult, ALU.bypass), sem=s_nrm)

        if stage < 3:
            ar.release()
            continue

        p.wait("tensor", s_nrm)
        for qg in range(16):
            otps = psum.f(0, 512)
            p.wait("tensor", s_oc)
            nb = len(kbs)
            sc_done = {}
            for it in range(nb + 1):
                if it < nb:
                    bi, batch = it, kbs[it]
                    stps = psum.f(512 + (bi % 2) * 1536, 1536)
                    p.wait("tensor", s_ex, exp_read[bi % 2])
                    for j, kb in enumerate(batch):
                        p.op("tensor", lambda e, stps=stps, j=j, kb=kb, qg=qg: e.matmul(
                            stps[:, j * 512:(j + 1) * 512],
                            kc[:, kb * 128:(kb + 1) * 128],
                            qc[:, qg * 512:(qg + 1) * 512],
                            start=True, stop=True),
                            sem=(s_sc if j == len(batch) - 1 else None))
                    sc_done[bi] = p.val(s_sc)
                if it >= 1:
                    bi, batch = it - 1, kbs[it - 1]
                    stps = psum.f(512 + (bi % 2) * 1536, 1536)
                    ptbuf = ptb[bi % 2]
                    p.wait("scalar", s_sc, sc_done[bi])
                    p.wait("scalar", s_nrm)
                    p.wait("scalar", s_av, ptb_last_read[bi % 2])
                    n = len(batch) * 512
                    p.op("scalar", lambda e, stps=stps, ptbuf=ptbuf, n=n: e.activation(
                        ptbuf[:, 0:n], stps[:, 0:n], AF.Exp, bias=0.0, scale=1.0),
                        sem=s_ex)
                    exp_read[bi % 2] = p.val(s_ex)
                    p.wait("tensor", s_ex)
                    for j, kb in enumerate(batch):
                        gk, tbk = kb // 4, kb % 4
                        vsl = vbuf[:, tbk, gk, :]
                        first = (bi == 0 and j == 0)
                        last = (bi == nb - 1 and j == len(batch) - 1)
                        p.op("tensor", lambda e, vsl=vsl, otps=otps, ptbuf=ptbuf, j=j,
                             first=first, last=last: e.matmul(
                                 otps[0:96, :], vsl, ptbuf[:, j * 512:(j + 1) * 512],
                                 start=first, stop=last), sem=s_av)
                    ptb_last_read[bi % 2] = p.val(s_av)
            p.wait("vector", s_av)
            p.wait("vector", s_ot)
            p.op("vector", lambda e, otps=otps: e.tensor_copy(
                stag[0:65, :], otps[0:65, :]), sem=s_oc)
            p.wait("tensor", s_oc)
            p.wait("tensor", s_ov)
            for tbq in range(4):
                px = psum.f(3584, 512)[:, tbq * 128:(tbq + 1) * 128]
                p.op("tensor", lambda e, px=px, tbq=tbq: e.transpose(
                    px, stag[:, tbq * 128:(tbq + 1) * 128], cons["ident_f32"]),
                    sem=s_ot)
            p.wait("vector", s_ot)
            for tbq in range(4):
                px = psum.f(3584, 512)[:, tbq * 128:(tbq + 1) * 128]
                p.op("vector", lambda e, px=px: e.tensor_scalar(
                    rc, px[:, 64:65], 1e-35, None, ALU.add, ALU.bypass), sem=s_oev)
                p.wait("vector", s_oev)
                p.op("vector", lambda e: e.reciprocal(rc, rc), sem=s_oev)
                p.wait("vector", s_oev)
                p.op("vector", lambda e, px=px, tbq=tbq, qg=qg: e.tensor_scalar(
                    ych[:, tbq, qg * 64:(qg + 1) * 64], px[:, 0:64], rc, None,
                    ALU.mult, ALU.bypass), sem=s_ov)

        for ti in range(4):
            t = c * 4 + ti
            p.wait("vector", s_ov)
            p.op("vector", lambda e, t=t, ti=ti: e.tensor_scalar(
                scr, x_sb[:, t, :], mrstash[:, 2 * ti:2 * ti + 1],
                mrstash[:, 2 * ti + 1:2 * ti + 2], ALU.subtract, ALU.mult), sem=s_fin)
            p.wait("vector", s_fin)
            p.op("vector", lambda e, ti=ti: e.tensor_tensor(
                ych[:, ti, :], ych[:, ti, :], scr, ALU.add), sem=s_fin)
            p.wait("vector", s_fin)
            for hh in range(2):
                p.op("vector", lambda e, t=t, ti=ti, hh=hh: e.bn_stats(
                    bnstash[:, (t * 2 + hh) * 6:(t * 2 + hh) * 6 + 6],
                    ych[:, ti, hh * 512:(hh + 1) * 512]), sem=s_fin)
            p.wait("scalar", s_fin)
            p.wait("scalar", s_yst, p.val(s_yst))
            p.op("scalar", lambda e, t=t, ti=ti: e.dma_start(
                out=y_hbm.rearrange("(t p) d -> p t d", p=128)[:, t, :],
                in_=ych[:, ti, :]), sem=s_yst, k=16)
        ar.release()

    if stage < 4:
        return s_fin

    p.wait("vector", s_fin)
    p.op("vector", lambda e: e.bn_aggr(mv2, bnstash), sem=s_fin)
    p.wait("vector", s_fin)
    p.op("vector", lambda e: e.tensor_scalar(
        msq, mv2[:, 0:1], mv2[:, 0:1], None, ALU.mult, ALU.bypass), sem=s_fin)
    p.wait("vector", s_fin)
    p.op("vector", lambda e: e.tensor_tensor(
        mv2[:, 1:2], mv2[:, 1:2], msq, ALU.add), sem=s_fin)
    psr = psum.f(1024, 512)[0:1, 0:2]
    p.wait("tensor", s_fin)
    p.op("tensor", lambda e, psr=psr: e.matmul(
        psr, cons["onesc_f32"], mv2, start=True, stop=True), sem=s_fin)
    p.wait("vector", s_fin)
    p.op("vector", lambda e, psr=psr: e.tensor_copy(red_sb[0:1, 0:2], psr), sem=s_fin)
    p.wait("gpsimd", s_fin)
    p.op("gpsimd", lambda e: e.dma_start(out=cc_in[0:1, :], in_=red_sb), sem=s_cc, k=16)
    p.wait("gpsimd", s_cc)
    p.op("gpsimd", lambda e: e.collective_compute(
        "AllReduce", ALU.add, replica_groups=groups,
        ins=[cc_in.ap().opt()], outs=[cc_out.ap().opt()]), sem=s_cc, k=1)
    p.wait("gpsimd", s_cc)
    p.op("gpsimd", lambda e: e.dma_start(out=red_sb, in_=cc_out[0:1, :]), sem=s_cc, k=16)
    for eng in ("vector", "scalar", "tensor"):
        p.wait(eng, s_cc)
    p.op("vector", lambda e: e.tensor_scalar(
        red_sb[0:1, 2:4], red_sb[0:1, 0:2], 1.0 / (4 * 128), None, ALU.mult,
        ALU.bypass), sem=s_fin)
    p.wait("vector", s_fin)
    p.op("vector", lambda e: e.tensor_tensor(
        red_sb[0:1, 4:5], red_sb[0:1, 2:3], red_sb[0:1, 2:3], ALU.mult), sem=s_fin)
    p.wait("vector", s_fin)
    p.op("vector", lambda e: e.tensor_tensor(
        red_sb[0:1, 5:6], red_sb[0:1, 3:4], red_sb[0:1, 4:5], ALU.subtract), sem=s_fin)
    p.wait("scalar", s_fin)
    p.op("scalar", lambda e: e.activation(
        red_sb[0:1, 6:7], red_sb[0:1, 5:6], AF.Sqrt, bias=cons["eps128"][0:1, :],
        scale=1.0), sem=s_fin)
    p.wait("vector", s_fin)
    p.op("vector", lambda e: e.reciprocal(red_sb[0:1, 7:8], red_sb[0:1, 6:7]), sem=s_fin)
    p.wait("vector", s_fin)
    p.op("vector", lambda e: e.tensor_copy(red_sb[0:1, 8:9], red_sb[0:1, 7:8]), sem=s_fin)
    p.wait("vector", s_fin)
    p.op("vector", lambda e: e.tensor_tensor(
        red_sb[0:1, 9:10], red_sb[0:1, 2:3], red_sb[0:1, 7:8], ALU.mult), sem=s_fin)
    p.wait("vector", s_fin)
    p.op("vector", lambda e: e.tensor_scalar(
        red_sb[0:1, 9:10], red_sb[0:1, 9:10], -1.0, None, ALU.mult, ALU.bypass), sem=s_fin)
    psb = psum.f(1024, 512)[:, 0:2]
    p.wait("tensor", s_fin)
    p.op("tensor", lambda e, psb=psb: e.matmul(
        psb, cons["ones_f32"], red_sb[0:1, 8:10], start=True, stop=True), sem=s_fin)
    p.wait("vector", s_fin)
    p.op("vector", lambda e, psb=psb: e.tensor_copy(bcast_sb, psb), sem=s_fin)
    ar.mark()
    yb = [ar.alloc([128, 1024], F32) for _ in range(2)]
    s_yl = [p.sem(pref + "yl0"), p.sem(pref + "yl1")]
    p.wait("sync", s_yst)
    p.wait("sync", s_cc)
    for t in range(NT):
        b = yb[t % 2]
        if t >= 2:
            p.wait("sync", s_fin, p.val(s_fin) - 2)
        p.op("sync", lambda e, b=b, t=t: e.dma_start(
            out=b, in_=y_hbm.rearrange("(t p) d -> p t d", p=128)[:, t, :]),
            sem=s_yl[t % 2], k=16)
        p.wait("vector", s_yl[t % 2])
        p.op("vector", lambda e, b=b, t=t: e.scalar_tensor_tensor(
            b, b, bcast_sb[:, 0:1], x_sb[:, t, :], ALU.mult, ALU.add), sem=s_fin)
        p.wait("vector", s_fin)
        p.op("vector", lambda e, b=b, t=t: e.tensor_scalar(
            x_sb[:, t, :], b, bcast_sb[:, 1:2], None, ALU.add, ALU.bypass), sem=s_fin)
    ar.release()
    return s_fin



L1_SCALE = 1.0 / (96.0 ** 0.5)


def emit_l1a(p, ar, psum, cons, x_sb, w_kvd, w_q, w_qr, w_kr, w_kup_loc, w_vup_loc,
             w_o, cos_own, sin_own, cos_all, sin_all, masks, qidx, attnidx, pref="b1",
             nheads=4, ngb=16, stage=9):
    nc = p.nc
    s_w = p.sem(pref + "w")
    s_tmm = p.sem(pref + "tmm")
    s_tev = p.sem(pref + "tev")
    s_pn = p.sem(pref + "pn")
    s_qkv = p.sem(pref + "qkv")
    s_qev = p.sem(pref + "qev")
    s_cc = p.sem(pref + "cc")
    s_kst = p.sem(pref + "kst")
    s_kh = p.sem(pref + "kh")
    s_sc = p.sem(pref + "sc")
    s_ex = p.sem(pref + "ex")
    s_av = p.sem(pref + "av")
    s_oev = p.sem(pref + "oev")
    s_nrm = p.sem(pref + "nrm")
    s_fin = p.sem(pref + "fin")
    s_dl = [p.sem(pref + "dl0"), p.sem(pref + "dl1")]

    agq_in = nc.dram_tensor(pref + "_aqin", [1024, 2048], BF16)
    agq_out = nc.dram_tensor(pref + "_aqout", [4096, 2048], BF16)
    ag1_in = nc.dram_tensor(pref + "_ag1in", [160, 2048], BF16)
    ag1_out = nc.dram_tensor(pref + "_ag1out", [640, 2048], BF16)
    ag2_in = [nc.dram_tensor(pref + "_ag2i%d" % i, [128, 8192], BF16)
              for i in range(2)]
    ag2_out = [nc.dram_tensor(pref + "_ag2o%d" % i, [512, 8192], BF16)
               for i in range(2)]
    groups = [[0, 1, 2, 3], [4, 5, 6, 7]]

    ar.mark()
    xln_fm = ar.alloc([128, 8, 2048], BF16)
    scr = ar.alloc([128, 1024], F32)
    scrb = ar.alloc([128, 1024], BF16)
    stats = ar.alloc([128, 16], F32)
    mr = ar.alloc([128, 2], F32)
    for t in range(NT):
        if t > 0:
            p.wait("scalar", s_tmm)
        emit_prenorm_tile(p, ar, x_sb[:, t, :], scr, scrb, stats, mr, s_pn, cons)
        p.wait("tensor", s_pn)
        for dt in range(8):
            slot = p.val(s_tmm) % 2
            ps = psum.t_u8[:, 12288 + slot * 2048: 12288 + slot * 2048 + 256].bitcast(BF16)
            if p.val(s_tmm) >= 2:
                p.wait("tensor", s_tev, p.val(s_tev) - 1)
            p.op("tensor", lambda e, ps=ps, dt=dt: e.transpose(
                ps, scrb[:, dt * 128:(dt + 1) * 128], cons["ident_bf"]), sem=s_tmm)
            p.wait("vector", s_tmm)
            p.op("vector", lambda e, ps=ps, dt=dt, t=t: e.tensor_copy(
                xln_fm[:, dt, t * 128:(t + 1) * 128], ps), sem=s_tev)

    wq_sb = ar.alloc([128, 8, 1024], BF16)
    wkvd_sb = ar.alloc([128, 8, 128], BF16)
    wqr_sb = ar.alloc([128, 8, 32], BF16)
    p.op("gpsimd", lambda e: e.dma_start(
        out=wq_sb, in_=w_q.rearrange("(t p) c -> p t c", p=128)), sem=s_w, k=16)
    p.op("gpsimd", lambda e: e.dma_start(
        out=wkvd_sb, in_=w_kvd.rearrange("(t p) c -> p t c", p=128)), sem=s_w, k=16)
    p.op("gpsimd", lambda e: e.dma_start(
        out=wqr_sb, in_=w_qr.rearrange("(t p) c -> p t c", p=128)), sem=s_w, k=16)
    costile = ar.alloc([128, 16, 32], F32)
    p.op("sync", lambda e: e.dma_start(
        out=costile[:, :, 0:16], in_=cos_own.rearrange("(t p) f -> p t f", p=128)),
        sem=s_kh, k=16)
    p.op("sync", lambda e: e.dma_start(
        out=costile[:, :, 16:32], in_=sin_own.rearrange("(t p) f -> p t f", p=128)),
        sem=s_kh, k=16)
    p.wait("tensor", s_w)
    p.wait("tensor", s_kh)
    p.wait("vector", s_kh)
    p.wait("tensor", s_tev)

    qloc = ar.alloc([128, 8, 2048], BF16)
    ckv = ar.alloc([128, 2048], BF16)
    qrt = ar.alloc([128, 16, 32], F32)
    nmm = 0
    for mt in range(8):
        for nt4 in range(4):
            ps = psum.f((nmm % 2) * 512, 512)
            if nmm >= 2:
                p.wait("tensor", s_qev, p.val(s_qev) - 1)
            for dt in range(8):
                p.op("tensor", lambda e, ps=ps, mt=mt, nt4=nt4, dt=dt: e.matmul(
                    ps, wq_sb[:, dt, mt * 128:(mt + 1) * 128],
                    xln_fm[:, dt, nt4 * 512:(nt4 + 1) * 512],
                    start=(dt == 0), stop=(dt == 7)),
                    sem=(s_qkv if dt == 7 else None))
            p.wait("vector", s_qkv)
            p.op("vector", lambda e, ps=ps, mt=mt, nt4=nt4: e.tensor_copy(
                qloc[:, mt, nt4 * 512:(nt4 + 1) * 512], ps), sem=s_qev)
            nmm += 1
    for nt4 in range(4):
        ps = psum.f((nmm % 2) * 512, 512)
        if nmm >= 2:
            p.wait("tensor", s_qev, p.val(s_qev) - 1)
        for dt in range(8):
            p.op("tensor", lambda e, ps=ps, nt4=nt4, dt=dt: e.matmul(
                ps, wkvd_sb[:, dt, :], xln_fm[:, dt, nt4 * 512:(nt4 + 1) * 512],
                start=(dt == 0), stop=(dt == 7)),
                sem=(s_qkv if dt == 7 else None))
        p.wait("vector", s_qkv)
        p.op("vector", lambda e, ps=ps, nt4=nt4: e.tensor_copy(
            ckv[:, nt4 * 512:(nt4 + 1) * 512], ps), sem=s_qev)
        nmm += 1
    for t in range(NT):
        ps = psum.f(1024 + (t % 2) * 512, 512)[:, 0:32]
        if t >= 2:
            p.wait("tensor", s_qev, p.val(s_qev) - 1)
        for dt in range(8):
            p.op("tensor", lambda e, ps=ps, t=t, dt=dt: e.matmul(
                ps, xln_fm[:, dt, t * 128:(t + 1) * 128], wqr_sb[:, dt, :],
                start=(dt == 0), stop=(dt == 7)),
                sem=(s_qkv if dt == 7 else None))
        p.wait("vector", s_qkv)
        p.op("vector", lambda e, ps=ps, t=t: e.tensor_copy(qrt[:, t, :], ps), sem=s_qev)
    rot0 = ar.alloc([128, 16, 32], F32)
    emit_rotary(p, qrt, costile, NT, s_qev, rot0)
    stg = ar.alloc([128, 128], BF16)
    qrf = ar.alloc([96, 2048], BF16)
    p.op("vector", lambda e: e.memset(stg, 0.0), sem=s_qev)
    for t in range(NT):
        p.wait("vector", s_qev)
        p.wait("vector", s_tmm)
        p.op("vector", lambda e, t=t: e.tensor_copy(
            stg[:, 64:96], qrt[:, t, :]), sem=s_qev)
        p.wait("tensor", s_qev)
        p.wait("tensor", s_tev)
        ps = psum.t_u8[:, 12288:12288 + 256].bitcast(BF16)
        p.op("tensor", lambda e, ps=ps: e.transpose(
            ps, stg, cons["ident_bf"]), sem=s_tmm)
        p.wait("vector", s_tmm)
        p.op("vector", lambda e, ps=ps, t=t: e.tensor_copy(
            qrf[64:96, t * 128:(t + 1) * 128], ps[64:96, :]), sem=s_tev)

    p.wait("scalar", s_qev)
    p.wait("scalar", s_tev)
    for mt in range(8):
        p.op("scalar", lambda e, mt=mt: e.dma_start(
            out=agq_in[mt * 128:(mt + 1) * 128, :], in_=qloc[:, mt, :]), sem=s_cc, k=16)
    p.op("scalar", lambda e: e.dma_start(
        out=ag1_in[0:32, :], in_=qrf[64:96, :]), sem=s_cc, k=16)
    p.op("scalar", lambda e: e.dma_start(
        out=ag1_in[32:160, :], in_=ckv), sem=s_cc, k=16)
    cc_stores = p.val(s_cc)
    p.wait("gpsimd", s_cc)
    p.op("gpsimd", lambda e: e.collective_compute(
        "AllGather", ALU.bypass, replica_groups=groups,
        ins=[ag1_in.ap().opt()], outs=[ag1_out.ap().opt()]), sem=s_cc, k=1)
    cc_ag1 = p.val(s_cc)
    p.op("gpsimd", lambda e: e.collective_compute(
        "AllGather", ALU.bypass, replica_groups=groups,
        ins=[agq_in.ap().opt()], outs=[agq_out.ap().opt()]), sem=s_cc, k=1)
    cc_agq = p.val(s_cc)
    ar.release()

    if stage < 2:
        for eng in ENGS:
            p.wait(eng, s_cc)
        return s_cc

    for eng in ("tensor", "vector", "scalar"):
        p.wait(eng, s_cc, cc_stores)
    ar.mark()
    ckv_g = ar.alloc([128, 4, 2048], BF16)
    kstack = ar.alloc([128, 8192], BF16)
    qstack = ar.alloc([128, 8192], BF16)
    vform = ar.alloc([128, 64, 130], BF16)
    attn = ar.alloc([128, 2, 8192], BF16)
    ptb = [ar.alloc([128, 1536], BF16) for _ in range(2)]
    wkup_sb = ar.alloc([128, 256], BF16)
    wvup_sb = ar.alloc([128, 256], BF16)
    masks_sb = ar.alloc([128, 4, 512], BF16)
    p.wait("sync", s_cc, cc_ag1)
    for r in range(4):
        p.op("sync", lambda e, r=r: e.dma_start(
            out=ckv_g[:, r, :], in_=ag1_out.ap()[r * 160 + 32: r * 160 + 160, :]),
            sem=s_kh, k=16)
    ar.mark()
    cosall = ar.alloc([128, 64, 32], BF16)
    p.wait("gpsimd", s_cc, cc_stores)
    p.op("gpsimd", lambda e: e.dma_start(
        out=cosall[:, :, 0:16], in_=cos_all.rearrange("(t p) f -> p t f", p=128)),
        sem=s_kst, k=16)
    p.op("gpsimd", lambda e: e.dma_start(
        out=cosall[:, :, 16:32], in_=sin_all.rearrange("(t p) f -> p t f", p=128)),
        sem=s_kst, k=16)
    wkr_sb = ar.alloc([128, 32], BF16)
    p.op("gpsimd", lambda e: e.dma_start(out=wkr_sb, in_=w_kr), sem=s_kst, k=16)
    p.op("gpsimd", lambda e: e.dma_start(out=wkup_sb, in_=w_kup_loc), sem=s_kst, k=16)
    p.op("gpsimd", lambda e: e.dma_start(out=wvup_sb, in_=w_vup_loc), sem=s_kst, k=16)
    p.op("sync", lambda e: e.dma_start(out=masks_sb, in_=masks.rearrange("j p q -> p j q")),
         sem=s_kh, k=16)
    p.wait("tensor", s_kst)
    p.wait("vector", s_kst)
    p.wait("tensor", s_kh)
    p.wait("vector", s_kh)

    g2c = []
    for r in range(4):
        for bi, blk in enumerate(BLOCKS[r]):
            g2c.append((blk, r, bi))
    g2c.sort()
    gcol = {blk: (r, bi) for blk, r, bi in g2c}
    xq = {blk: (r * 4 + bi) for blk, r, bi in g2c}

    krt = ar.alloc([128, 64, 32], BF16)
    nrp = 0
    for gb in range(16):
        r, bi = gcol[gb]
        for tt in range(4):
            gt = gb * 4 + tt
            ps = psum.f(1024 + (nrp % 2) * 512, 512)[:, 0:32]
            if nrp >= 2:
                p.wait("tensor", s_qev, p.val(s_qev) - 1)
            p.op("tensor", lambda e, ps=ps, r=r, bi=bi, tt=tt: e.matmul(
                ps, ckv_g[:, r, bi * 512 + tt * 128: bi * 512 + (tt + 1) * 128],
                wkr_sb[:, 0:32], start=True, stop=True),
                sem=s_qkv)
            p.wait("vector", s_qkv)
            p.op("vector", lambda e, ps=ps, gt=gt: e.tensor_copy(krt[:, gt, :], ps),
                 sem=s_qev)
            nrp += 1
    rot0c = ar.alloc([128, 64, 32], BF16)
    emit_rotary(p, krt, cosall, 64, s_qev, rot0c)
    stgc = ar.alloc([128, 128], BF16)
    p.op("vector", lambda e: e.memset(stgc, 0.0), sem=s_qev)
    for gt in range(64):
        p.wait("vector", s_qev)
        p.wait("vector", s_tmm)
        p.op("vector", lambda e, gt=gt: e.tensor_copy(
            stgc[:, 64:96], krt[:, gt, :]), sem=s_qev)
        p.wait("tensor", s_qev)
        p.wait("tensor", s_tev)
        ps = psum.t_u8[:, 12288:12288 + 256].bitcast(BF16)
        p.op("tensor", lambda e, ps=ps: e.transpose(
            ps, stgc, cons["ident_bf"]), sem=s_tmm)
        p.wait("vector", s_tmm)
        p.op("vector", lambda e, ps=ps, gt=gt: e.tensor_copy(
            kstack[64:96, gt * 128:(gt + 1) * 128], ps[64:96, :]), sem=s_tev)
    p.wait("vector", s_tev)
    p.op("vector", lambda e: e.memset(kstack[96:128, :], 0.0), sem=s_qev)
    ar.release()

    p.op("vector", lambda e: e.memset(qstack[96:128, :], 0.0), sem=s_qev)
    onesq = ar.alloc([128, 64], BF16)
    p.op("vector", lambda e: e.memset(onesq, 1.0), sem=s_qev)
    p.op("vector", lambda e: e.memset(attn, 0.0), sem=s_qev)
    onesn = ar.alloc([96, 1], BF16)
    p.op("vector", lambda e: e.memset(onesn, 1.0), sem=s_qev)
    rrow = ar.alloc([1, 512], F32)
    qgstg = ar.alloc([128, 4, 2048], BF16)
    qidx_sb = ar.alloc([128, 4, 4], U32)
    p.wait("sync", s_kh)
    p.op("sync", lambda e: e.dma_start(out=qidx_sb, in_=qidx), sem=s_kh, k=16)
    Rsb = ar.alloc([128, 512], F32)
    umax = ar.alloc([1, 8], F32)
    negu = ar.alloc([128, 1], F32)
    sq2 = [ar.alloc([96, 512], BF16) for _ in range(2)]
    unorm = ar.alloc([1, 32], F32)
    ostag = ar.alloc([128, 512], F32)
    s_sq = p.sem(pref + "usq")
    s_nq = p.sem(pref + "unq")
    s_nr = p.sem(pref + "unr")
    s_oc2 = p.sem(pref + "oc2")
    p.wait("sync", s_kh)
    for r in range(4):
        p.op("sync", lambda e, r=r: e.dma_start(
            out=qstack[64:96, r * 2048:(r + 1) * 2048],
            in_=ag1_out.ap()[r * 160: r * 160 + 32, :]), sem=s_kh, k=16)

    kbs3 = {}
    for gb in range(ngb):
        ext = (gb + 1) * 4
        kbs3[gb] = [list(range(i, min(i + 3, ext))) for i in range(0, ext, 3)]

    ptb_last_read = {0: 0, 1: 0}
    exp_read = {0: 0, 1: 0}
    for hi in range(nheads):
        par = hi % 2
        if hi == 0:
            p.wait("gpsimd", s_cc, cc_agq)
        p.wait("gpsimd", s_av)
        p.wait("gpsimd", s_kh)
        p.wait("gpsimd", s_qev)
        for r in range(4):
            p.op("gpsimd", lambda e, r=r, hi=hi: e.indirect_dma_start(
                out=qgstg[:, r, :], out_offset=None, in_=agq_out.ap(),
                in_offset=IndirectOffsetOnAxis(ap=qidx_sb[:, hi, r:r + 1], axis=0)),
                sem=s_dl[par], k=16)
        p.wait("vector", s_dl[par])
        for r in range(4):
            p.op("vector", lambda e, r=r: e.tensor_copy(
                qstack[0:64, r * 2048:(r + 1) * 2048], qgstg[0:64, r, :]), sem=s_qev)
        p.wait("tensor", s_ex)
        p.wait("tensor", s_oev)
        p.wait("vector", s_sc)
        p.wait("vector", s_av)
        for gb in range(16):
            r, bi = gcol[gb]
            ps = psum.f((gb % 2) * 512, 512)[0:64, :]
            if gb >= 2:
                p.wait("tensor", s_qev, p.val(s_qev) - 1)
            p.op("tensor", lambda e, ps=ps, r=r, bi=bi, hi=hi: e.matmul(
                ps, wkup_sb[:, hi * 64:(hi + 1) * 64],
                ckv_g[:, r, bi * 512:(bi + 1) * 512], start=True, stop=True),
                sem=s_qkv)
            p.wait("vector", s_qkv)
            p.op("vector", lambda e, ps=ps, gb=gb: e.tensor_copy(
                kstack[0:64, gb * 512:(gb + 1) * 512], ps), sem=s_qev)
        p.wait("vector", s_qev)
        p.op("vector", lambda e: e.memset(vform, 0.0), sem=s_qev)
        p.wait("vector", s_qev)
        oc = 0 if par == 1 else 64
        p.op("vector", lambda e, oc=oc: e.tensor_copy(
            vform[:, :, oc:oc + 1], onesq.unsqueeze(2)), sem=s_qev)
        vcol = 0 if par == 0 else 64
        for gt in range(64):
            gb4 = gt // 4
            r, bi = gcol[gb4]
            tt = gt % 4
            ps = psum.f(1024 + (gt % 2) * 512, 512)[:, 0:64]
            if gt >= 2:
                p.wait("tensor", s_qev, p.val(s_qev) - 1)
            p.op("tensor", lambda e, ps=ps, r=r, bi=bi, tt=tt, hi=hi: e.matmul(
                ps, ckv_g[:, r, bi * 512 + tt * 128: bi * 512 + (tt + 1) * 128],
                wvup_sb[:, hi * 64:(hi + 1) * 64], start=True, stop=True),
                sem=s_qkv)
            p.wait("vector", s_qkv)
            p.op("vector", lambda e, ps=ps, gt=gt, vcol=vcol: e.tensor_copy(
                vform[:, gt, vcol:vcol + 64], ps), sem=s_qev)
        p.wait("vector", s_dl[par])
        p.wait("scalar", s_qev)
        for idx in range(32):
            ui, blk = idx // 16, idx % 16
            buf = qstack if ui == 0 else kstack
            sq_b = sq2[idx % 2]
            psu = psum.f(3072 + (idx % 2) * 512, 512)[0:1, :]
            p.wait("scalar", s_nq, p.val(s_nq) - 1)
            p.op("scalar", lambda e, buf=buf, blk=blk, sq_b=sq_b: e.activation(
                sq_b, buf[0:96, blk * 512:(blk + 1) * 512], AF.Square,
                bias=0.0, scale=1.0), sem=s_sq)
            p.wait("tensor", s_sq)
            p.wait("tensor", s_nr, p.val(s_nr) - 1)
            p.op("tensor", lambda e, psu=psu, sq_b=sq_b: e.matmul(
                psu, onesn, sq_b, start=True, stop=True), sem=s_nq)
            p.wait("vector", s_nq)
            p.op("vector", lambda e, psu=psu, idx=idx: e.tensor_reduce(
                unorm[0:1, idx:idx + 1], psu, mybir.AxisListType.X, ALU.max),
                sem=s_nr)
        p.wait("vector", s_nr)
        p.op("vector", lambda e: e.tensor_reduce(
            umax[0:1, 0:1], unorm[0:1, 0:16], mybir.AxisListType.X, ALU.max),
            sem=s_nrm)
        p.op("vector", lambda e: e.tensor_reduce(
            umax[0:1, 1:2], unorm[0:1, 16:32], mybir.AxisListType.X, ALU.max),
            sem=s_nrm)
        p.wait("vector", s_nrm)
        p.op("vector", lambda e: e.tensor_tensor(
            umax[0:1, 3:4], umax[0:1, 0:1], umax[0:1, 1:2], ALU.mult), sem=s_nrm)
        p.wait("scalar", s_nrm)
        p.op("scalar", lambda e: e.activation(
            umax[0:1, 4:5], umax[0:1, 3:4], AF.Sqrt, bias=0.0, scale=1.0), sem=s_nrm)
        p.wait("scalar", s_nrm)
        p.op("scalar", lambda e: e.mul(umax[0:1, 5:6], umax[0:1, 4:5], -L1_SCALE), sem=s_nrm)
        psn = psum.f(3584, 512)[:, 0:1]
        p.wait("tensor", s_nrm)
        p.op("tensor", lambda e, ps=psn: e.matmul(
            ps, cons["ones_f32"], umax[0:1, 5:6], start=True, stop=True), sem=s_nrm)
        p.wait("vector", s_nrm)
        p.op("vector", lambda e, ps=psn: e.tensor_copy(negu, ps), sem=s_nrm)

        if stage < 3:
            continue

        p.wait("tensor", s_nrm)
        p.wait("tensor", s_nr)
        for gb in range(ngb):
            otps = psum.f(0, 512)
            p.wait("tensor", s_oc2)
            batches = kbs3[gb]
            nb = len(batches)
            sc_done = {}
            qcol = xq[gb] * 512
            for it in range(nb + 1):
                if it < nb:
                    bi2, batch = it, batches[it]
                    stps = psum.f(512 + (bi2 % 2) * 1536, 1536)
                    p.wait("tensor", s_ex, exp_read[bi2 % 2])
                    for j, kt in enumerate(batch):
                        p.op("tensor", lambda e, stps=stps, j=j, kt=kt, qcol=qcol: e.matmul(
                            stps[:, j * 512:(j + 1) * 512],
                            kstack[:, kt * 128:(kt + 1) * 128],
                            qstack[:, qcol:qcol + 512],
                            start=True, stop=True),
                            sem=(s_sc if j == len(batch) - 1 else None))
                    if any(kt - gb * 4 >= 0 for kt in batch):
                        p.wait("vector", s_sc)
                        for j, kt in enumerate(batch):
                            jj = kt - gb * 4
                            if jj >= 0:
                                p.op("vector", lambda e, stps=stps, j=j, jj=jj: e.tensor_tensor(
                                    stps[:, j * 512:(j + 1) * 512],
                                    stps[:, j * 512:(j + 1) * 512],
                                    masks_sb[:, jj, :], ALU.add), sem=s_sc)
                    sc_done[bi2] = p.val(s_sc)
                if it >= 1:
                    bi2, batch = it - 1, batches[it - 1]
                    stps = psum.f(512 + (bi2 % 2) * 1536, 1536)
                    ptbuf = ptb[bi2 % 2]
                    p.wait("scalar", s_sc, sc_done[bi2])
                    p.wait("scalar", s_nrm)
                    p.wait("scalar", s_av, ptb_last_read[bi2 % 2])
                    n = len(batch) * 512
                    p.op("scalar", lambda e, stps=stps, ptbuf=ptbuf, n=n: e.activation(
                        ptbuf[:, 0:n], stps[:, 0:n], AF.Exp, bias=negu, scale=L1_SCALE),
                        sem=s_ex)
                    exp_read[bi2 % 2] = p.val(s_ex)
                    p.wait("tensor", s_ex)
                    for j, kt in enumerate(batch):
                        first = (bi2 == 0 and j == 0)
                        last = (bi2 == nb - 1 and j == len(batch) - 1)
                        vap = vform[:, kt, 0:65] if par == 0 else vform[:, kt, 0:128]
                        oap = otps[0:65, :] if par == 0 else otps
                        p.op("tensor", lambda e, vap=vap, oap=oap, ptbuf=ptbuf, j=j,
                             first=first, last=last: e.matmul(
                                 oap, vap, ptbuf[:, j * 512:(j + 1) * 512],
                                 start=first, stop=last), sem=s_av)
                    ptb_last_read[bi2 % 2] = p.val(s_av)
            p.wait("vector", s_av)
            p.wait("vector", s_oev)
            p.op("vector", lambda e, otps=otps: e.tensor_copy(ostag, otps),
                 sem=s_oc2)
            sumrow = ostag[64:65, :] if par == 0 else ostag[0:1, :]
            p.op("vector", lambda e, sumrow=sumrow: e.tensor_scalar(
                rrow, sumrow, 1e-35, None, ALU.add, ALU.bypass), sem=s_oev)
            p.wait("vector", s_oev)
            p.op("vector", lambda e: e.reciprocal(rrow, rrow), sem=s_oev)
            psR = psum.f(3584, 512)
            p.wait("tensor", s_oev)
            p.op("tensor", lambda e, ps=psR: e.matmul(
                ps, cons["ones_f32"], rrow, start=True, stop=True), sem=s_oev)
            p.wait("vector", s_oev)
            p.op("vector", lambda e, ps=psR: e.tensor_copy(Rsb, ps), sem=s_oev)
            p.wait("vector", s_oev)
            vr = (0, 64) if par == 0 else (64, 128)
            qcol2 = xq[gb] * 512
            p.op("vector", lambda e, vr=vr, hi=hi, qcol2=qcol2: e.tensor_tensor(
                attn[vr[0]:vr[1], hi // 2, qcol2:qcol2 + 512],
                ostag[vr[0]:vr[1], :], Rsb[vr[0]:vr[1], :], ALU.mult), sem=s_oev)

        if stage >= 4 and hi in (1, 3):
            pr = hi // 2
            p.wait("scalar", s_oev)
            p.op("scalar", lambda e, pr=pr: e.dma_start(
                out=ag2_in[pr].ap(), in_=attn[:, pr, :]), sem=s_cc, k=16)
            p.wait("gpsimd", s_cc)
            p.op("gpsimd", lambda e, pr=pr: e.collective_compute(
                "AllGather", ALU.bypass, replica_groups=groups,
                ins=[ag2_in[pr].ap().opt()], outs=[ag2_out[pr].ap().opt()]),
                sem=s_cc, k=1)

    if stage < 4:
        return s_oev

    for eng in ENGS:
        p.wait(eng, s_cc)
    ar.release()

    ar.mark()
    attn_own = ar.alloc([128, 8, 2048], BF16)
    wo_sb = ar.alloc([128, 8, 1024], BF16)
    aidx_sb = ar.alloc([128, 8], U32)
    p.wait("sync", s_kh)
    p.op("sync", lambda e: e.dma_start(out=aidx_sb, in_=attnidx), sem=s_kh, k=16)
    p.op("gpsimd", lambda e: e.dma_start(
        out=wo_sb, in_=w_o.rearrange("(t p) c -> p t c", p=128)), sem=s_w, k=16)
    p.wait("gpsimd", s_kh)
    ag2v = [t.ap().rearrange("r (g c) -> (r g) c", g=4) for t in ag2_out]
    for dt in range(8):
        p.op("gpsimd", lambda e, dt=dt: e.indirect_dma_start(
            out=attn_own[:, dt, :], out_offset=None, in_=ag2v[dt % 2],
            in_offset=IndirectOffsetOnAxis(ap=aidx_sb[:, dt:dt + 1], axis=0)),
            sem=s_kst, k=16)
    p.wait("tensor", s_kst)
    p.wait("tensor", s_kh)
    p.wait("tensor", s_w)
    ystg = ar.alloc([128, 512], F32)
    for mt in range(8):
        for nt4 in range(4):
            ps = psum.f(((mt * 4 + nt4) % 2) * 512, 512)
            if mt * 4 + nt4 >= 2:
                p.wait("tensor", s_qev, p.val(s_qev) - 1)
            for dt in range(8):
                p.op("tensor", lambda e, ps=ps, mt=mt, nt4=nt4, dt=dt: e.matmul(
                    ps, wo_sb[:, dt, mt * 128:(mt + 1) * 128],
                    attn_own[:, dt, nt4 * 512:(nt4 + 1) * 512],
                    start=(dt == 0), stop=(dt == 7)),
                    sem=(s_qkv if dt == 7 else None))
            p.wait("vector", s_qkv)
            p.wait("vector", s_tmm)
            p.op("vector", lambda e, ps=ps: e.tensor_copy(ystg, ps), sem=s_qev)
            for sub in range(4):
                t = nt4 * 4 + sub
                px = psum.f(1536, 128)
                p.wait("tensor", s_qev)
                p.wait("tensor", s_fin)
                p.op("tensor", lambda e, px=px, sub=sub: e.transpose(
                    px, ystg[:, sub * 128:(sub + 1) * 128], cons["ident_f32"]),
                    sem=s_tmm)
                p.wait("vector", s_tmm)
                p.op("vector", lambda e, px=px, t=t, mt=mt: e.tensor_tensor(
                    x_sb[:, t, mt * 128:(mt + 1) * 128],
                    x_sb[:, t, mt * 128:(mt + 1) * 128], px, ALU.add), sem=s_fin)
    ar.release()
    return s_fin




def emit_rotary(p, buf, cs, nt, s_ev, r0):
    v1 = buf[:, 0:nt, 0:16]
    v2 = buf[:, 0:nt, 16:32]
    co = cs[:, 0:nt, 0:16]
    si = cs[:, 0:nt, 16:32]
    sA = r0[:, 0:nt, 0:16]
    sB = r0[:, 0:nt, 16:32]
    for f in [
        lambda e: e.tensor_tensor(sA, v1, co, ALU.mult),
        lambda e: e.tensor_tensor(sB, v1, si, ALU.mult),
        lambda e: e.tensor_tensor(v1, v2, si, ALU.mult),
        lambda e: e.tensor_tensor(v1, sA, v1, ALU.subtract),
        lambda e: e.tensor_tensor(v2, v2, co, ALU.mult),
        lambda e: e.tensor_tensor(v2, v2, sB, ALU.add),
    ]:
        p.wait("vector", s_ev)
        p.op("vector", f, sem=s_ev)



CAP = 640
CROW = CAP + 1
MROWS = TPC + CAP
N_BIS = 30


def emit_l0f_wprep(p, ar, psum, cons, wg_loc, wu_loc, wd_loc, pref="m0p"):
    nc = p.nc
    s_w = p.sem(pref + "w")
    s_tmm = p.sem(pref + "tmm")
    s_tev = p.sem(pref + "tev")
    s_wst = p.sem(pref + "wst")
    s_wst2 = p.sem(pref + "wst2")
    s_cc = p.sem(pref + "cc")
    s_zr = p.sem(pref + "zr")
    s_pn = p.sem(pref + "pn")
    wgT_loc = nc.dram_tensor(pref + "_wgTl", [4096, 1024], BF16)
    wuT_loc = nc.dram_tensor(pref + "_wuTl", [4096, 1024], BF16)
    wdT_loc = nc.dram_tensor(pref + "_wdTl", [4096, 1024], BF16)
    wgT_all = nc.dram_tensor(pref + "_wgTa", [8 * 4096, 1024], BF16, addr_space="Shared")
    wuT_all = nc.dram_tensor(pref + "_wuTa", [8 * 4096, 1024], BF16, addr_space="Shared")
    wdT_all = nc.dram_tensor(pref + "_wdTa", [8 * 4096, 1024], BF16, addr_space="Shared")
    compact = nc.dram_tensor(pref + "_cmp", [8 * CROW, 1040], BF16).ap()
    moeacc = nc.dram_tensor(pref + "_acc", [MROWS, 1024], F32).ap()
    zt = ar.alloc([128, 1040], BF16)

    p.op("vector", lambda e: e.memset(zt, 0.0), sem=s_pn)
    p.wait("gpsimd", s_pn)
    for r0 in range(0, 8 * CROW, 128):
        r1 = min(r0 + 128, 8 * CROW)
        p.op("gpsimd", lambda e, r0=r0, r1=r1: e.dma_start(
            out=compact[r0:r1, :], in_=zt[0:r1 - r0, :]), sem=s_zr, k=16)
    ztf = zt.bitcast(F32)
    for r0 in range(0, MROWS, 128):
        r1 = min(r0 + 128, MROWS)
        for c0 in (0, 512):
            p.op("gpsimd", lambda e, r0=r0, r1=r1, c0=c0: e.dma_start(
                out=moeacc[r0:r1, c0:c0 + 512], in_=ztf[0:r1 - r0, 0:512]),
                sem=s_zr, k=16)

    for w_loc, wT_loc, R, C, grp in ((wg_loc, wgT_loc, 4096, 1024, "r"),
                                     (wu_loc, wuT_loc, 4096, 1024, "r"),
                                     (wd_loc, wdT_loc, 1024, 4096, "c")):
        emit_wT_to_hbm(p, ar, psum, cons, w_loc, R, C,
                       wT_loc.ap().rearrange("(t p) c -> t p c", p=128),
                       grp, (s_w, s_tmm, s_tev, s_wst, s_wst2),
                       store_eng="sync")
    p.wait("tensor", s_tev)
    return {"wgT_all": wgT_all, "wuT_all": wuT_all, "wdT_all": wdT_all,
            "wgT_loc": wgT_loc, "wuT_loc": wuT_loc, "wdT_loc": wdT_loc,
            "compact": compact, "moeacc": moeacc,
            "s_cc": s_cc, "s_zr": s_zr, "zr_total": p.val(s_zr),
            "s_wst": s_wst, "s_wst2": s_wst2,
            "gates": [(s_wst, p.val(s_wst)), (s_wst2, p.val(s_wst2))]}


def emit_l1f_wprep(p, ar, psum, cons, w_fc, w_proj, pref="f1p"):
    nc = p.nc
    s_w = p.sem(pref + "w")
    s_tmm = p.sem(pref + "tmm")
    s_tev = p.sem(pref + "tev")
    s_wst = p.sem(pref + "wst")
    s_wst2 = p.sem(pref + "wst2")
    wfcT_hbm = nc.dram_tensor(pref + "_wfcT", [32, 128, 1024], BF16).ap()
    wprT_hbm = nc.dram_tensor(pref + "_wprT", [32, 128, 1024], BF16).ap()
    emit_wT_to_hbm(p, ar, psum, cons, w_fc, 4096, 1024, wfcT_hbm, "r",
                   (s_w, s_tmm, s_tev, s_wst, s_wst2), store_eng="sync")
    emit_wT_to_hbm(p, ar, psum, cons, w_proj, 1024, 4096, wprT_hbm, "c",
                   (s_w, s_tmm, s_tev, s_wst, s_wst2), store_eng="sync")
    p.wait("tensor", s_tev)
    return {"wfcT": wfcT_hbm, "wprT": wprT_hbm,
            "s_wst": s_wst, "s_wst2": s_wst2,
            "gates": [(s_wst, p.val(s_wst)), (s_wst2, p.val(s_wst2))]}


def emit_l0f(p, ar, psum, cons, x_sb, gate_w, wp,
             m32, erow, idrow, trasht, pref="m0", nexp=8, stage=9,
             dbg=None):
    nc = p.nc
    s_w = p.sem(pref + "w")
    s_tmm = p.sem(pref + "tmm")
    s_tev = p.sem(pref + "tev")
    s_wst = p.sem(pref + "wst")
    s_cc = wp["s_cc"]
    s_ca = p.sem(pref + "ca")
    s_cg = p.sem(pref + "cg")
    s_pn = p.sem(pref + "pn")
    s_mm = p.sem(pref + "mm")
    s_ev = p.sem(pref + "ev")
    s_hh = p.sem(pref + "hh")
    s_dm = p.sem(pref + "dm")
    s_xe = p.sem(pref + "xe")
    s_kh = p.sem(pref + "kh")
    s_bi = p.sem(pref + "bi")
    s_sca = p.sem(pref + "sca")
    s_add = p.sem(pref + "add")
    s_ws = [p.sem(pref + "ws0"), p.sem(pref + "ws1")]
    s_wd = [p.sem(pref + "wd0"), p.sem(pref + "wd1")]
    s_fin = p.sem(pref + "fin")

    af_in = nc.dram_tensor(pref + "_afin", [8, TPC], F32)
    af_out = nc.dram_tensor(pref + "_afout", [32, TPC], F32)
    compact = wp["compact"]
    moeacc = wp["moeacc"]
    wg_v = wp["wgT_all"].ap().rearrange("(E t p) c -> E t p c", E=8, p=128)
    wu_v = wp["wuT_all"].ap().rearrange("(E t p) c -> E t p c", E=8, p=128)
    wd_v = wp["wdT_all"].ap().rearrange("(E t p) c -> E t p c", E=8, p=128)

    affin = ar.alloc([128, NT, 8], F32)
    scr = ar.alloc([128, 1024], F32)
    stats = ar.alloc([128, 24], F32)
    mr = ar.alloc([128, 2], F32)
    gwT = ar.alloc([128, 8, 8], F32)
    xlt = ar.alloc([128, 8, 128], F32)
    ar.mark()
    comb = ar.alloc([128, NT, 1040], BF16)
    gw_stg = ar.alloc([128, 1024], F32)
    p.op("sync", lambda e: e.dma_start(out=gw_stg[0:8, :], in_=gate_w), sem=s_kh, k=16)
    p.wait("tensor", s_kh)
    for dt in range(8):
        slot = p.val(s_tmm) % 2
        ps = psum.t_u8[:, 12288 + slot * 2048: 12288 + slot * 2048 + 512].bitcast(F32)
        if p.val(s_tmm) >= 2:
            p.wait("tensor", s_tev, p.val(s_tev) - 1)
        p.op("tensor", lambda e, ps=ps, dt=dt: e.transpose(
            ps, gw_stg[:, dt * 128:(dt + 1) * 128], cons["ident_f32"]), sem=s_tmm)
        p.wait("vector", s_tmm)
        p.op("vector", lambda e, ps=ps, dt=dt: e.tensor_copy(
            gwT[:, dt, :], ps[:, 0:8]), sem=s_tev)

    for t in range(NT):
        if t > 0:
            p.wait("vector", s_tmm)
        emit_prenorm_tile(p, ar, x_sb[:, t, :], scr, comb[:, t, 0:1024],
                          stats, mr, s_pn, cons)
        p.wait("tensor", s_pn)
        for dt in range(8):
            slot = p.val(s_tmm) % 2
            ps = psum.t_u8[:, 12288 + slot * 2048: 12288 + slot * 2048 + 512].bitcast(F32)
            p.wait("tensor", s_tev, p.val(s_tev) - 1)
            p.op("tensor", lambda e, ps=ps, dt=dt: e.transpose(
                ps, scr[:, dt * 128:(dt + 1) * 128], cons["ident_f32"]), sem=s_tmm)
            p.wait("vector", s_tmm)
            p.op("vector", lambda e, ps=ps, dt=dt: e.tensor_copy(
                xlt[:, dt, :], ps), sem=s_tev)
        lg = psum.f(2048 + (t % 2) * 512, 512)[:, 0:8]
        p.wait("tensor", s_tev)
        if t >= 2:
            p.wait("tensor", s_ev, p.val(s_ev) - 1)
        for dt in range(8):
            p.op("tensor", lambda e, lg=lg, dt=dt: e.matmul(
                lg, xlt[:, dt, :], gwT[:, dt, :], start=(dt == 0), stop=(dt == 7)),
                sem=(s_mm if dt == 7 else None))
        p.wait("vector", s_mm)
        p.op("vector", lambda e, lg=lg: e.reduce_max(
            stats[:, 16:17], lg, mybir.AxisListType.X), sem=s_bi)
        p.wait("vector", s_bi)
        p.op("vector", lambda e: e.tensor_scalar(
            stats[:, 17:18], stats[:, 16:17], -1.0, None, ALU.mult, ALU.bypass),
            sem=s_bi)
        p.wait("scalar", s_bi)
        p.op("scalar", lambda e, lg=lg, t=t: e.activation(
            affin[:, t, :], lg, AF.Exp, bias=stats[:, 17:18], scale=1.0), sem=s_ev)
        p.wait("vector", s_ev)
        p.op("vector", lambda e, t=t: e.reduce_sum(
            stats[:, 18:19], affin[:, t, :], mybir.AxisListType.X), sem=s_bi)
        p.wait("vector", s_bi)
        p.op("vector", lambda e: e.reciprocal(
            stats[:, 19:20], stats[:, 18:19]), sem=s_bi)
        p.wait("vector", s_bi)
        p.op("vector", lambda e, t=t: e.tensor_scalar(
            affin[:, t, :], affin[:, t, :], stats[:, 19:20], None, ALU.mult,
            ALU.bypass), sem=s_bi)

    if stage < 1:
        return s_bi

    aem = ar.alloc([128, TPC], F32)
    stg = ar.alloc([128, 128], F32)
    p.op("vector", lambda e: e.memset(stg, 0.0), sem=s_bi)
    p.wait("vector", s_ev)
    for t in range(NT):
        p.wait("vector", s_bi)
        p.wait("vector", s_tmm)
        p.op("vector", lambda e, t=t: e.tensor_copy(
            stg[:, 0:8], affin[:, t, :]), sem=s_bi)
        p.wait("tensor", s_bi)
        slot = p.val(s_tmm) % 2
        ps = psum.t_u8[:, 12288 + slot * 2048: 12288 + slot * 2048 + 512].bitcast(F32)
        p.wait("tensor", s_tev, p.val(s_tev) - 1)
        p.op("tensor", lambda e, ps=ps: e.transpose(
            ps, stg, cons["ident_f32"]), sem=s_tmm)
        p.wait("vector", s_tmm)
        p.op("vector", lambda e, ps=ps, t=t: e.tensor_copy(
            aem[0:8, t * 128:(t + 1) * 128], ps[0:8, :]), sem=s_tev)
    p.wait("scalar", s_tev)
    p.op("scalar", lambda e: e.dma_start(out=af_in.ap(), in_=aem[0:8, :]),
         sem=s_ca, k=16)
    p.wait("gpsimd", wp["s_wst"])
    p.wait("gpsimd", wp["s_wst2"])
    p.op("gpsimd", lambda e: e.collective_compute(
        "AllGather", ALU.bypass, replica_groups=[[0, 1, 2, 3, 4, 5, 6, 7]],
        ins=[wp["wgT_loc"].ap().opt()], outs=[wp["wgT_all"].ap().opt()]),
        sem=s_cc, k=1)
    p.wait("gpsimd", s_ca, 16)
    p.op("gpsimd", lambda e: e.collective_compute(
        "AllGather", ALU.bypass, replica_groups=[[0, 1, 2, 3], [4, 5, 6, 7]],
        ins=[af_in.ap().opt()], outs=[af_out.ap().opt()]), sem=s_cg, k=1)
    p.op("gpsimd", lambda e: e.collective_compute(
        "AllGather", ALU.bypass, replica_groups=[[0, 1, 2, 3, 4, 5, 6, 7]],
        ins=[wp["wuT_loc"].ap().opt()], outs=[wp["wuT_all"].ap().opt()]),
        sem=s_cc, k=1)
    p.op("gpsimd", lambda e: e.collective_compute(
        "AllGather", ALU.bypass, replica_groups=[[0, 1, 2, 3, 4, 5, 6, 7]],
        ins=[wp["wdT_loc"].ap().opt()], outs=[wp["wdT_all"].ap().opt()]),
        sem=s_cc, k=1)

    af32 = ar.alloc([32, TPC], F32)
    junk = ar.alloc([32, TPC], F32)
    m32_sb = ar.alloc([32, 32], F32)
    bis = ar.alloc([32, 8], F32)
    selu32 = ar.alloc([32, 1], U8)
    p.wait("sync", s_cg)
    p.op("sync", lambda e: e.dma_start(out=af32, in_=af_out.ap()), sem=s_kh, k=16)
    p.op("sync", lambda e: e.dma_start(out=m32_sb, in_=m32), sem=s_kh, k=16)
    p.wait("vector", s_kh)
    p.wait("tensor", s_kh)
    p.op("vector", lambda e: e.memset(bis[:, 0:1], 0.0), sem=s_bi)
    p.op("vector", lambda e: e.memset(bis[:, 1:2], 1.0), sem=s_bi)
    for it in range(N_BIS):
        p.wait("vector", s_bi)
        p.op("vector", lambda e: e.tensor_tensor(
            bis[:, 2:3], bis[:, 0:1], bis[:, 1:2], ALU.add), sem=s_bi)
        p.wait("vector", s_bi)
        p.op("vector", lambda e: e.tensor_scalar(
            bis[:, 2:3], bis[:, 2:3], 0.5, None, ALU.mult, ALU.bypass), sem=s_bi)
        p.wait("vector", s_bi)
        p.op("vector", lambda e: e.tensor_scalar(
            junk, af32, bis[:, 2:3], 0.0, ALU.is_ge, ALU.add,
            accum_out=bis[:, 4:5]), sem=s_bi)
        ps = psum.f(2048, 512)
        p.wait("tensor", s_bi)
        p.op("tensor", lambda e, ps=ps: e.matmul(
            ps[0:32, 0:1], m32_sb, bis[:, 4:5], start=True, stop=True), sem=s_bi)
        p.wait("vector", s_bi)
        p.op("vector", lambda e, ps=ps: e.tensor_copy(
            bis[:, 3:4], ps[0:32, 0:1]), sem=s_bi)
        p.wait("vector", s_bi)
        p.op("vector", lambda e: e.tensor_scalar(
            selu32, bis[:, 3:4], 2048.0, None, ALU.is_ge, ALU.bypass), sem=s_bi)
        p.wait("vector", s_bi)
        p.op("vector", lambda e: e.select(
            bis[:, 5:6], selu32, bis[:, 2:3], bis[:, 0:1]), sem=s_bi)
        p.op("vector", lambda e: e.select(
            bis[:, 6:7], selu32, bis[:, 1:2], bis[:, 2:3]), sem=s_bi)
        p.wait("vector", s_bi)
        p.op("vector", lambda e: e.tensor_copy(bis[:, 0:1], bis[:, 5:6]), sem=s_bi)
        p.op("vector", lambda e: e.tensor_copy(bis[:, 1:2], bis[:, 6:7]), sem=s_bi)

    if dbg and "bis" in dbg:
        p.wait("scalar", s_bi)
        p.op("scalar", lambda e: e.dma_start(out=dbg["bis"], in_=bis), sem=s_kh, k=16)
    if dbg and "aem" in dbg:
        p.wait("scalar", s_tev)
        p.op("scalar", lambda e: e.dma_start(out=dbg["aem"], in_=aem[0:8, :]), sem=s_kh, k=16)
    if dbg and "af32" in dbg:
        p.wait("scalar", s_kh)
        p.op("scalar", lambda e: e.dma_start(out=dbg["af32"], in_=af32), sem=s_kh, k=16)
    if stage < 2:
        p.wait("scalar", s_kh)
        return s_bi

    idsb = ar.alloc([128, NT, 1], I16)
    p.op("sync", lambda e: e.dma_start(out=idsb, in_=idrow), sem=s_kh, k=16)
    maskf = ar.alloc([8, TPC], F32)
    posem = ar.alloc([8, TPC], F32)
    ones8 = ar.alloc([8, TPC], F32)
    selm = ar.alloc([8, TPC], U8)
    selmf = ar.alloc([8, TPC], F32)
    posx = ar.alloc([8, TPC], F32)
    bigc = ar.alloc([8, TPC], F32)
    erow_sb = ar.alloc([8, 1], F32)
    pos_tm = ar.alloc([128, NT, 8], F32)
    posu = ar.alloc([128, NT, 8], U32)
    p.op("sync", lambda e: e.dma_start(out=erow_sb, in_=erow), sem=s_kh, k=16)
    p.op("vector", lambda e: e.memset(ones8, 1.0), sem=s_bi)
    p.op("vector", lambda e: e.memset(bigc, float(CAP)), sem=s_bi)
    p.wait("vector", s_kh)
    p.wait("vector", s_bi)
    p.op("vector", lambda e: e.tensor_scalar(
        maskf, aem[0:8, :], bis[0:8, 0:1], None, ALU.is_ge, ALU.bypass), sem=s_bi)
    if dbg and "maskf" in dbg:
        p.wait("scalar", s_bi)
        p.op("scalar", lambda e: e.dma_start(out=dbg["maskf"], in_=maskf), sem=s_kh, k=16)
        p.wait("vector", s_kh)
    p.wait("vector", s_bi)
    p.op("vector", lambda e: e.tensor_tensor_scan(
        posem, ones8, maskf, 0.0, ALU.mult, ALU.add), sem=s_bi)
    if dbg and "scan" in dbg:
        p.wait("scalar", s_bi)
        p.op("scalar", lambda e: e.dma_start(out=dbg["scan"], in_=posem), sem=s_kh, k=16)
        p.wait("vector", s_kh)
    p.wait("vector", s_bi)
    p.op("vector", lambda e: e.tensor_scalar(
        posem, posem, -1.0, None, ALU.add, ALU.bypass), sem=s_bi)
    p.wait("vector", s_bi)
    p.op("vector", lambda e: e.tensor_scalar(
        selmf, posem, float(CAP), None, ALU.is_lt, ALU.bypass), sem=s_bi)
    p.wait("vector", s_bi)
    p.op("vector", lambda e: e.tensor_tensor(
        selmf, selmf, maskf, ALU.mult), sem=s_bi)
    p.wait("vector", s_bi)
    p.op("vector", lambda e: e.tensor_scalar(
        selm, selmf, 0.5, None, ALU.is_ge, ALU.bypass), sem=s_bi)
    p.wait("vector", s_bi)
    p.op("vector", lambda e: e.select(posx, selm, posem, bigc), sem=s_bi)
    p.wait("vector", s_bi)
    p.op("vector", lambda e: e.tensor_scalar(
        posem, posx, erow_sb, None, ALU.add, ALU.bypass), sem=s_bi)
    p.op("vector", lambda e: e.memset(stg, 0.0), sem=s_bi)
    for t in range(NT):
        p.wait("vector", s_bi)
        p.wait("vector", s_tmm)
        p.op("vector", lambda e, t=t: e.tensor_copy(
            stg[0:8, :], posem[0:8, t * 128:(t + 1) * 128]), sem=s_bi)
        p.wait("tensor", s_bi)
        slot = p.val(s_tmm) % 2
        ps = psum.t_u8[:, 12288 + slot * 2048: 12288 + slot * 2048 + 512].bitcast(F32)
        p.wait("tensor", s_tev, p.val(s_tev) - 1)
        p.op("tensor", lambda e, ps=ps: e.transpose(
            ps, stg, cons["ident_f32"]), sem=s_tmm)
        p.wait("vector", s_tmm)
        p.op("vector", lambda e, ps=ps, t=t: e.tensor_copy(
            pos_tm[:, t, :], ps[:, 0:8]), sem=s_tev)
    p.wait("vector", s_tev)
    p.op("vector", lambda e: e.tensor_copy(posu, pos_tm), sem=s_bi)
    p.wait("vector", s_ev)
    p.op("vector", lambda e: e.tensor_copy(comb[:, :, 1024:1032], affin), sem=s_bi)
    p.wait("vector", s_kh)
    p.op("vector", lambda e: e.tensor_copy(
        comb[:, :, 1032:1033], idsb.bitcast(BF16)), sem=s_bi)
    p.wait("gpsimd", s_bi)
    p.wait("gpsimd", s_pn)
    p.wait("gpsimd", wp["s_zr"], wp["zr_total"])
    for e_ in range(nexp):
        for t in range(NT):
            p.op("gpsimd", lambda e, e_=e_, t=t: e.indirect_dma_start(
                out=compact, out_offset=IndirectOffsetOnAxis(
                    ap=posu[:, t, e_:e_ + 1], axis=0),
                in_=comb[:, t, :], in_offset=None), sem=s_sca, k=16)

    if dbg and "posu" in dbg:
        p.wait("scalar", s_bi)
        p.op("scalar", lambda e: e.dma_start(out=dbg["posu"], in_=posu), sem=s_kh, k=16)
        p.wait("scalar", s_kh)
    if dbg and "cmp" in dbg:
        p.wait("gpsimd", s_sca)
        p.op("gpsimd", lambda e: e.dma_start(out=dbg["cmp"], in_=compact), sem=s_sca, k=16)
    if stage < 3:
        return s_sca

    ar.release()
    ar.mark()
    rb = ar.alloc([128, 5, 1040], BF16)
    xin_fm = ar.alloc([128, 8, CAP], BF16)
    h2 = ar.alloc([128, 32, CAP], BF16)
    xe_sb = ar.alloc([128, 5, 1024], BF16)
    gat = ar.alloc([128, 5, 1], F32)
    idf = ar.alloc([128, 5, 1], F32)
    idxf = ar.alloc([128, 5, 1], F32)
    idxu = ar.alloc([128, 5, 1], U32)
    selg = ar.alloc([128, 5, 1], U8)
    trash_sb = ar.alloc([128, 5, 1], F32)
    wgb = [ar.alloc([128, 8, 128], BF16) for _ in range(2)]
    wub = [ar.alloc([128, 8, 128], BF16) for _ in range(2)]
    wdb = [ar.alloc([128, 512], BF16) for _ in range(2)]
    p.op("sync", lambda e: e.dma_start(out=trash_sb, in_=trasht), sem=s_kh, k=16)
    p.wait("vector", s_kh)
    p.wait("sync", s_cc, 2)
    for e_ in range(nexp):
        p.wait("sync", s_sca)
        p.wait("sync", s_tev)
        p.wait("sync", s_hh)
        p.op("sync", lambda e, e_=e_: e.dma_start(
            out=rb, in_=compact[e_ * CROW: e_ * CROW + CAP, :].rearrange(
                "(c p) w -> p c w", p=128)), sem=s_kh, k=16)
        p.wait("vector", s_kh)
        if e_ > 0:
            p.wait("vector", s_add)
        p.op("vector", lambda e, e_=e_: e.tensor_copy(
            gat, rb[:, :, 1024 + e_:1025 + e_]), sem=s_ev)
        p.op("vector", lambda e: e.tensor_copy(
            idf, rb[:, :, 1032:1033].bitcast(I16)), sem=s_ev)
        p.wait("vector", s_ev)
        p.op("vector", lambda e: e.tensor_scalar(
            selg, gat, 0.0, None, ALU.is_equal, ALU.bypass), sem=s_ev)
        p.wait("vector", s_ev)
        p.op("vector", lambda e: e.select(idxf, selg, trash_sb, idf), sem=s_ev)
        p.wait("vector", s_ev)
        p.op("vector", lambda e: e.tensor_copy(idxu, idxf), sem=s_ev)
        p.wait("tensor", s_kh)
        for c5 in range(5):
            for dt in range(8):
                slot = p.val(s_tmm) % 2
                ps = psum.t_u8[:, 12288 + slot * 2048:
                               12288 + slot * 2048 + 256].bitcast(BF16)
                p.wait("tensor", s_tev, p.val(s_tev) - 1)
                p.op("tensor", lambda e, ps=ps, c5=c5, dt=dt: e.transpose(
                    ps, rb[:, c5, dt * 128:(dt + 1) * 128], cons["ident_bf"]),
                    sem=s_tmm)
                p.wait("vector", s_tmm)
                p.op("vector", lambda e, ps=ps, c5=c5, dt=dt: e.tensor_copy(
                    xin_fm[:, dt, c5 * 128:(c5 + 1) * 128], ps), sem=s_tev)
        p.wait("tensor", s_tev)
        p.wait("tensor", s_xe)
        for it in range(32):
            for wb, hbm in ((wgb[it % 2], wg_v), (wub[it % 2], wu_v)):
                if p.val(s_mm) >= 2:
                    p.wait("sync", s_mm, p.val(s_mm) - 1)
                p.op("sync", lambda e, wb=wb, hbm=hbm, e_=e_, it=it: e.dma_start(
                    out=wb, in_=hbm[e_, it].rearrange("p (d c) -> p d c", d=8)),
                    sem=s_ws[it % 2], k=16)
            p.wait("tensor", s_ws[it % 2])
            b0 = (it % 2) * 2048
            psg0 = psum.f(b0, 512)
            psg1 = psum.f(b0 + 512, 512)[:, 0:128]
            psu0 = psum.f(b0 + 1024, 512)
            psu1 = psum.f(b0 + 1536, 512)[:, 0:128]
            if it >= 2:
                p.wait("tensor", s_ev, p.val(s_ev) - 2)
                p.wait("tensor", s_hh, p.val(s_hh) - 2)
            for dt in range(8):
                last = dt == 7
                p.op("tensor", lambda e, ps=psg0, wb=wgb[it % 2], dt=dt: e.matmul(
                    ps, wb[:, dt, :], xin_fm[:, dt, 0:512],
                    start=(dt == 0), stop=(dt == 7)))
                p.op("tensor", lambda e, ps=psg1, wb=wgb[it % 2], dt=dt: e.matmul(
                    ps, wb[:, dt, :], xin_fm[:, dt, 512:640],
                    start=(dt == 0), stop=(dt == 7)))
                p.op("tensor", lambda e, ps=psu0, wb=wub[it % 2], dt=dt: e.matmul(
                    ps, wb[:, dt, :], xin_fm[:, dt, 0:512],
                    start=(dt == 0), stop=(dt == 7)))
                p.op("tensor", lambda e, ps=psu1, wb=wub[it % 2], dt=dt: e.matmul(
                    ps, wb[:, dt, :], xin_fm[:, dt, 512:640],
                    start=(dt == 0), stop=(dt == 7)),
                    sem=(s_mm if last else None))
            p.wait("scalar", s_mm)
            p.op("scalar", lambda e, ps=psg0, it=it: e.activation(
                h2[:, it, 0:512], ps, AF.Silu, bias=0.0, scale=1.0), sem=s_ev)
            p.op("scalar", lambda e, ps=psg1, it=it: e.activation(
                h2[:, it, 512:640], ps, AF.Silu, bias=0.0, scale=1.0), sem=s_ev)
            p.wait("vector", s_ev)
            p.op("vector", lambda e, ps=psu0, it=it: e.tensor_tensor(
                h2[:, it, 0:512], h2[:, it, 0:512], ps, ALU.mult), sem=s_hh)
            p.op("vector", lambda e, ps=psu1, it=it: e.tensor_tensor(
                h2[:, it, 512:640], h2[:, it, 512:640], ps, ALU.mult), sem=s_hh)
        if e_ == 0:
            p.wait("sync", s_cc, 3)
        for half in range(2):
            p.wait("tensor", s_ev)
            p.wait("tensor", s_hh)
            if half == 1:
                p.wait("tensor", s_xe)
            for it in range(32):
                wb = wdb[it % 2]
                if p.val(s_dm) >= 2:
                    p.wait("sync", s_dm, p.val(s_dm) - 1)
                p.op("sync", lambda e, wb=wb, e_=e_, it=it, half=half: e.dma_start(
                    out=wb, in_=wd_v[e_, it][:, half * 512:(half + 1) * 512]),
                    sem=s_wd[it % 2], k=16)
                p.wait("tensor", s_wd[it % 2])
                for jc in range(5):
                    p.op("tensor", lambda e, jc=jc, wb=wb, it=it: e.matmul(
                        psum.f(jc * 512, 512), h2[:, it, jc * 128:(jc + 1) * 128],
                        wb, start=(it == 0), stop=(it == 31)),
                        sem=(s_dm if jc == 4 else None))
            p.wait("vector", s_dm)
            for jc in range(5):
                p.op("vector", lambda e, jc=jc, half=half: e.tensor_scalar(
                    xe_sb[:, jc, half * 512:(half + 1) * 512],
                    psum.f(jc * 512, 512), gat[:, jc, :], None, ALU.mult,
                    ALU.bypass), sem=s_xe)
        p.wait("gpsimd", s_xe)
        p.wait("gpsimd", s_add)
        if e_ == 0:
            p.wait("gpsimd", wp["s_zr"], wp["zr_total"])
        for jc in range(5):
            p.op("gpsimd", lambda e, jc=jc: e.indirect_dma_start(
                out=moeacc, out_offset=IndirectOffsetOnAxis(
                    ap=idxu[:, jc, :], axis=0),
                in_=xe_sb[:, jc, :], in_offset=None, compute_op=ALU.add),
                sem=s_add, k=16)

    if stage < 4:
        return s_sca

    if dbg and "acc" in dbg:
        p.wait("gpsimd", s_sca, p.val(s_sca))
        p.op("gpsimd", lambda e: e.dma_start(out=dbg["acc"], in_=moeacc), sem=s_sca, k=16)
    yb = [ar.alloc([128, 1024], F32) for _ in range(2)]
    s_yl = [p.sem(pref + "yl0"), p.sem(pref + "yl1")]
    p.wait("sync", s_sca)
    p.wait("sync", s_add)
    for t in range(NT):
        b = yb[t % 2]
        if t >= 2:
            p.wait("sync", s_fin, p.val(s_fin) - 1)
        p.op("sync", lambda e, b=b, t=t: e.dma_start(
            out=b, in_=moeacc.rearrange("(t p) d -> p t d", p=128)[:, t, :]),
            sem=s_yl[t % 2], k=16)
        p.wait("vector", s_yl[t % 2])
        p.op("vector", lambda e, b=b, t=t: e.tensor_tensor(
            x_sb[:, t, :], x_sb[:, t, :], b, ALU.add), sem=s_fin)
    ar.release()
    return s_fin


def l0f_host_inputs(core_id):
    m32 = np.zeros((32, 32), dtype=np.float32)
    for i in range(32):
        for j in range(32):
            if i % 8 == j % 8:
                m32[i, j] = 1.0
    erow = (np.arange(8, dtype=np.float32) * CROW).reshape(8, 1)
    idrow = np.zeros((128, NT, 1), dtype=np.int16)
    for t in range(NT):
        idrow[:, t, 0] = t * 128 + np.arange(128)
    trasht = np.zeros((128, 5, 1), dtype=np.float32)
    for c5 in range(5):
        trasht[:, c5, 0] = TPC + c5 * 128 + np.arange(128)
    return {"m32": m32, "erow": erow, "idrow": idrow,
            "trasht": trasht}



def build_l0a_launch(nchunks=4, stage=9):
    nc = bass.Bass("TRN2", target_bir_lowering=False)
    xin = nc.dram_tensor("xin", [TPC, D], F32, kind="ExternalInput").ap()
    wq = nc.dram_tensor("wq", [1024, 1024], F32, kind="ExternalInput").ap()
    wk = nc.dram_tensor("wk", [1024, 1024], F32, kind="ExternalInput").ap()
    wv = nc.dram_tensor("wv", [1024, 1024], F32, kind="ExternalInput").ap()
    cin = declare_consts(nc)
    xout = nc.dram_tensor("xout", [TPC, D], F32, kind="ExternalOutput").ap()

    p = Plan(nc)
    ar = Arena(nc, p._stack)
    psum = Psum(nc, p._stack)
    cons = load_consts(p, ar, cin)
    s_ld = p.sem("ldx")
    x_sb = ar.alloc([128, NT, 1024], F32)
    p.op("sync", lambda e: e.dma_start(
        out=x_sb, in_=xin.rearrange("(t p) d -> p t d", p=128)), sem=s_ld, k=16)
    for eng in ("tensor", "vector", "scalar"):
        p.wait(eng, s_ld, 16)
    s_done = emit_l0a(p, ar, psum, cons, x_sb, wq, wk, wv, nchunks=nchunks, stage=stage)
    s_st = p.sem("st")
    p.wait("scalar", s_done)
    p.op("scalar", lambda e: e.dma_start(
        out=xout.rearrange("(t p) d -> p t d", p=128), in_=x_sb), sem=s_st, k=16)
    p.wait("scalar", s_st, 16)
    p.run()
    return nc


def l1a_host_inputs(core_slot):
    inv = 1.0 / 10000.0 ** (np.arange(0, 32, 2, dtype=np.float32) / 32.0)
    t = np.arange(SEQ, dtype=np.float32)
    fr = np.outer(t, inv)
    cos, sin = np.cos(fr).astype(np.float32), np.sin(fr).astype(np.float32)
    own = np.concatenate([np.arange(b * BLK, (b + 1) * BLK) for b in BLOCKS[core_slot]])
    masks = np.zeros((4, 128, 512), dtype=np.float32)
    for j in range(4):
        for r in range(128):
            masks[j, r, :] = np.where(np.arange(512) >= r + 128 * j, 0.0, -1e30)
    masks = masks.astype(bfloat16)
    hg = core_slot
    qidx = np.zeros((128, 4, 4), dtype=np.uint32)
    for hi in range(4):
        for r in range(4):
            for pp in range(64):
                qidx[pp, hi, r] = r * 1024 + hg * 256 + hi * 64 + pp
    attnidx = np.zeros((128, 8), dtype=np.uint32)
    for dt in range(8):
        for pp in range(128):
            attnidx[pp, dt] = ((dt // 2) * 128 + pp) * 4 + hg
    return {
        "cos_own": cos[own], "sin_own": sin[own],
        "cos_all": cos, "sin_all": sin,
        "masks": masks, "qidx": qidx, "attnidx": attnidx,
    }


def build_l1a_launch(nheads=4, ngb=16, stage=9):
    nc = bass.Bass("TRN2", target_bir_lowering=False)
    xin = nc.dram_tensor("xin", [TPC, D], F32, kind="ExternalInput").ap()
    w_kvd = nc.dram_tensor("w_kvd", [1024, 128], F32, kind="ExternalInput").ap()
    w_q = nc.dram_tensor("w_q", [1024, 1024], F32, kind="ExternalInput").ap()
    w_qr = nc.dram_tensor("w_qr", [1024, 32], F32, kind="ExternalInput").ap()
    w_kr = nc.dram_tensor("w_kr", [128, 32], F32, kind="ExternalInput").ap()
    w_kup_loc = nc.dram_tensor("w_kup_loc", [128, 256], F32, kind="ExternalInput").ap()
    w_vup_loc = nc.dram_tensor("w_vup_loc", [128, 256], F32, kind="ExternalInput").ap()
    w_o = nc.dram_tensor("w_o", [1024, 1024], F32, kind="ExternalInput").ap()
    cos_own = nc.dram_tensor("cos_own", [TPC, 16], F32, kind="ExternalInput").ap()
    sin_own = nc.dram_tensor("sin_own", [TPC, 16], F32, kind="ExternalInput").ap()
    cos_all = nc.dram_tensor("cos_all", [SEQ, 16], F32, kind="ExternalInput").ap()
    sin_all = nc.dram_tensor("sin_all", [SEQ, 16], F32, kind="ExternalInput").ap()
    masks = nc.dram_tensor("masks", [4, 128, 512], BF16, kind="ExternalInput").ap()
    qidx = nc.dram_tensor("qidx", [128, 4, 4], U32, kind="ExternalInput").ap()
    attnidx = nc.dram_tensor("attnidx", [128, 8], U32, kind="ExternalInput").ap()
    cin = declare_consts(nc)
    xout = nc.dram_tensor("xout", [TPC, D], F32, kind="ExternalOutput").ap()

    p = Plan(nc)
    ar = Arena(nc, p._stack)
    psum = Psum(nc, p._stack)
    cons = load_consts(p, ar, cin)
    s_ld = p.sem("ldx")
    x_sb = ar.alloc([128, NT, 1024], F32)
    p.op("sync", lambda e: e.dma_start(
        out=x_sb, in_=xin.rearrange("(t p) d -> p t d", p=128)), sem=s_ld, k=16)
    for eng in ("tensor", "vector", "scalar"):
        p.wait(eng, s_ld, 16)
    s_done = emit_l1a(p, ar, psum, cons, x_sb, w_kvd, w_q, w_qr, w_kr,
                      w_kup_loc, w_vup_loc, w_o, cos_own, sin_own, cos_all,
                      sin_all, masks, qidx, attnidx, nheads=nheads, ngb=ngb, stage=stage)
    s_st = p.sem("st")
    p.wait("scalar", s_done)
    p.op("scalar", lambda e: e.dma_start(
        out=xout.rearrange("(t p) d -> p t d", p=128), in_=x_sb), sem=s_st, k=16)
    p.wait("scalar", s_st, 16)
    p.run()
    return nc


def build_l1f_launch(tbs=4, stage=9):
    nc = bass.Bass("TRN2", target_bir_lowering=False)
    xin = nc.dram_tensor("xin", [TPC, D], F32, kind="ExternalInput").ap()
    w_fc = nc.dram_tensor("w_fc", [4096, 1024], F32, kind="ExternalInput").ap()
    w_proj = nc.dram_tensor("w_proj", [1024, 4096], F32, kind="ExternalInput").ap()
    cin = declare_consts(nc)
    xout = nc.dram_tensor("xout", [TPC, D], F32, kind="ExternalOutput").ap()

    p = Plan(nc)
    ar = Arena(nc, p._stack)
    psum = Psum(nc, p._stack)
    cons = load_consts(p, ar, cin)
    s_ld = p.sem("ldx")
    x_sb = ar.alloc([128, NT, 1024], F32)
    p.op("sync", lambda e: e.dma_start(
        out=x_sb, in_=xin.rearrange("(t p) d -> p t d", p=128)), sem=s_ld, k=16)
    for eng in ("tensor", "vector", "scalar"):
        p.wait(eng, s_ld, 16)
    wp1 = emit_l1f_wprep(p, ar, psum, cons, w_fc, w_proj)
    s_done = emit_l1f(p, ar, psum, cons, x_sb, wp1, tbs=tbs, stage=stage)
    s_st = p.sem("st")
    p.wait("scalar", s_done)
    p.op("scalar", lambda e: e.dma_start(
        out=xout.rearrange("(t p) d -> p t d", p=128), in_=x_sb), sem=s_st, k=16)
    p.wait("scalar", s_st, 16)
    p.run()
    return nc


def build_l0f_launch(nexp=8, stage=9):
    nc = bass.Bass("TRN2", target_bir_lowering=False)
    xin = nc.dram_tensor("xin", [TPC, D], F32, kind="ExternalInput").ap()
    gate_w = nc.dram_tensor("gate_w", [8, 1024], F32, kind="ExternalInput").ap()
    wg_loc = nc.dram_tensor("wg_loc", [4096, 1024], F32, kind="ExternalInput").ap()
    wu_loc = nc.dram_tensor("wu_loc", [4096, 1024], F32, kind="ExternalInput").ap()
    wd_loc = nc.dram_tensor("wd_loc", [1024, 4096], F32, kind="ExternalInput").ap()
    m32 = nc.dram_tensor("m32", [32, 32], F32, kind="ExternalInput").ap()
    erow = nc.dram_tensor("erow", [8, 1], F32, kind="ExternalInput").ap()
    idrow = nc.dram_tensor("idrow", [128, NT, 1], I16, kind="ExternalInput").ap()
    trasht = nc.dram_tensor("trasht", [128, 5, 1], F32, kind="ExternalInput").ap()
    cin = declare_consts(nc)
    xout = nc.dram_tensor("xout", [TPC, D], F32, kind="ExternalOutput").ap()

    p = Plan(nc)
    ar = Arena(nc, p._stack)
    psum = Psum(nc, p._stack)
    cons = load_consts(p, ar, cin)
    s_ld = p.sem("ldx")
    x_sb = ar.alloc([128, NT, 1024], F32)
    p.op("sync", lambda e: e.dma_start(
        out=x_sb, in_=xin.rearrange("(t p) d -> p t d", p=128)), sem=s_ld, k=16)
    for eng in ("tensor", "vector", "scalar"):
        p.wait(eng, s_ld, 16)
    wp0 = emit_l0f_wprep(p, ar, psum, cons, wg_loc, wu_loc, wd_loc)
    s_done = emit_l0f(p, ar, psum, cons, x_sb, gate_w, wp0,
                      m32, erow, idrow, trasht, nexp=nexp, stage=stage)
    s_st = p.sem("st")
    p.wait("scalar", s_done)
    p.op("scalar", lambda e: e.dma_start(
        out=xout.rearrange("(t p) d -> p t d", p=128), in_=x_sb), sem=s_st, k=16)
    p.wait("scalar", s_st, 16)
    p.run()
    return nc


def build_fused_launch(stage=9):
    nc = bass.Bass("TRN2", target_bir_lowering=False)
    xin = nc.dram_tensor("xin", [TPC, D], F32, kind="ExternalInput").ap()
    wq = nc.dram_tensor("wq", [1024, 1024], F32, kind="ExternalInput").ap()
    wk = nc.dram_tensor("wk", [1024, 1024], F32, kind="ExternalInput").ap()
    wv = nc.dram_tensor("wv", [1024, 1024], F32, kind="ExternalInput").ap()
    gate_w = nc.dram_tensor("gate_w", [8, 1024], F32, kind="ExternalInput").ap()
    wg_loc = nc.dram_tensor("wg_loc", [4096, 1024], F32, kind="ExternalInput").ap()
    wu_loc = nc.dram_tensor("wu_loc", [4096, 1024], F32, kind="ExternalInput").ap()
    wd_loc = nc.dram_tensor("wd_loc", [1024, 4096], F32, kind="ExternalInput").ap()
    m32 = nc.dram_tensor("m32", [32, 32], F32, kind="ExternalInput").ap()
    erow = nc.dram_tensor("erow", [8, 1], F32, kind="ExternalInput").ap()
    idrow = nc.dram_tensor("idrow", [128, NT, 1], I16, kind="ExternalInput").ap()
    trasht = nc.dram_tensor("trasht", [128, 5, 1], F32, kind="ExternalInput").ap()
    w_kvd = nc.dram_tensor("w_kvd", [1024, 128], F32, kind="ExternalInput").ap()
    w_q = nc.dram_tensor("w_q", [1024, 1024], F32, kind="ExternalInput").ap()
    w_qr = nc.dram_tensor("w_qr", [1024, 32], F32, kind="ExternalInput").ap()
    w_kr = nc.dram_tensor("w_kr", [128, 32], F32, kind="ExternalInput").ap()
    w_kup_loc = nc.dram_tensor("w_kup_loc", [128, 256], F32, kind="ExternalInput").ap()
    w_vup_loc = nc.dram_tensor("w_vup_loc", [128, 256], F32, kind="ExternalInput").ap()
    w_o = nc.dram_tensor("w_o", [1024, 1024], F32, kind="ExternalInput").ap()
    cos_own = nc.dram_tensor("cos_own", [TPC, 16], F32, kind="ExternalInput").ap()
    sin_own = nc.dram_tensor("sin_own", [TPC, 16], F32, kind="ExternalInput").ap()
    cos_all = nc.dram_tensor("cos_all", [SEQ, 16], F32, kind="ExternalInput").ap()
    sin_all = nc.dram_tensor("sin_all", [SEQ, 16], F32, kind="ExternalInput").ap()
    masks = nc.dram_tensor("masks", [4, 128, 512], BF16, kind="ExternalInput").ap()
    qidx = nc.dram_tensor("qidx", [128, 4, 4], U32, kind="ExternalInput").ap()
    attnidx = nc.dram_tensor("attnidx", [128, 8], U32, kind="ExternalInput").ap()
    w_fc = nc.dram_tensor("w_fc", [4096, 1024], F32, kind="ExternalInput").ap()
    w_proj = nc.dram_tensor("w_proj", [1024, 4096], F32, kind="ExternalInput").ap()
    cin = declare_consts(nc)
    xout = nc.dram_tensor("xout", [TPC, D], F32, kind="ExternalOutput").ap()

    p = Plan(nc)
    ar = Arena(nc, p._stack)
    psum = Psum(nc, p._stack)
    cons = load_consts(p, ar, cin)
    s_ld = p.sem("ldx")
    x_sb = ar.alloc([128, NT, 1024], F32)
    p.op("sync", lambda e: e.dma_start(
        out=x_sb, in_=xin.rearrange("(t p) d -> p t d", p=128)), sem=s_ld, k=16)
    for eng in ("tensor", "vector", "scalar", "gpsimd"):
        p.wait(eng, s_ld, 16)

    wps = {}

    def prelude():
        wps["wp0"] = emit_l0f_wprep(p, ar, psum, cons, wg_loc, wu_loc, wd_loc)
        wps["wp1"] = emit_l1f_wprep(p, ar, psum, cons, w_fc, w_proj)
        return wps["wp0"]["gates"] + wps["wp1"]["gates"]

    ar.mark()
    s_done = emit_l0a(p, ar, psum, cons, x_sb, wq, wk, wv, stage=stage,
                      prelude=prelude)
    wp0, wp1 = wps["wp0"], wps["wp1"]
    for eng in ENGS:
        p.wait(eng, s_done)
    ar.release()
    ar.mark()
    s_done = emit_l0f(p, ar, psum, cons, x_sb, gate_w, wp0,
                      m32, erow, idrow, trasht, stage=stage)
    for eng in ENGS:
        p.wait(eng, s_done)
    ar.release()
    ar.mark()
    s_done = emit_l1a(p, ar, psum, cons, x_sb, w_kvd, w_q, w_qr, w_kr,
                      w_kup_loc, w_vup_loc, w_o, cos_own, sin_own, cos_all,
                      sin_all, masks, qidx, attnidx, stage=stage)
    for eng in ENGS:
        p.wait(eng, s_done)
    ar.release()
    ar.mark()
    s_done = emit_l1f(p, ar, psum, cons, x_sb, wp1, stage=stage)
    ar.release()
    s_st = p.sem("st")
    p.wait("scalar", s_done)
    p.op("scalar", lambda e: e.dma_start(
        out=xout.rearrange("(t p) d -> p t d", p=128), in_=x_sb), sem=s_st, k=16)
    p.wait("scalar", s_st, 16)
    p.run()
    return nc


FUSED = True

_NC_CACHE = {}


def _get_nc(name, builder):
    if name not in _NC_CACHE:
        _NC_CACHE[name] = builder()
    return _NC_CACHE[name]


def kernel(**inputs):
    f32 = np.float32
    x = np.ascontiguousarray(np.asarray(inputs["L0_attn_x"], dtype=f32))
    cin = const_inputs()
    shards = shard_rows(x)

    if FUSED:
        nc = _get_nc("fused", build_fused_launch)
        wg = np.asarray(inputs["L0_ffn_w_gate"], f32)
        wu = np.asarray(inputs["L0_ffn_w_up"], f32)
        wd = np.asarray(inputs["L0_ffn_w_down"], f32)
        wkup = np.asarray(inputs["L1_attn_W_k_up"], f32)
        wvup = np.asarray(inputs["L1_attn_W_v_up"], f32)
        ca = np.ascontiguousarray
        in_maps = []
        for c in range(NCORES):
            hg = c % 4
            in_maps.append({
                "xin": shards[c],
                "wq": ca(np.asarray(inputs["L0_attn_Wq"], f32)),
                "wk": ca(np.asarray(inputs["L0_attn_Wk"], f32)),
                "wv": ca(np.asarray(inputs["L0_attn_Wv"], f32)),
                "gate_w": ca(np.asarray(inputs["L0_ffn_gate_w"], f32)),
                "wg_loc": ca(wg[c]), "wu_loc": ca(wu[c]), "wd_loc": ca(wd[c]),
                "w_kvd": ca(np.asarray(inputs["L1_attn_W_kv_down"], f32)),
                "w_q": ca(np.asarray(inputs["L1_attn_W_q"], f32)),
                "w_qr": ca(np.asarray(inputs["L1_attn_W_q_rope"], f32)),
                "w_kr": ca(np.asarray(inputs["L1_attn_W_k_rope"], f32)),
                "w_kup_loc": ca(wkup[:, hg * 256:(hg + 1) * 256]),
                "w_vup_loc": ca(wvup[:, hg * 256:(hg + 1) * 256]),
                "w_o": ca(np.asarray(inputs["L1_attn_W_o"], f32)),
                "w_fc": ca(np.asarray(inputs["L1_ffn_W_fc"], f32)),
                "w_proj": ca(np.asarray(inputs["L1_ffn_W_proj"], f32)),
                **l0f_host_inputs(c), **l1a_host_inputs(hg), **cin})
        res = run_bass_kernel_spmd(nc, in_maps, core_ids=list(range(NCORES)))
        return unshard_rows([res.results[c]["xout"] for c in range(NCORES)]
                            ).astype(np.float32)

    nc = _get_nc("l0a", build_l0a_launch)
    in_maps = [{"xin": shards[c],
                "wq": np.ascontiguousarray(np.asarray(inputs["L0_attn_Wq"], f32)),
                "wk": np.ascontiguousarray(np.asarray(inputs["L0_attn_Wk"], f32)),
                "wv": np.ascontiguousarray(np.asarray(inputs["L0_attn_Wv"], f32)),
                **cin} for c in range(NCORES)]
    res = run_bass_kernel_spmd(nc, in_maps, core_ids=list(range(NCORES)))
    shards = [np.ascontiguousarray(res.results[c]["xout"]) for c in range(NCORES)]

    nc = _get_nc("l0f", build_l0f_launch)
    wg = np.asarray(inputs["L0_ffn_w_gate"], f32)
    wu = np.asarray(inputs["L0_ffn_w_up"], f32)
    wd = np.asarray(inputs["L0_ffn_w_down"], f32)
    in_maps = [{"xin": shards[c],
                "gate_w": np.ascontiguousarray(np.asarray(inputs["L0_ffn_gate_w"], f32)),
                "wg_loc": np.ascontiguousarray(wg[c]),
                "wu_loc": np.ascontiguousarray(wu[c]),
                "wd_loc": np.ascontiguousarray(wd[c]),
                **l0f_host_inputs(c), **cin} for c in range(NCORES)]
    res = run_bass_kernel_spmd(nc, in_maps, core_ids=list(range(NCORES)))
    shards = [np.ascontiguousarray(res.results[c]["xout"]) for c in range(NCORES)]

    nc = _get_nc("l1a", build_l1a_launch)
    wkup = np.asarray(inputs["L1_attn_W_k_up"], f32)
    wvup = np.asarray(inputs["L1_attn_W_v_up"], f32)
    in_maps = []
    for c in range(NCORES):
        hg = c % 4
        in_maps.append({
            "xin": shards[c],
            "w_kvd": np.ascontiguousarray(np.asarray(inputs["L1_attn_W_kv_down"], f32)),
            "w_q": np.ascontiguousarray(np.asarray(inputs["L1_attn_W_q"], f32)),
            "w_qr": np.ascontiguousarray(np.asarray(inputs["L1_attn_W_q_rope"], f32)),
            "w_kr": np.ascontiguousarray(np.asarray(inputs["L1_attn_W_k_rope"], f32)),
            "w_kup_loc": np.ascontiguousarray(wkup[:, hg * 256:(hg + 1) * 256]),
            "w_vup_loc": np.ascontiguousarray(wvup[:, hg * 256:(hg + 1) * 256]),
            "w_o": np.ascontiguousarray(np.asarray(inputs["L1_attn_W_o"], f32)),
            **l1a_host_inputs(hg), **cin})
    res = run_bass_kernel_spmd(nc, in_maps, core_ids=list(range(NCORES)))
    shards = [np.ascontiguousarray(res.results[c]["xout"]) for c in range(NCORES)]

    nc = _get_nc("l1f", build_l1f_launch)
    in_maps = [{"xin": shards[c],
                "w_fc": np.ascontiguousarray(np.asarray(inputs["L1_ffn_W_fc"], f32)),
                "w_proj": np.ascontiguousarray(np.asarray(inputs["L1_ffn_W_proj"], f32)),
                **cin} for c in range(NCORES)]
    res = run_bass_kernel_spmd(nc, in_maps, core_ids=list(range(NCORES)))
    shards = [res.results[c]["xout"] for c in range(NCORES)]

    return unshard_rows(shards).astype(np.float32)


# version 18
# speedup vs baseline: 1.0354x; 1.0354x over previous
import contextlib

import numpy as np
import ml_dtypes

import concourse.bass as bass
import concourse.mybir as mybir
from concourse.bass_utils import run_bass_kernel_spmd
from concourse.bass import IndirectOffsetOnAxis, ds

F32 = mybir.dt.float32
BF16 = mybir.dt.bfloat16
U32 = mybir.dt.uint32
I32 = mybir.dt.int32
I16 = mybir.dt.int16
U8 = mybir.dt.uint8
AF = mybir.ActivationFunctionType
ALU = mybir.AluOpType
bfloat16 = ml_dtypes.bfloat16

NCORES = 8
D = 1024
SEQ = 8192
B = 2
BLK = 512
TPC = 2048
NT = TPC // 128
BLOCKS = [sorted(s) for s in ([0, 15, 7, 8], [1, 14, 6, 9], [2, 13, 5, 10], [3, 12, 4, 11])]
LN_EPS = 1e-5

ENGS = ("sync", "tensor", "vector", "scalar", "gpsimd")



class Plan:

    def __init__(self, nc):
        self.nc = nc
        self.ops = {e: [] for e in ENGS}
        self.sem_handles = {}
        self.cnt = {}
        self._stack = contextlib.ExitStack()

    def sem(self, name):
        assert name not in self.sem_handles
        self.sem_handles[name] = self._stack.enter_context(self.nc.semaphore(name))
        self.cnt[name] = 0
        return name

    def emit(self, eng, fn):
        self.ops[eng].append(fn)

    def op(self, eng, make, sem=None, k=1):
        h = self.sem_handles[sem] if sem else None
        if sem:
            self.cnt[sem] += k

        def fn(e):
            i = make(e)
            if h is not None:
                i.then_inc(h, k)
        self.emit(eng, fn)
        if sem:
            return self.cnt[sem]

    def wait(self, eng, sem, value=None):
        v = self.cnt[sem] if value is None else value
        if v <= 0:
            return
        h = self.sem_handles[sem]
        self.emit(eng, lambda e: e.wait_ge(h, v))

    def val(self, sem):
        return self.cnt[sem]

    def barrier(self, sem_suffix, producers_done):
        for eng in ENGS:
            for sem, v in producers_done:
                self.wait(eng, sem, v)

    def run(self):
        with self._stack:
            with self.nc.Block() as block:
                for name in ENGS:
                    def body(eng, _n=name):
                        for fn in self.ops[_n]:
                            fn(eng)
                    getattr(block, name)(body)


class Arena:

    def __init__(self, nc, stack, nbytes=206 * 1024, name="arena"):
        self.t = stack.enter_context(nc.sbuf_tensor(name, [128, nbytes], U8))
        self.off = 0
        self.nbytes = nbytes
        self.marks = []
        self.peak = 0

    def alloc(self, shape, dtype, align=128):
        assert shape[0] <= 128
        esz = {F32: 4, BF16: 2, U32: 4, I32: 4, I16: 2, U8: 1}[dtype]
        n = 1
        for s in shape[1:]:
            n *= s
        self.off = (self.off + align - 1) // align * align
        start = self.off
        self.off += n * esz
        self.peak = max(self.peak, self.off)
        limit = self.nbytes - getattr(self, "reserved", 0)
        assert self.off <= limit, f"arena overflow: {self.off} > {limit}"
        v = self.t[0:shape[0], start:start + n * esz]
        if dtype != U8:
            v = v.bitcast(dtype)
        if len(shape) > 2:
            dims = " ".join(f"d{i}" for i in range(1, len(shape)))
            v = v.rearrange(f"p ({dims}) -> p {dims}",
                            **{f"d{i}": shape[i] for i in range(1, len(shape) - 1)})
        return v

    def mark(self):
        self.marks.append(self.off)

    def release(self):
        self.off = self.marks.pop()


class ArenaTop:

    def __init__(self, ar, size):
        ar.reserved = size
        self.par = ar
        self.base = ar.nbytes - size
        self.off = self.base
        self.peak = self.base
        self.nbytes = ar.nbytes
        self.t = ar.t
        self.marks = []

    alloc = Arena.alloc
    mark = Arena.mark
    release = Arena.release



def shard_rows(x_full):
    shards = []
    for c in range(NCORES):
        b = c // 4
        rows = [x_full[b, blk * BLK:(blk + 1) * BLK] for blk in BLOCKS[c % 4]]
        shards.append(np.ascontiguousarray(np.concatenate(rows, axis=0)))
    return shards


def unshard_rows(shards):
    out = np.zeros((B, SEQ, D), dtype=shards[0].dtype)
    for c in range(NCORES):
        b = c // 4
        for i, blk in enumerate(BLOCKS[c % 4]):
            out[b, blk * BLK:(blk + 1) * BLK] = shards[c][i * BLK:(i + 1) * BLK]
    return out


def const_inputs():
    return {
        "ident_bf": np.eye(128, dtype=bfloat16),
        "ident_f32": np.eye(128, dtype=np.float32),
        "ones_f32": np.ones((1, 128), dtype=np.float32),
    }


CONST_SPECS = [("ident_bf", [128, 128], BF16),
               ("ident_f32", [128, 128], F32),
               ("ones_f32", [1, 128], F32)]


def declare_consts(nc):
    return {name: nc.dram_tensor(name, shape, dt, kind="ExternalInput").ap()
            for name, shape, dt in CONST_SPECS}


def load_consts(p, ar, cin):
    out = {}
    s = p.sem("ldc")
    for name, shape, dt in CONST_SPECS:
        out[name] = ar.alloc(shape, dt)
        p.op("sync", lambda e, d=out[name], src=cin[name]: e.dma_start(out=d, in_=src),
             sem=s, k=16)
    out["eps128"] = ar.alloc([128, 1], F32)
    p.op("vector", lambda e: e.memset(out["eps128"], LN_EPS), sem=s, k=1)
    out["onesc_f32"] = ar.alloc([128, 1], F32)
    p.op("vector", lambda e: e.memset(out["onesc_f32"], 1.0), sem=s, k=1)
    for eng in ("tensor", "vector", "scalar", "gpsimd"):
        p.wait(eng, s)
    return out



def emit_transposes_bf(p, psum, cons, src, dst_fn, rtiles, ctiles, s_tmm, s_tev,
                       psum_byte0=2048 * 4):
    nslot = 4
    for rt in range(rtiles):
        for ct in range(ctiles):
            slot = p.val(s_tmm) % nslot
            ps = psum.t_u8[:, psum_byte0 + slot * 2048: psum_byte0 + slot * 2048 + 256].bitcast(BF16)
            if p.val(s_tmm) >= nslot:
                p.wait("tensor", s_tev, p.val(s_tev) - nslot + 1)
            p.op("tensor", lambda e, ps=ps, rt=rt, ct=ct: e.transpose(
                ps, src[:, rt, ct * 128:(ct + 1) * 128], cons["ident_bf"]), sem=s_tmm)
            p.wait("vector", s_tmm)
            p.op("vector", lambda e, ps=ps, d=dst_fn(rt, ct): e.tensor_copy(d, ps),
                 sem=s_tev)


class Psum:

    def __init__(self, nc, stack):
        self.h = stack.enter_context(nc.psum_tensor("ps", [128, 4096], F32))
        self.f32 = self.h.ap()
        self.t_u8 = self.f32.bitcast(U8)

    def f(self, col0, ncols):
        return self.f32[:, col0:col0 + ncols]


def emit_prenorm_tile(p, ar, x_t, scr, scrb, stats, mr, s_ev, cons=None):
    p.op("vector", lambda e: e.bn_stats(stats[:, 0:6], x_t[:, 0:512]), sem=s_ev)
    p.op("vector", lambda e: e.bn_stats(stats[:, 6:12], x_t[:, 512:1024]), sem=s_ev)
    p.wait("vector", s_ev)
    p.op("vector", lambda e: e.bn_aggr(mr[:, 0:2], stats[:, 0:12]), sem=s_ev)
    p.wait("scalar", s_ev)
    p.op("scalar", lambda e: e.activation(
        stats[:, 12:13], mr[:, 1:2], AF.Sqrt, bias=cons["eps128"], scale=1.0), sem=s_ev)
    p.wait("vector", s_ev)
    p.op("vector", lambda e: e.reciprocal(stats[:, 13:14], stats[:, 12:13]), sem=s_ev)
    p.wait("vector", s_ev)
    p.op("vector", lambda e: e.tensor_scalar(
        scr, x_t, mr[:, 0:1], stats[:, 13:14], ALU.subtract, ALU.mult), sem=s_ev)
    p.wait("scalar", s_ev)
    p.op("scalar", lambda e: e.copy(scrb, scr), sem=s_ev)



def emit_wT_to_hbm(p, ar, psum, cons, w, R, C, hbm, group, sems, psum_byte0=8192,
                   load_eng="gpsimd", store_eng="scalar", half_rt_cap=None,
                   nraw=2):
    if len(sems) == 5:
        s_w, s_tmm, s_tev, s_wst, s_wst2 = sems
        wsts = [s_wst, s_wst2]
    else:
        s_w, s_tmm, s_tev, s_wst = sems
        wsts = [s_wst, s_wst]
    RT, CT = R // 128, C // 128
    half_rt = max(1, min(RT, 16384 // C))
    if half_rt_cap is not None:
        half_rt = min(half_rt, half_rt_cap)
    nhalf = (RT + half_rt - 1) // half_rt
    nb = min(nraw, nhalf)
    ar.mark()
    raw = [ar.alloc([128, half_rt, C], BF16) for _ in range(nb)]
    chunk = [ar.alloc([128, CT * 128 if group == "r" else half_rt * 128], BF16)
             for _ in range(2)]
    nch = 0
    tmm_after_half = {}
    w_t = w.rearrange("(t p) c -> p t c", p=128)
    halves = list(range(0, RT, half_rt))
    s_w_base = p.val(s_w)
    for hidx in range(nb):
        h0 = halves[hidx]
        h1 = min(h0 + half_rt, RT)
        p.op(load_eng, lambda e, h0=h0, h1=h1, r=raw[hidx % nb]: e.dma_start(
            out=r[:, 0:h1 - h0, :], in_=w_t[:, h0:h1, :]), sem=s_w, k=16)
    for hidx, h0 in enumerate(halves):
        h1 = min(h0 + half_rt, RT)
        rbuf = raw[hidx % nb]
        p.wait("tensor", s_w, s_w_base + 16 * (hidx + 1))
        outer = range(h0, h1) if group == "r" else range(CT)
        inner = range(CT) if group == "r" else range(h0, h1)
        for g in outer:
            cb = chunk[nch % 2]
            sw = wsts[nch % 2]
            if nch >= 2:
                p.wait("vector", sw, p.val(sw))
            nch += 1
            for i_, it in enumerate(inner):
                rt, ct = (g, it) if group == "r" else (it, g)
                slot = p.val(s_tmm) % 4
                ps = psum.t_u8[:, psum_byte0 + slot * 2048: psum_byte0 + slot * 2048 + 256].bitcast(BF16)
                if p.val(s_tmm) >= 4:
                    p.wait("tensor", s_tev, p.val(s_tev) - 3)
                p.op("tensor", lambda e, ps=ps, rt=rt, ct=ct, h0=h0, rbuf=rbuf: e.transpose(
                    ps, rbuf[:, rt - h0, ct * 128:(ct + 1) * 128], cons["ident_bf"]),
                    sem=s_tmm)
                p.wait("vector", s_tmm)
                p.op("vector", lambda e, ps=ps, cb=cb, i_=i_: e.tensor_copy(
                    cb[:, i_ * 128:(i_ + 1) * 128], ps), sem=s_tev)
            p.wait(store_eng, s_tev)
            if group == "r":
                dst = hbm[g]
            else:
                dst = hbm[g][:, h0 * 128:h1 * 128]
            p.wait(store_eng, sw, p.val(sw))
            p.op(store_eng, lambda e, cb=cb, dst=dst, n=(len(list(inner))) * 128:
                 e.dma_start(out=dst, in_=cb[:, 0:n]), sem=sw, k=16)
        tmm_after_half[hidx] = p.val(s_tmm)
        if hidx + nb < nhalf:
            nh0 = halves[hidx + nb]
            nh1 = min(nh0 + half_rt, RT)
            p.wait(load_eng, s_tmm, tmm_after_half[hidx])
            p.op(load_eng, lambda e, nh0=nh0, nh1=nh1, rbuf=rbuf: e.dma_start(
                out=rbuf[:, 0:nh1 - nh0, :], in_=w_t[:, nh0:nh1, :]), sem=s_w, k=16)
    for sw in set(wsts):
        p.wait(load_eng, sw)
        p.wait("vector", sw)
        p.wait(store_eng, sw)
    ar.release()


def emit_l1f(p, ar, psum, cons, xin_sb, wp, pref="f1", tbs=4, stage=9):
    nc = p.nc
    s_tmm = p.sem(pref + "tmm")
    s_tev = p.sem(pref + "tev")
    s_mm = p.sem(pref + "mm")
    s_ev = p.sem(pref + "ev")
    s_y = p.sem(pref + "y")
    s_lif = p.sem(pref + "lif")
    s_ws = [p.sem(pref + "ws0"), p.sem(pref + "ws1")]

    wfcT_hbm = wp["wfcT"]
    wprT_hbm = wp["wprT"]
    wfcb = [ar.alloc([128, 8, 128], BF16) for _ in range(2)]
    wprb = [ar.alloc([128, 1024], BF16) for _ in range(2)]
    p.wait("sync", wp["s_wst"])
    p.wait("sync", wp["s_wst2"])
    nstep = 5
    for tb in range(tbs):
        ar.mark()
        xln_fm = ar.alloc([128, 8, 512], BF16)
        h2 = ar.alloc([128, 32, 512], BF16)
        scr = ar.alloc([128, 1024], F32)
        scrb = ar.alloc([128, 1024], BF16)
        stats = ar.alloc([128, 16], F32)
        mr = ar.alloc([128, 2], F32)
        ysb = ar.alloc([128, 1024], F32)
        memt = ar.alloc([128, 1024], F32)
        acct = ar.alloc([128, 1024], F32)
        spkt = ar.alloc([128, 1024], F32)

        for ti in range(4):
            t = tb * 4 + ti
            if ti > 0 or tb > 0:
                p.wait("scalar", s_tmm)
            emit_prenorm_tile(p, ar, xin_sb[:, t, :], scr, scrb, stats, mr, s_ev, cons)
            p.wait("tensor", s_ev)
            if tb > 0 and ti == 0:
                p.wait("tensor", s_y, p.val(s_y))
            for dt in range(8):
                slot = dt % 2
                ps = psum.t_u8[:, 12288 + slot * 2048: 12288 + slot * 2048 + 256].bitcast(BF16)
                if ti * 8 + dt >= 2:
                    p.wait("tensor", s_tev, p.val(s_tev) - 1)
                p.op("tensor", lambda e, ps=ps, dt=dt: e.transpose(
                    ps, scrb[:, dt * 128:(dt + 1) * 128], cons["ident_bf"]), sem=s_tmm)
                p.wait("vector", s_tmm)
                p.op("vector", lambda e, ps=ps, dt=dt, ti=ti: e.tensor_copy(
                    xln_fm[:, dt, ti * 128:(ti + 1) * 128], ps), sem=s_tev)

        p.wait("tensor", s_tev)
        p.wait("sync", s_mm)
        if stage < 2:
            ar.release()
            continue
        for ft in range(32):
            wb = wfcb[ft % 2]
            if ft >= 2:
                p.wait("sync", s_mm, p.val(s_mm) - 1)
            p.op("sync", lambda e, wb=wb, ft=ft: e.dma_start(
                out=wb, in_=wfcT_hbm[ft].rearrange("p (t c) -> p t c", t=8)),
                sem=s_ws[ft % 2], k=16)
            p.wait("tensor", s_ws[ft % 2])
            ps = psum.f((ft % 2) * 512, 512)
            if ft >= 2:
                p.wait("tensor", s_ev, p.val(s_ev) - 3)
            for dt in range(8):
                p.op("tensor", lambda e, ps=ps, wb=wb, dt=dt: e.matmul(
                    ps, wb[:, dt, :], xln_fm[:, dt, :],
                    start=(dt == 0), stop=(dt == 7)),
                    sem=(s_mm if dt == 7 else None))
            p.wait("vector", s_mm)
            p.op("vector", lambda e, ps=ps, ft=ft: e.tensor_scalar(
                h2[:, ft, :], ps, 0.0, None, ALU.max, ALU.bypass), sem=s_ev)
            p.wait("vector", s_ev)
            p.op("vector", lambda e, ft=ft: e.tensor_tensor(
                h2[:, ft, :], h2[:, ft, :], h2[:, ft, :], ALU.mult), sem=s_ev)

        if stage < 3:
            ar.release()
            continue
        p.wait("tensor", s_ev)
        p.wait("sync", s_mm)
        for ft in range(32):
            wb = wprb[ft % 2]
            if ft >= 2:
                p.wait("sync", s_mm, p.val(s_mm) - 8)
            p.op("sync", lambda e, wb=wb, ft=ft: e.dma_start(
                out=wb, in_=wprT_hbm[ft]), sem=s_ws[ft % 2], k=16)
            p.wait("tensor", s_ws[ft % 2])
            for ti in range(4):
                for nchk in range(2):
                    ps = psum.f((ti * 2 + nchk) * 512, 512)
                    p.op("tensor", lambda e, ps=ps, wb=wb, ft=ft, ti=ti, nchk=nchk: e.matmul(
                        ps, h2[:, ft, ti * 128:(ti + 1) * 128],
                        wb[:, nchk * 512:(nchk + 1) * 512],
                        start=(ft == 0), stop=(ft == 31)),
                        sem=s_mm)
        if stage < 4:
            ar.release()
            continue
        for ti in range(4):
            t = tb * 4 + ti
            p.wait("scalar", s_mm)
            p.wait("scalar", s_lif)
            for nchk in range(2):
                ps = psum.f((ti * 2 + nchk) * 512, 512)
                p.op("scalar", lambda e, ps=ps, nchk=nchk: e.copy(
                    ysb[:, nchk * 512:(nchk + 1) * 512], ps), sem=s_y)
            p.wait("vector", s_y)
            p.op("vector", lambda e: e.tensor_scalar(
                acct, ysb, 1.0, None, ALU.is_gt, ALU.bypass), sem=s_lif)
            p.wait("vector", s_lif)
            p.op("vector", lambda e: e.tensor_copy(spkt, acct), sem=s_lif)
            p.op("vector", lambda e: e.tensor_copy(memt, ysb), sem=s_lif)
            for st in range(1, nstep):
                p.wait("vector", s_lif)
                p.op("vector", lambda e: e.scalar_tensor_tensor(
                    memt, memt, 0.9, ysb, ALU.mult, ALU.add), sem=s_lif)
                p.wait("vector", s_lif)
                p.op("vector", lambda e: e.tensor_tensor(
                    memt, memt, spkt, ALU.subtract), sem=s_lif)
                p.wait("vector", s_lif)
                p.op("vector", lambda e: e.tensor_scalar(
                    spkt, memt, 1.0, None, ALU.is_gt, ALU.bypass), sem=s_lif)
                p.wait("vector", s_lif)
                p.op("vector", lambda e: e.tensor_tensor(
                    acct, acct, spkt, ALU.add), sem=s_lif)
            p.wait("vector", s_lif)
            p.op("vector", lambda e, t=t: e.scalar_tensor_tensor(
                xin_sb[:, t, :], acct, 0.2, xin_sb[:, t, :], ALU.mult, ALU.add),
                sem=s_lif)
        ar.release()
    return s_lif



def emit_l0a(p, ar, psum, cons, x_sb, wq, wk, wv, pref="a0", nchunks=4, stage=9,
             prelude=None):
    nc = p.nc
    s_w = p.sem(pref + "w")
    s_tmm = p.sem(pref + "tmm")
    s_tev = p.sem(pref + "tev")
    s_pn = p.sem(pref + "pn")
    s_qkv = p.sem(pref + "qkv")
    s_qev = p.sem(pref + "qev")
    s_sc = p.sem(pref + "sc")
    s_ex = p.sem(pref + "ex")
    s_av = p.sem(pref + "av")
    s_oev = p.sem(pref + "oev")
    s_oc = p.sem(pref + "oc")
    s_ot = p.sem(pref + "ot")
    s_ov = p.sem(pref + "ov")
    s_nrm = p.sem(pref + "nrm")
    s_yst = p.sem(pref + "yst")
    s_cc = p.sem(pref + "cc")
    s_fin = p.sem(pref + "fin")

    y_hbm = nc.dram_tensor(pref + "_y", [TPC, D], F32).ap()
    cc_in = nc.dram_tensor(pref + "_ccin", [1, 16], F32)
    cc_out = nc.dram_tensor(pref + "_ccout", [1, 16], F32)
    groups = [[0, 1, 2, 3], [4, 5, 6, 7]]

    wT = {}
    for name in ("q", "k", "v"):
        wT[name] = ar.alloc([128, 8, 1024], BF16)
    ar.mark()
    raw = ar.alloc([128, 8, 1024], BF16)
    for name, w in (("q", wq), ("k", wk), ("v", wv)):
        if name != "q":
            p.wait("gpsimd", s_tmm)
        p.op("gpsimd", lambda e, w=w: e.dma_start(
            out=raw, in_=w.rearrange("(t p) c -> p t c", p=128)), sem=s_w, k=16)
        p.wait("tensor", s_w)
        emit_transposes_bf(p, psum, cons, raw,
                           lambda rt, ct, name=name: wT[name][:, ct, rt * 128:(rt + 1) * 128],
                           8, 8, s_tmm, s_tev)
    p.wait("gpsimd", s_tmm)
    ar.release()
    if prelude is not None:
        p.wait("tensor", s_tev)
        gates = prelude()
        p.barrier("wp", gates)

    bnstash = ar.alloc([128, 192], F32)
    mv2 = ar.alloc([128, 2], F32)
    msq = ar.alloc([128, 1], F32)
    red_sb = ar.alloc([1, 16], F32)
    bcast_sb = ar.alloc([128, 2], F32)
    onescol_bf = ar.alloc([128, 1], BF16)
    p.op("vector", lambda e: e.memset(onescol_bf, 1.0), sem=s_pn)
    p.op("vector", lambda e: e.memset(red_sb, 0.0), sem=s_pn)

    kbs = [list(range(i, min(i + 3, 64))) for i in range(0, 64, 3)]
    ptb_last_read = {0: 0, 1: 0}
    exp_read = {0: 0, 1: 0}

    for c in range(nchunks):
        ar.mark()
        xln_fm = ar.alloc([128, 8, 512], BF16)
        qc = ar.alloc([128, 8192], BF16)
        kc = ar.alloc([128, 8192], BF16)
        vbuf = ar.alloc([128, 4, 16, 96], BF16)
        ptb = [ar.alloc([128, 1536], BF16) for _ in range(2)]
        stag = ar.alloc([128, 512], F32)
        ych = ar.alloc([128, 4, 1024], F32)
        scr = ar.alloc([128, 1024], F32)
        scrb = ar.alloc([128, 1024], BF16)
        stats = ar.alloc([128, 16], F32)
        mr = ar.alloc([128, 2], F32)
        mrstash = ar.alloc([128, 8], F32)
        sq = ar.alloc([64, 512], BF16)
        umax = ar.alloc([1, 8], F32)
        negu = ar.alloc([128, 1], F32)
        rc = ar.alloc([128, 1], F32)

        p.wait("tensor", s_ex)
        p.wait("tensor", s_oc)
        p.wait("vector", s_qkv)
        p.wait("vector", s_sc)
        p.wait("vector", s_av)
        p.wait("vector", s_ot)
        p.wait("vector", s_ov)
        p.wait("scalar", s_av)
        p.op("vector", lambda e: e.memset(stag, 0.0), sem=s_pn)
        p.op("vector", lambda e: e.memset(vbuf, 1.0), sem=s_pn)
        p.op("vector", lambda e: e.memset(qc[64:128, :], 0.0), sem=s_pn)
        p.op("vector", lambda e: e.memset(kc[64:128, :], 0.0), sem=s_pn)

        for ti in range(4):
            t = c * 4 + ti
            if t > 0:
                p.wait("scalar", s_tmm)
            emit_prenorm_tile(p, ar, x_sb[:, t, :], scr, scrb, stats, mr, s_pn, cons)
            p.wait("vector", s_pn)
            p.op("vector", lambda e, ti=ti: e.tensor_copy(
                mrstash[:, 2 * ti:2 * ti + 1], mr[:, 0:1]), sem=s_pn)
            p.op("vector", lambda e, ti=ti: e.tensor_copy(
                mrstash[:, 2 * ti + 1:2 * ti + 2], stats[:, 13:14]), sem=s_pn)
            p.wait("tensor", s_pn)
            for dt in range(8):
                slot = dt % 2
                ps = psum.t_u8[:, 12288 + slot * 2048: 12288 + slot * 2048 + 256].bitcast(BF16)
                if p.val(s_tmm) >= 2:
                    p.wait("tensor", s_tev, p.val(s_tev) - 1)
                p.op("tensor", lambda e, ps=ps, dt=dt: e.transpose(
                    ps, scrb[:, dt * 128:(dt + 1) * 128], cons["ident_bf"]), sem=s_tmm)
                p.wait("vector", s_tmm)
                p.op("vector", lambda e, ps=ps, dt=dt, ti=ti: e.tensor_copy(
                    xln_fm[:, dt, ti * 128:(ti + 1) * 128], ps), sem=s_tev)

        p.wait("tensor", s_tev)
        nstrip = 0
        for name, dstbuf in (("q", qc), ("k", kc)):
            for g in range(16):
                ps = psum.f((nstrip % 2) * 512, 512)[0:64, :]
                if nstrip >= 2:
                    p.wait("tensor", s_qev, p.val(s_qev) - 1)
                for dt in range(8):
                    p.op("tensor", lambda e, ps=ps, name=name, g=g, dt=dt: e.matmul(
                        ps, wT[name][:, dt, g * 64:(g + 1) * 64], xln_fm[:, dt, :],
                        start=(dt == 0), stop=(dt == 7)),
                        sem=(s_qkv if dt == 7 else None))
                p.wait("vector", s_qkv)
                p.op("vector", lambda e, ps=ps, dstbuf=dstbuf, g=g: e.tensor_copy(
                    dstbuf[0:64, g * 512:(g + 1) * 512], ps), sem=s_qev)
                nstrip += 1
        for tb in range(4):
            for dvc in range(2):
                ps = psum.f((nstrip % 2) * 512, 512)
                if nstrip >= 2:
                    p.wait("tensor", s_qev, p.val(s_qev) - 1)
                for dt in range(8):
                    p.op("tensor", lambda e, ps=ps, tb=tb, dvc=dvc, dt=dt: e.matmul(
                        ps, xln_fm[:, dt, tb * 128:(tb + 1) * 128],
                        wT["v"][:, dt, dvc * 512:(dvc + 1) * 512],
                        start=(dt == 0), stop=(dt == 7)),
                        sem=(s_qkv if dt == 7 else None))
                p.wait("vector", s_qkv)
                dst = vbuf[:, tb, dvc * 8:(dvc + 1) * 8, 0:64]
                p.op("vector", lambda e, ps=ps, dst=dst: e.tensor_copy(dst, ps), sem=s_qev)
                nstrip += 1

        if stage < 2:
            ar.release()
            continue

        p.op("vector", lambda e: e.memset(umax, 0.0), sem=s_nrm)
        p.wait("vector", s_qev)
        p.op("vector", lambda e: e.memset(kc[64:65, :], 1.0), sem=s_nrm)
        p.wait("scalar", s_qev)
        for blk in range(16):
            p.wait("scalar", s_nrm)
            p.op("scalar", lambda e, blk=blk: e.activation(
                sq, kc[0:64, blk * 512:(blk + 1) * 512], AF.Square,
                bias=0.0, scale=1.0), sem=s_nrm)
            ps = psum.f(1024, 512)[0:1, :]
            p.wait("tensor", s_nrm)
            p.op("tensor", lambda e, ps=ps: e.matmul(
                ps, onescol_bf[0:64, :], sq[0:64, :], start=True, stop=True),
                sem=s_nrm)
            p.wait("vector", s_nrm)
            p.op("vector", lambda e, ps=ps: e.tensor_reduce(
                umax[0:1, 2:3], ps, mybir.AxisListType.X, ALU.max), sem=s_nrm)
            p.wait("vector", s_nrm)
            p.op("vector", lambda e: e.tensor_tensor(
                umax[0:1, 0:1], umax[0:1, 0:1], umax[0:1, 2:3], ALU.max), sem=s_nrm)
        p.wait("scalar", s_nrm)
        p.op("scalar", lambda e: e.activation(
            umax[0:1, 4:5], umax[0:1, 0:1], AF.Sqrt, bias=0.0, scale=1.0), sem=s_nrm)
        ps_nb = psum.f(1024, 128)[:, 0:1]
        p.wait("tensor", s_nrm)
        p.op("tensor", lambda e, ps=ps_nb: e.matmul(
            ps, cons["ones_f32"], umax[0:1, 4:5], start=True, stop=True), sem=s_nrm)
        p.wait("vector", s_nrm)
        p.op("vector", lambda e, ps=ps_nb: e.tensor_copy(negu, ps), sem=s_nrm)
        for g in range(16):
            p.wait("scalar", s_nrm)
            p.op("scalar", lambda e, g=g: e.activation(
                sq, qc[0:64, g * 512:(g + 1) * 512], AF.Square,
                bias=0.0, scale=1.0), sem=s_nrm)
            psq = psum.f(1536, 512)[64:65, :]
            p.wait("tensor", s_nrm)
            p.op("tensor", lambda e, psq=psq: e.matmul(
                psq, onescol_bf[0:64, :], sq[0:64, :], start=True, stop=True,
                tile_position=(0, 64)), sem=s_nrm)
            p.wait("scalar", s_nrm)
            p.op("scalar", lambda e, psq=psq: e.activation(
                scr[64:65, 0:512], psq, AF.Sqrt, bias=0.0, scale=1.0), sem=s_nrm)
            p.wait("vector", s_nrm)
            p.op("vector", lambda e, g=g: e.tensor_scalar(
                qc[64:65, g * 512:(g + 1) * 512], scr[64:65, 0:512],
                negu[64:65, 0:1], -1.0, ALU.mult, ALU.subtract), sem=s_nrm)
        p.wait("vector", s_nrm)
        p.op("vector", lambda e: e.tensor_scalar(
            qc[64:65, :], qc[64:65, :], -1.0, None, ALU.mult, ALU.bypass), sem=s_nrm)

        if stage < 3:
            ar.release()
            continue

        p.wait("tensor", s_nrm)
        def mk_drain(qg, otps):
            def drain():
                p.wait("vector", s_av)
                p.wait("vector", s_ot)
                p.op("vector", lambda e, otps=otps: e.tensor_copy(
                    stag[0:65, :], otps[0:65, :]), sem=s_oc)
                p.wait("tensor", s_oc)
                p.wait("tensor", s_ov)
                for tbq in range(4):
                    px = psum.f(3584, 512)[:, tbq * 128:(tbq + 1) * 128]
                    p.op("tensor", lambda e, px=px, tbq=tbq: e.transpose(
                        px, stag[:, tbq * 128:(tbq + 1) * 128], cons["ident_f32"]),
                        sem=s_ot)
                p.wait("vector", s_ot)
                for tbq in range(4):
                    px = psum.f(3584, 512)[:, tbq * 128:(tbq + 1) * 128]
                    p.op("vector", lambda e, px=px: e.tensor_scalar(
                        rc, px[:, 64:65], 1e-35, None, ALU.add, ALU.bypass),
                        sem=s_oev)
                    p.wait("vector", s_oev)
                    p.op("vector", lambda e: e.reciprocal(rc, rc), sem=s_oev)
                    p.wait("vector", s_oev)
                    p.op("vector", lambda e, px=px, tbq=tbq, qg=qg: e.tensor_scalar(
                        ych[:, tbq, qg * 64:(qg + 1) * 64], px[:, 0:64], rc, None,
                        ALU.mult, ALU.bypass), sem=s_ov)
            return drain

        pending_drain = None
        for qg in range(16):
            otps = psum.f(0, 512)
            nb = len(kbs)
            sc_done = {}
            for it in range(nb + 1):
                if it < nb:
                    bi, batch = it, kbs[it]
                    stps = psum.f(512 + (bi % 2) * 1536, 1536)
                    p.wait("tensor", s_ex, exp_read[bi % 2])
                    for j, kb in enumerate(batch):
                        p.op("tensor", lambda e, stps=stps, j=j, kb=kb, qg=qg: e.matmul(
                            stps[:, j * 512:(j + 1) * 512],
                            kc[:, kb * 128:(kb + 1) * 128],
                            qc[:, qg * 512:(qg + 1) * 512],
                            start=True, stop=True),
                            sem=(s_sc if j == len(batch) - 1 else None))
                    sc_done[bi] = p.val(s_sc)
                if it == 1 and pending_drain is not None:
                    pending_drain()
                    pending_drain = None
                if it >= 1:
                    bi, batch = it - 1, kbs[it - 1]
                    stps = psum.f(512 + (bi % 2) * 1536, 1536)
                    ptbuf = ptb[bi % 2]
                    p.wait("scalar", s_sc, sc_done[bi])
                    p.wait("scalar", s_nrm)
                    p.wait("scalar", s_av, ptb_last_read[bi % 2])
                    n = len(batch) * 512
                    p.op("scalar", lambda e, stps=stps, ptbuf=ptbuf, n=n: e.activation(
                        ptbuf[:, 0:n], stps[:, 0:n], AF.Exp, bias=0.0, scale=1.0),
                        sem=s_ex)
                    exp_read[bi % 2] = p.val(s_ex)
                    if bi == 0:
                        p.wait("tensor", s_oc)
                    p.wait("tensor", s_ex)
                    for j, kb in enumerate(batch):
                        gk, tbk = kb // 4, kb % 4
                        vsl = vbuf[:, tbk, gk, :]
                        first = (bi == 0 and j == 0)
                        last = (bi == nb - 1 and j == len(batch) - 1)
                        p.op("tensor", lambda e, vsl=vsl, otps=otps, ptbuf=ptbuf, j=j,
                             first=first, last=last: e.matmul(
                                 otps[0:96, :], vsl, ptbuf[:, j * 512:(j + 1) * 512],
                                 start=first, stop=last), sem=s_av)
                    ptb_last_read[bi % 2] = p.val(s_av)
            pending_drain = mk_drain(qg, otps)
        pending_drain()

        for ti in range(4):
            t = c * 4 + ti
            p.wait("vector", s_ov)
            p.op("vector", lambda e, t=t, ti=ti: e.tensor_scalar(
                scr, x_sb[:, t, :], mrstash[:, 2 * ti:2 * ti + 1],
                mrstash[:, 2 * ti + 1:2 * ti + 2], ALU.subtract, ALU.mult), sem=s_fin)
            p.wait("vector", s_fin)
            p.op("vector", lambda e, ti=ti: e.tensor_tensor(
                ych[:, ti, :], ych[:, ti, :], scr, ALU.add), sem=s_fin)
            p.wait("vector", s_fin)
            for hh in range(2):
                p.op("vector", lambda e, t=t, ti=ti, hh=hh: e.bn_stats(
                    bnstash[:, (t * 2 + hh) * 6:(t * 2 + hh) * 6 + 6],
                    ych[:, ti, hh * 512:(hh + 1) * 512]), sem=s_fin)
            p.wait("scalar", s_fin)
            p.wait("scalar", s_yst, p.val(s_yst))
            p.op("scalar", lambda e, t=t, ti=ti: e.dma_start(
                out=y_hbm.rearrange("(t p) d -> p t d", p=128)[:, t, :],
                in_=ych[:, ti, :]), sem=s_yst, k=16)
        ar.release()

    if stage < 4:
        return s_fin

    p.wait("vector", s_fin)
    p.op("vector", lambda e: e.bn_aggr(mv2, bnstash), sem=s_fin)
    p.wait("vector", s_fin)
    p.op("vector", lambda e: e.tensor_scalar(
        msq, mv2[:, 0:1], mv2[:, 0:1], None, ALU.mult, ALU.bypass), sem=s_fin)
    p.wait("vector", s_fin)
    p.op("vector", lambda e: e.tensor_tensor(
        mv2[:, 1:2], mv2[:, 1:2], msq, ALU.add), sem=s_fin)
    psr = psum.f(1024, 512)[0:1, 0:2]
    p.wait("tensor", s_fin)
    p.op("tensor", lambda e, psr=psr: e.matmul(
        psr, cons["onesc_f32"], mv2, start=True, stop=True), sem=s_fin)
    p.wait("vector", s_fin)
    p.op("vector", lambda e, psr=psr: e.tensor_copy(red_sb[0:1, 0:2], psr), sem=s_fin)
    p.wait("gpsimd", s_fin)
    p.op("gpsimd", lambda e: e.dma_start(out=cc_in[0:1, :], in_=red_sb), sem=s_cc, k=16)
    p.wait("gpsimd", s_cc)
    p.op("gpsimd", lambda e: e.collective_compute(
        "AllReduce", ALU.add, replica_groups=groups,
        ins=[cc_in.ap().opt()], outs=[cc_out.ap().opt()]), sem=s_cc, k=1)
    p.wait("gpsimd", s_cc)
    p.op("gpsimd", lambda e: e.dma_start(out=red_sb, in_=cc_out[0:1, :]), sem=s_cc, k=16)
    for eng in ("vector", "scalar", "tensor"):
        p.wait(eng, s_cc)
    p.op("vector", lambda e: e.tensor_scalar(
        red_sb[0:1, 2:4], red_sb[0:1, 0:2], 1.0 / (4 * 128), None, ALU.mult,
        ALU.bypass), sem=s_fin)
    p.wait("vector", s_fin)
    p.op("vector", lambda e: e.tensor_tensor(
        red_sb[0:1, 4:5], red_sb[0:1, 2:3], red_sb[0:1, 2:3], ALU.mult), sem=s_fin)
    p.wait("vector", s_fin)
    p.op("vector", lambda e: e.tensor_tensor(
        red_sb[0:1, 5:6], red_sb[0:1, 3:4], red_sb[0:1, 4:5], ALU.subtract), sem=s_fin)
    p.wait("scalar", s_fin)
    p.op("scalar", lambda e: e.activation(
        red_sb[0:1, 6:7], red_sb[0:1, 5:6], AF.Sqrt, bias=cons["eps128"][0:1, :],
        scale=1.0), sem=s_fin)
    p.wait("vector", s_fin)
    p.op("vector", lambda e: e.reciprocal(red_sb[0:1, 7:8], red_sb[0:1, 6:7]), sem=s_fin)
    p.wait("vector", s_fin)
    p.op("vector", lambda e: e.tensor_copy(red_sb[0:1, 8:9], red_sb[0:1, 7:8]), sem=s_fin)
    p.wait("vector", s_fin)
    p.op("vector", lambda e: e.tensor_tensor(
        red_sb[0:1, 9:10], red_sb[0:1, 2:3], red_sb[0:1, 7:8], ALU.mult), sem=s_fin)
    p.wait("vector", s_fin)
    p.op("vector", lambda e: e.tensor_scalar(
        red_sb[0:1, 9:10], red_sb[0:1, 9:10], -1.0, None, ALU.mult, ALU.bypass), sem=s_fin)
    psb = psum.f(1024, 512)[:, 0:2]
    p.wait("tensor", s_fin)
    p.op("tensor", lambda e, psb=psb: e.matmul(
        psb, cons["ones_f32"], red_sb[0:1, 8:10], start=True, stop=True), sem=s_fin)
    p.wait("vector", s_fin)
    p.op("vector", lambda e, psb=psb: e.tensor_copy(bcast_sb, psb), sem=s_fin)
    ar.mark()
    yb = [ar.alloc([128, 1024], F32) for _ in range(2)]
    s_yl = [p.sem(pref + "yl0"), p.sem(pref + "yl1")]
    p.wait("sync", s_yst)
    p.wait("sync", s_cc)
    for t in range(NT):
        b = yb[t % 2]
        if t >= 2:
            p.wait("sync", s_fin, p.val(s_fin) - 2)
        p.op("sync", lambda e, b=b, t=t: e.dma_start(
            out=b, in_=y_hbm.rearrange("(t p) d -> p t d", p=128)[:, t, :]),
            sem=s_yl[t % 2], k=16)
        p.wait("vector", s_yl[t % 2])
        p.op("vector", lambda e, b=b, t=t: e.scalar_tensor_tensor(
            b, b, bcast_sb[:, 0:1], x_sb[:, t, :], ALU.mult, ALU.add), sem=s_fin)
        p.wait("vector", s_fin)
        p.op("vector", lambda e, b=b, t=t: e.tensor_scalar(
            x_sb[:, t, :], b, bcast_sb[:, 1:2], None, ALU.add, ALU.bypass), sem=s_fin)
    ar.release()
    return s_fin



L1_SCALE = 1.0 / (96.0 ** 0.5)


def emit_l1a(p, ar, psum, cons, x_sb, w_kvd, w_q, w_qr, w_kr, w_kup_loc, w_vup_loc,
             w_o, cos_own, sin_own, cos_all, sin_all, masks, qidx, attnidx, pref="b1",
             nheads=4, ngb=16, stage=9):
    nc = p.nc
    s_w = p.sem(pref + "w")
    s_tmm = p.sem(pref + "tmm")
    s_tev = p.sem(pref + "tev")
    s_pn = p.sem(pref + "pn")
    s_qkv = p.sem(pref + "qkv")
    s_qev = p.sem(pref + "qev")
    s_cc = p.sem(pref + "cc")
    s_kst = p.sem(pref + "kst")
    s_kh = p.sem(pref + "kh")
    s_sc = p.sem(pref + "sc")
    s_ex = p.sem(pref + "ex")
    s_av = p.sem(pref + "av")
    s_oev = p.sem(pref + "oev")
    s_nrm = p.sem(pref + "nrm")
    s_fin = p.sem(pref + "fin")
    s_dl = [p.sem(pref + "dl0"), p.sem(pref + "dl1")]

    agq_in = nc.dram_tensor(pref + "_aqin", [1024, 2048], BF16)
    agq_out = nc.dram_tensor(pref + "_aqout", [4096, 2048], BF16)
    ag1_in = nc.dram_tensor(pref + "_ag1in", [160, 2048], BF16)
    ag1_out = nc.dram_tensor(pref + "_ag1out", [640, 2048], BF16)
    ag2_in = [nc.dram_tensor(pref + "_ag2i%d" % i, [128, 8192], BF16)
              for i in range(2)]
    ag2_out = [nc.dram_tensor(pref + "_ag2o%d" % i, [512, 8192], BF16)
               for i in range(2)]
    groups = [[0, 1, 2, 3], [4, 5, 6, 7]]

    ar.mark()
    xln_fm = ar.alloc([128, 8, 2048], BF16)
    scr = ar.alloc([128, 1024], F32)
    scrb = ar.alloc([128, 1024], BF16)
    stats = ar.alloc([128, 16], F32)
    mr = ar.alloc([128, 2], F32)
    for t in range(NT):
        if t > 0:
            p.wait("scalar", s_tmm)
        emit_prenorm_tile(p, ar, x_sb[:, t, :], scr, scrb, stats, mr, s_pn, cons)
        p.wait("tensor", s_pn)
        for dt in range(8):
            slot = p.val(s_tmm) % 2
            ps = psum.t_u8[:, 12288 + slot * 2048: 12288 + slot * 2048 + 256].bitcast(BF16)
            if p.val(s_tmm) >= 2:
                p.wait("tensor", s_tev, p.val(s_tev) - 1)
            p.op("tensor", lambda e, ps=ps, dt=dt: e.transpose(
                ps, scrb[:, dt * 128:(dt + 1) * 128], cons["ident_bf"]), sem=s_tmm)
            p.wait("vector", s_tmm)
            p.op("vector", lambda e, ps=ps, dt=dt, t=t: e.tensor_copy(
                xln_fm[:, dt, t * 128:(t + 1) * 128], ps), sem=s_tev)

    wq_sb = ar.alloc([128, 8, 1024], BF16)
    wkvd_sb = ar.alloc([128, 8, 128], BF16)
    wqr_sb = ar.alloc([128, 8, 32], BF16)
    p.op("gpsimd", lambda e: e.dma_start(
        out=wq_sb, in_=w_q.rearrange("(t p) c -> p t c", p=128)), sem=s_w, k=16)
    p.op("gpsimd", lambda e: e.dma_start(
        out=wkvd_sb, in_=w_kvd.rearrange("(t p) c -> p t c", p=128)), sem=s_w, k=16)
    p.op("gpsimd", lambda e: e.dma_start(
        out=wqr_sb, in_=w_qr.rearrange("(t p) c -> p t c", p=128)), sem=s_w, k=16)
    costile = ar.alloc([128, 16, 32], F32)
    p.op("sync", lambda e: e.dma_start(
        out=costile[:, :, 0:16], in_=cos_own.rearrange("(t p) f -> p t f", p=128)),
        sem=s_kh, k=16)
    p.op("sync", lambda e: e.dma_start(
        out=costile[:, :, 16:32], in_=sin_own.rearrange("(t p) f -> p t f", p=128)),
        sem=s_kh, k=16)
    p.wait("tensor", s_w)
    p.wait("tensor", s_kh)
    p.wait("vector", s_kh)
    p.wait("tensor", s_tev)

    qloc = ar.alloc([128, 8, 2048], BF16)
    ckv = ar.alloc([128, 2048], BF16)
    qrt = ar.alloc([128, 16, 32], F32)
    nmm = 0
    for mt in range(8):
        for nt4 in range(4):
            ps = psum.f((nmm % 2) * 512, 512)
            if nmm >= 2:
                p.wait("tensor", s_qev, p.val(s_qev) - 1)
            for dt in range(8):
                p.op("tensor", lambda e, ps=ps, mt=mt, nt4=nt4, dt=dt: e.matmul(
                    ps, wq_sb[:, dt, mt * 128:(mt + 1) * 128],
                    xln_fm[:, dt, nt4 * 512:(nt4 + 1) * 512],
                    start=(dt == 0), stop=(dt == 7)),
                    sem=(s_qkv if dt == 7 else None))
            p.wait("vector", s_qkv)
            p.op("vector", lambda e, ps=ps, mt=mt, nt4=nt4: e.tensor_copy(
                qloc[:, mt, nt4 * 512:(nt4 + 1) * 512], ps), sem=s_qev)
            nmm += 1
    for nt4 in range(4):
        ps = psum.f((nmm % 2) * 512, 512)
        if nmm >= 2:
            p.wait("tensor", s_qev, p.val(s_qev) - 1)
        for dt in range(8):
            p.op("tensor", lambda e, ps=ps, nt4=nt4, dt=dt: e.matmul(
                ps, wkvd_sb[:, dt, :], xln_fm[:, dt, nt4 * 512:(nt4 + 1) * 512],
                start=(dt == 0), stop=(dt == 7)),
                sem=(s_qkv if dt == 7 else None))
        p.wait("vector", s_qkv)
        p.op("vector", lambda e, ps=ps, nt4=nt4: e.tensor_copy(
            ckv[:, nt4 * 512:(nt4 + 1) * 512], ps), sem=s_qev)
        nmm += 1
    for t in range(NT):
        ps = psum.f(1024 + (t % 2) * 512, 512)[:, 0:32]
        if t >= 2:
            p.wait("tensor", s_qev, p.val(s_qev) - 1)
        for dt in range(8):
            p.op("tensor", lambda e, ps=ps, t=t, dt=dt: e.matmul(
                ps, xln_fm[:, dt, t * 128:(t + 1) * 128], wqr_sb[:, dt, :],
                start=(dt == 0), stop=(dt == 7)),
                sem=(s_qkv if dt == 7 else None))
        p.wait("vector", s_qkv)
        p.op("vector", lambda e, ps=ps, t=t: e.tensor_copy(qrt[:, t, :], ps), sem=s_qev)
    rot0 = ar.alloc([128, 16, 32], F32)
    emit_rotary(p, qrt, costile, NT, s_qev, rot0)
    stg = ar.alloc([128, 128], BF16)
    qrf = ar.alloc([96, 2048], BF16)
    p.op("vector", lambda e: e.memset(stg, 0.0), sem=s_qev)
    for t in range(NT):
        p.wait("vector", s_qev)
        p.wait("vector", s_tmm)
        p.op("vector", lambda e, t=t: e.tensor_copy(
            stg[:, 64:96], qrt[:, t, :]), sem=s_qev)
        p.wait("tensor", s_qev)
        p.wait("tensor", s_tev)
        ps = psum.t_u8[:, 12288:12288 + 256].bitcast(BF16)
        p.op("tensor", lambda e, ps=ps: e.transpose(
            ps, stg, cons["ident_bf"]), sem=s_tmm)
        p.wait("vector", s_tmm)
        p.op("vector", lambda e, ps=ps, t=t: e.tensor_copy(
            qrf[64:96, t * 128:(t + 1) * 128], ps[64:96, :]), sem=s_tev)

    p.wait("scalar", s_qev)
    p.wait("scalar", s_tev)
    for mt in range(8):
        p.op("scalar", lambda e, mt=mt: e.dma_start(
            out=agq_in[mt * 128:(mt + 1) * 128, :], in_=qloc[:, mt, :]), sem=s_cc, k=16)
    p.op("scalar", lambda e: e.dma_start(
        out=ag1_in[0:32, :], in_=qrf[64:96, :]), sem=s_cc, k=16)
    p.op("scalar", lambda e: e.dma_start(
        out=ag1_in[32:160, :], in_=ckv), sem=s_cc, k=16)
    cc_stores = p.val(s_cc)
    p.wait("gpsimd", s_cc)
    p.op("gpsimd", lambda e: e.collective_compute(
        "AllGather", ALU.bypass, replica_groups=groups,
        ins=[ag1_in.ap().opt()], outs=[ag1_out.ap().opt()]), sem=s_cc, k=1)
    cc_ag1 = p.val(s_cc)
    p.op("gpsimd", lambda e: e.collective_compute(
        "AllGather", ALU.bypass, replica_groups=groups,
        ins=[agq_in.ap().opt()], outs=[agq_out.ap().opt()]), sem=s_cc, k=1)
    cc_agq = p.val(s_cc)
    ar.release()

    if stage < 2:
        for eng in ENGS:
            p.wait(eng, s_cc)
        return s_cc

    for eng in ("tensor", "vector", "scalar"):
        p.wait(eng, s_cc, cc_stores)
    ar.mark()
    ckv_g = ar.alloc([128, 4, 2048], BF16)
    kstack = ar.alloc([128, 8192], BF16)
    qstack = ar.alloc([128, 8192], BF16)
    vform = ar.alloc([128, 64, 130], BF16)
    attn = ar.alloc([128, 2, 8192], BF16)
    ptb = [ar.alloc([128, 1536], BF16) for _ in range(2)]
    wkup_sb = ar.alloc([128, 256], BF16)
    wvup_sb = ar.alloc([128, 256], BF16)
    masks_sb = ar.alloc([128, 4, 512], BF16)
    p.wait("sync", s_cc, cc_ag1)
    for r in range(4):
        p.op("sync", lambda e, r=r: e.dma_start(
            out=ckv_g[:, r, :], in_=ag1_out.ap()[r * 160 + 32: r * 160 + 160, :]),
            sem=s_kh, k=16)
    ar.mark()
    cosall = ar.alloc([128, 64, 32], BF16)
    p.wait("gpsimd", s_cc, cc_stores)
    p.op("gpsimd", lambda e: e.dma_start(
        out=cosall[:, :, 0:16], in_=cos_all.rearrange("(t p) f -> p t f", p=128)),
        sem=s_kst, k=16)
    p.op("gpsimd", lambda e: e.dma_start(
        out=cosall[:, :, 16:32], in_=sin_all.rearrange("(t p) f -> p t f", p=128)),
        sem=s_kst, k=16)
    wkr_sb = ar.alloc([128, 32], BF16)
    p.op("gpsimd", lambda e: e.dma_start(out=wkr_sb, in_=w_kr), sem=s_kst, k=16)
    p.op("gpsimd", lambda e: e.dma_start(out=wkup_sb, in_=w_kup_loc), sem=s_kst, k=16)
    p.op("gpsimd", lambda e: e.dma_start(out=wvup_sb, in_=w_vup_loc), sem=s_kst, k=16)
    p.op("sync", lambda e: e.dma_start(out=masks_sb, in_=masks.rearrange("j p q -> p j q")),
         sem=s_kh, k=16)
    p.wait("tensor", s_kst)
    p.wait("vector", s_kst)
    p.wait("tensor", s_kh)
    p.wait("vector", s_kh)

    g2c = []
    for r in range(4):
        for bi, blk in enumerate(BLOCKS[r]):
            g2c.append((blk, r, bi))
    g2c.sort()
    gcol = {blk: (r, bi) for blk, r, bi in g2c}
    xq = {blk: (r * 4 + bi) for blk, r, bi in g2c}

    krt = ar.alloc([128, 64, 32], BF16)
    nrp = 0
    for gb in range(16):
        r, bi = gcol[gb]
        for tt in range(4):
            gt = gb * 4 + tt
            ps = psum.f(1024 + (nrp % 2) * 512, 512)[:, 0:32]
            if nrp >= 2:
                p.wait("tensor", s_qev, p.val(s_qev) - 1)
            p.op("tensor", lambda e, ps=ps, r=r, bi=bi, tt=tt: e.matmul(
                ps, ckv_g[:, r, bi * 512 + tt * 128: bi * 512 + (tt + 1) * 128],
                wkr_sb[:, 0:32], start=True, stop=True),
                sem=s_qkv)
            p.wait("vector", s_qkv)
            p.op("vector", lambda e, ps=ps, gt=gt: e.tensor_copy(krt[:, gt, :], ps),
                 sem=s_qev)
            nrp += 1
    rot0c = ar.alloc([128, 64, 32], BF16)
    emit_rotary(p, krt, cosall, 64, s_qev, rot0c)
    stgc = ar.alloc([128, 128], BF16)
    p.op("vector", lambda e: e.memset(stgc, 0.0), sem=s_qev)
    for gt in range(64):
        p.wait("vector", s_qev)
        p.wait("vector", s_tmm)
        p.op("vector", lambda e, gt=gt: e.tensor_copy(
            stgc[:, 64:96], krt[:, gt, :]), sem=s_qev)
        p.wait("tensor", s_qev)
        p.wait("tensor", s_tev)
        ps = psum.t_u8[:, 12288:12288 + 256].bitcast(BF16)
        p.op("tensor", lambda e, ps=ps: e.transpose(
            ps, stgc, cons["ident_bf"]), sem=s_tmm)
        p.wait("vector", s_tmm)
        p.op("vector", lambda e, ps=ps, gt=gt: e.tensor_copy(
            kstack[64:96, gt * 128:(gt + 1) * 128], ps[64:96, :]), sem=s_tev)
    p.wait("vector", s_tev)
    p.op("vector", lambda e: e.memset(kstack[96:128, :], 0.0), sem=s_qev)
    ar.release()

    p.op("vector", lambda e: e.memset(qstack[96:128, :], 0.0), sem=s_qev)
    onesq = ar.alloc([128, 64], BF16)
    p.op("vector", lambda e: e.memset(onesq, 1.0), sem=s_qev)
    p.op("vector", lambda e: e.memset(attn, 0.0), sem=s_qev)
    onesn = ar.alloc([96, 1], BF16)
    p.op("vector", lambda e: e.memset(onesn, 1.0), sem=s_qev)
    rrow = ar.alloc([1, 512], F32)
    qgstg = ar.alloc([128, 4, 2048], BF16)
    qidx_sb = ar.alloc([128, 4, 4], U32)
    p.wait("sync", s_kh)
    p.op("sync", lambda e: e.dma_start(out=qidx_sb, in_=qidx), sem=s_kh, k=16)
    Rsb = ar.alloc([128, 512], F32)
    umax = ar.alloc([1, 8], F32)
    negu = ar.alloc([128, 1], F32)
    sq2 = [ar.alloc([96, 512], BF16) for _ in range(2)]
    unorm = ar.alloc([1, 32], F32)
    ostag = ar.alloc([128, 512], F32)
    s_sq = p.sem(pref + "usq")
    s_nq = p.sem(pref + "unq")
    s_nr = p.sem(pref + "unr")
    s_oc2 = p.sem(pref + "oc2")
    p.wait("sync", s_kh)
    for r in range(4):
        p.op("sync", lambda e, r=r: e.dma_start(
            out=qstack[64:96, r * 2048:(r + 1) * 2048],
            in_=ag1_out.ap()[r * 160: r * 160 + 32, :]), sem=s_kh, k=16)

    kbs3 = {}
    for gb in range(ngb):
        ext = (gb + 1) * 4
        kbs3[gb] = [list(range(i, min(i + 3, ext))) for i in range(0, ext, 3)]

    ptb_last_read = {0: 0, 1: 0}
    exp_read = {0: 0, 1: 0}
    for hi in range(nheads):
        par = hi % 2
        if hi == 0:
            p.wait("gpsimd", s_cc, cc_agq)
        p.wait("gpsimd", s_av)
        p.wait("gpsimd", s_kh)
        p.wait("gpsimd", s_qev)
        for r in range(4):
            p.op("gpsimd", lambda e, r=r, hi=hi: e.indirect_dma_start(
                out=qgstg[:, r, :], out_offset=None, in_=agq_out.ap(),
                in_offset=IndirectOffsetOnAxis(ap=qidx_sb[:, hi, r:r + 1], axis=0)),
                sem=s_dl[par], k=16)
        p.wait("vector", s_dl[par])
        for r in range(4):
            p.op("vector", lambda e, r=r: e.tensor_copy(
                qstack[0:64, r * 2048:(r + 1) * 2048], qgstg[0:64, r, :]), sem=s_qev)
        p.wait("tensor", s_ex)
        p.wait("tensor", s_oev)
        p.wait("vector", s_sc)
        p.wait("vector", s_av)
        for gb in range(16):
            r, bi = gcol[gb]
            ps = psum.f((gb % 2) * 512, 512)[0:64, :]
            if gb >= 2:
                p.wait("tensor", s_qev, p.val(s_qev) - 1)
            p.op("tensor", lambda e, ps=ps, r=r, bi=bi, hi=hi: e.matmul(
                ps, wkup_sb[:, hi * 64:(hi + 1) * 64],
                ckv_g[:, r, bi * 512:(bi + 1) * 512], start=True, stop=True),
                sem=s_qkv)
            p.wait("vector", s_qkv)
            p.op("vector", lambda e, ps=ps, gb=gb: e.tensor_copy(
                kstack[0:64, gb * 512:(gb + 1) * 512], ps), sem=s_qev)
        p.wait("vector", s_qev)
        p.op("vector", lambda e: e.memset(vform, 0.0), sem=s_qev)
        p.wait("vector", s_qev)
        oc = 0 if par == 1 else 64
        p.op("vector", lambda e, oc=oc: e.tensor_copy(
            vform[:, :, oc:oc + 1], onesq.unsqueeze(2)), sem=s_qev)
        vcol = 0 if par == 0 else 64
        for gt in range(64):
            gb4 = gt // 4
            r, bi = gcol[gb4]
            tt = gt % 4
            ps = psum.f(1024 + (gt % 2) * 512, 512)[:, 0:64]
            if gt >= 2:
                p.wait("tensor", s_qev, p.val(s_qev) - 1)
            p.op("tensor", lambda e, ps=ps, r=r, bi=bi, tt=tt, hi=hi: e.matmul(
                ps, ckv_g[:, r, bi * 512 + tt * 128: bi * 512 + (tt + 1) * 128],
                wvup_sb[:, hi * 64:(hi + 1) * 64], start=True, stop=True),
                sem=s_qkv)
            p.wait("vector", s_qkv)
            p.op("vector", lambda e, ps=ps, gt=gt, vcol=vcol: e.tensor_copy(
                vform[:, gt, vcol:vcol + 64], ps), sem=s_qev)
        p.wait("vector", s_dl[par])
        p.wait("scalar", s_qev)
        for idx in range(32):
            ui, blk = idx // 16, idx % 16
            buf = qstack if ui == 0 else kstack
            sq_b = sq2[idx % 2]
            psu = psum.f(3072 + (idx % 2) * 512, 512)[0:1, :]
            p.wait("scalar", s_nq, p.val(s_nq) - 1)
            p.op("scalar", lambda e, buf=buf, blk=blk, sq_b=sq_b: e.activation(
                sq_b, buf[0:96, blk * 512:(blk + 1) * 512], AF.Square,
                bias=0.0, scale=1.0), sem=s_sq)
            p.wait("tensor", s_sq)
            p.wait("tensor", s_nr, p.val(s_nr) - 1)
            p.op("tensor", lambda e, psu=psu, sq_b=sq_b: e.matmul(
                psu, onesn, sq_b, start=True, stop=True), sem=s_nq)
            p.wait("vector", s_nq)
            p.op("vector", lambda e, psu=psu, idx=idx: e.tensor_reduce(
                unorm[0:1, idx:idx + 1], psu, mybir.AxisListType.X, ALU.max),
                sem=s_nr)
        p.wait("vector", s_nr)
        p.op("vector", lambda e: e.tensor_reduce(
            umax[0:1, 0:1], unorm[0:1, 0:16], mybir.AxisListType.X, ALU.max),
            sem=s_nrm)
        p.op("vector", lambda e: e.tensor_reduce(
            umax[0:1, 1:2], unorm[0:1, 16:32], mybir.AxisListType.X, ALU.max),
            sem=s_nrm)
        p.wait("vector", s_nrm)
        p.op("vector", lambda e: e.tensor_tensor(
            umax[0:1, 3:4], umax[0:1, 0:1], umax[0:1, 1:2], ALU.mult), sem=s_nrm)
        p.wait("scalar", s_nrm)
        p.op("scalar", lambda e: e.activation(
            umax[0:1, 4:5], umax[0:1, 3:4], AF.Sqrt, bias=0.0, scale=1.0), sem=s_nrm)
        p.wait("scalar", s_nrm)
        p.op("scalar", lambda e: e.mul(umax[0:1, 5:6], umax[0:1, 4:5], -L1_SCALE), sem=s_nrm)
        psn = psum.f(3584, 512)[:, 0:1]
        p.wait("tensor", s_nrm)
        p.op("tensor", lambda e, ps=psn: e.matmul(
            ps, cons["ones_f32"], umax[0:1, 5:6], start=True, stop=True), sem=s_nrm)
        p.wait("vector", s_nrm)
        p.op("vector", lambda e, ps=psn: e.tensor_copy(negu, ps), sem=s_nrm)

        if stage < 3:
            continue

        p.wait("tensor", s_nrm)
        p.wait("tensor", s_nr)
        for gb in range(ngb):
            otps = psum.f(0, 512)
            p.wait("tensor", s_oc2)
            batches = kbs3[gb]
            nb = len(batches)
            sc_done = {}
            qcol = xq[gb] * 512
            for it in range(nb + 1):
                if it < nb:
                    bi2, batch = it, batches[it]
                    stps = psum.f(512 + (bi2 % 2) * 1536, 1536)
                    p.wait("tensor", s_ex, exp_read[bi2 % 2])
                    for j, kt in enumerate(batch):
                        p.op("tensor", lambda e, stps=stps, j=j, kt=kt, qcol=qcol: e.matmul(
                            stps[:, j * 512:(j + 1) * 512],
                            kstack[:, kt * 128:(kt + 1) * 128],
                            qstack[:, qcol:qcol + 512],
                            start=True, stop=True),
                            sem=(s_sc if j == len(batch) - 1 else None))
                    if any(kt - gb * 4 >= 0 for kt in batch):
                        p.wait("vector", s_sc)
                        for j, kt in enumerate(batch):
                            jj = kt - gb * 4
                            if jj >= 0:
                                p.op("vector", lambda e, stps=stps, j=j, jj=jj: e.tensor_tensor(
                                    stps[:, j * 512:(j + 1) * 512],
                                    stps[:, j * 512:(j + 1) * 512],
                                    masks_sb[:, jj, :], ALU.add), sem=s_sc)
                    sc_done[bi2] = p.val(s_sc)
                if it >= 1:
                    bi2, batch = it - 1, batches[it - 1]
                    stps = psum.f(512 + (bi2 % 2) * 1536, 1536)
                    ptbuf = ptb[bi2 % 2]
                    p.wait("scalar", s_sc, sc_done[bi2])
                    p.wait("scalar", s_nrm)
                    p.wait("scalar", s_av, ptb_last_read[bi2 % 2])
                    n = len(batch) * 512
                    p.op("scalar", lambda e, stps=stps, ptbuf=ptbuf, n=n: e.activation(
                        ptbuf[:, 0:n], stps[:, 0:n], AF.Exp, bias=negu, scale=L1_SCALE),
                        sem=s_ex)
                    exp_read[bi2 % 2] = p.val(s_ex)
                    p.wait("tensor", s_ex)
                    for j, kt in enumerate(batch):
                        first = (bi2 == 0 and j == 0)
                        last = (bi2 == nb - 1 and j == len(batch) - 1)
                        vap = vform[:, kt, 0:65] if par == 0 else vform[:, kt, 0:128]
                        oap = otps[0:65, :] if par == 0 else otps
                        p.op("tensor", lambda e, vap=vap, oap=oap, ptbuf=ptbuf, j=j,
                             first=first, last=last: e.matmul(
                                 oap, vap, ptbuf[:, j * 512:(j + 1) * 512],
                                 start=first, stop=last), sem=s_av)
                    ptb_last_read[bi2 % 2] = p.val(s_av)
            p.wait("vector", s_av)
            p.wait("vector", s_oev)
            p.op("vector", lambda e, otps=otps: e.tensor_copy(ostag, otps),
                 sem=s_oc2)
            sumrow = ostag[64:65, :] if par == 0 else ostag[0:1, :]
            p.op("vector", lambda e, sumrow=sumrow: e.tensor_scalar(
                rrow, sumrow, 1e-35, None, ALU.add, ALU.bypass), sem=s_oev)
            p.wait("vector", s_oev)
            p.op("vector", lambda e: e.reciprocal(rrow, rrow), sem=s_oev)
            psR = psum.f(3584, 512)
            p.wait("tensor", s_oev)
            p.op("tensor", lambda e, ps=psR: e.matmul(
                ps, cons["ones_f32"], rrow, start=True, stop=True), sem=s_oev)
            p.wait("vector", s_oev)
            p.op("vector", lambda e, ps=psR: e.tensor_copy(Rsb, ps), sem=s_oev)
            p.wait("vector", s_oev)
            vr = (0, 64) if par == 0 else (64, 128)
            qcol2 = xq[gb] * 512
            p.op("vector", lambda e, vr=vr, hi=hi, qcol2=qcol2: e.tensor_tensor(
                attn[vr[0]:vr[1], hi // 2, qcol2:qcol2 + 512],
                ostag[vr[0]:vr[1], :], Rsb[vr[0]:vr[1], :], ALU.mult), sem=s_oev)

        if stage >= 4 and hi in (1, 3):
            pr = hi // 2
            p.wait("scalar", s_oev)
            p.op("scalar", lambda e, pr=pr: e.dma_start(
                out=ag2_in[pr].ap(), in_=attn[:, pr, :]), sem=s_cc, k=16)
            p.wait("gpsimd", s_cc)
            p.op("gpsimd", lambda e, pr=pr: e.collective_compute(
                "AllGather", ALU.bypass, replica_groups=groups,
                ins=[ag2_in[pr].ap().opt()], outs=[ag2_out[pr].ap().opt()]),
                sem=s_cc, k=1)

    if stage < 4:
        return s_oev

    for eng in ENGS:
        p.wait(eng, s_cc)
    ar.release()

    ar.mark()
    attn_own = ar.alloc([128, 8, 2048], BF16)
    wo_sb = ar.alloc([128, 8, 1024], BF16)
    aidx_sb = ar.alloc([128, 8], U32)
    p.wait("sync", s_kh)
    p.op("sync", lambda e: e.dma_start(out=aidx_sb, in_=attnidx), sem=s_kh, k=16)
    p.op("gpsimd", lambda e: e.dma_start(
        out=wo_sb, in_=w_o.rearrange("(t p) c -> p t c", p=128)), sem=s_w, k=16)
    p.wait("gpsimd", s_kh)
    ag2v = [t.ap().rearrange("r (g c) -> (r g) c", g=4) for t in ag2_out]
    for dt in range(8):
        p.op("gpsimd", lambda e, dt=dt: e.indirect_dma_start(
            out=attn_own[:, dt, :], out_offset=None, in_=ag2v[dt % 2],
            in_offset=IndirectOffsetOnAxis(ap=aidx_sb[:, dt:dt + 1], axis=0)),
            sem=s_kst, k=16)
    p.wait("tensor", s_kst)
    p.wait("tensor", s_kh)
    p.wait("tensor", s_w)
    ystg = ar.alloc([128, 512], F32)
    for mt in range(8):
        for nt4 in range(4):
            ps = psum.f(((mt * 4 + nt4) % 2) * 512, 512)
            if mt * 4 + nt4 >= 2:
                p.wait("tensor", s_qev, p.val(s_qev) - 1)
            for dt in range(8):
                p.op("tensor", lambda e, ps=ps, mt=mt, nt4=nt4, dt=dt: e.matmul(
                    ps, wo_sb[:, dt, mt * 128:(mt + 1) * 128],
                    attn_own[:, dt, nt4 * 512:(nt4 + 1) * 512],
                    start=(dt == 0), stop=(dt == 7)),
                    sem=(s_qkv if dt == 7 else None))
            p.wait("vector", s_qkv)
            p.wait("vector", s_tmm)
            p.op("vector", lambda e, ps=ps: e.tensor_copy(ystg, ps), sem=s_qev)
            for sub in range(4):
                t = nt4 * 4 + sub
                px = psum.f(1536, 128)
                p.wait("tensor", s_qev)
                p.wait("tensor", s_fin)
                p.op("tensor", lambda e, px=px, sub=sub: e.transpose(
                    px, ystg[:, sub * 128:(sub + 1) * 128], cons["ident_f32"]),
                    sem=s_tmm)
                p.wait("vector", s_tmm)
                p.op("vector", lambda e, px=px, t=t, mt=mt: e.tensor_tensor(
                    x_sb[:, t, mt * 128:(mt + 1) * 128],
                    x_sb[:, t, mt * 128:(mt + 1) * 128], px, ALU.add), sem=s_fin)
    ar.release()
    return s_fin




def emit_rotary(p, buf, cs, nt, s_ev, r0):
    v1 = buf[:, 0:nt, 0:16]
    v2 = buf[:, 0:nt, 16:32]
    co = cs[:, 0:nt, 0:16]
    si = cs[:, 0:nt, 16:32]
    sA = r0[:, 0:nt, 0:16]
    sB = r0[:, 0:nt, 16:32]
    for f in [
        lambda e: e.tensor_tensor(sA, v1, co, ALU.mult),
        lambda e: e.tensor_tensor(sB, v1, si, ALU.mult),
        lambda e: e.tensor_tensor(v1, v2, si, ALU.mult),
        lambda e: e.tensor_tensor(v1, sA, v1, ALU.subtract),
        lambda e: e.tensor_tensor(v2, v2, co, ALU.mult),
        lambda e: e.tensor_tensor(v2, v2, sB, ALU.add),
    ]:
        p.wait("vector", s_ev)
        p.op("vector", f, sem=s_ev)



CAP = 640
CROW = CAP + 1
MROWS = TPC + CAP
N_BIS = 30


def emit_l0f_wprep(p, ar, psum, cons, wg_loc, wu_loc, wd_loc, pref="m0p"):
    nc = p.nc
    s_w = p.sem(pref + "w")
    s_tmm = p.sem(pref + "tmm")
    s_tev = p.sem(pref + "tev")
    s_wst = p.sem(pref + "wst")
    s_wst2 = p.sem(pref + "wst2")
    s_cc = p.sem(pref + "cc")
    s_zr = p.sem(pref + "zr")
    s_pn = p.sem(pref + "pn")
    wgT_loc = nc.dram_tensor(pref + "_wgTl", [4096, 1024], BF16)
    wuT_loc = nc.dram_tensor(pref + "_wuTl", [4096, 1024], BF16)
    wdT_loc = nc.dram_tensor(pref + "_wdTl", [4096, 1024], BF16)
    wgT_all = nc.dram_tensor(pref + "_wgTa", [8 * 4096, 1024], BF16, addr_space="Shared")
    wuT_all = nc.dram_tensor(pref + "_wuTa", [8 * 4096, 1024], BF16, addr_space="Shared")
    wdT_all = nc.dram_tensor(pref + "_wdTa", [8 * 4096, 1024], BF16, addr_space="Shared")
    compact = nc.dram_tensor(pref + "_cmp", [8 * CROW, 1040], BF16).ap()
    moeacc = nc.dram_tensor(pref + "_acc", [MROWS, 1024], F32).ap()
    zt = ar.alloc([128, 1040], BF16)

    p.op("vector", lambda e: e.memset(zt, 0.0), sem=s_pn)
    p.wait("gpsimd", s_pn)
    for r0 in range(0, 8 * CROW, 128):
        r1 = min(r0 + 128, 8 * CROW)
        p.op("gpsimd", lambda e, r0=r0, r1=r1: e.dma_start(
            out=compact[r0:r1, :], in_=zt[0:r1 - r0, :]), sem=s_zr, k=16)
    ztf = zt.bitcast(F32)
    for r0 in range(0, MROWS, 128):
        r1 = min(r0 + 128, MROWS)
        for c0 in (0, 512):
            p.op("gpsimd", lambda e, r0=r0, r1=r1, c0=c0: e.dma_start(
                out=moeacc[r0:r1, c0:c0 + 512], in_=ztf[0:r1 - r0, 0:512]),
                sem=s_zr, k=16)

    for w_loc, wT_loc, R, C, grp in ((wg_loc, wgT_loc, 4096, 1024, "r"),
                                     (wu_loc, wuT_loc, 4096, 1024, "r"),
                                     (wd_loc, wdT_loc, 1024, 4096, "c")):
        emit_wT_to_hbm(p, ar, psum, cons, w_loc, R, C,
                       wT_loc.ap().rearrange("(t p) c -> t p c", p=128),
                       grp, (s_w, s_tmm, s_tev, s_wst, s_wst2),
                       store_eng="sync")
    p.wait("tensor", s_tev)
    return {"wgT_all": wgT_all, "wuT_all": wuT_all, "wdT_all": wdT_all,
            "wgT_loc": wgT_loc, "wuT_loc": wuT_loc, "wdT_loc": wdT_loc,
            "compact": compact, "moeacc": moeacc,
            "s_cc": s_cc, "s_zr": s_zr, "zr_total": p.val(s_zr),
            "s_wst": s_wst, "s_wst2": s_wst2,
            "gates": [(s_wst, p.val(s_wst)), (s_wst2, p.val(s_wst2))]}


def emit_l1f_wprep(p, ar, psum, cons, w_fc, w_proj, pref="f1p"):
    nc = p.nc
    s_w = p.sem(pref + "w")
    s_tmm = p.sem(pref + "tmm")
    s_tev = p.sem(pref + "tev")
    s_wst = p.sem(pref + "wst")
    s_wst2 = p.sem(pref + "wst2")
    wfcT_hbm = nc.dram_tensor(pref + "_wfcT", [32, 128, 1024], BF16).ap()
    wprT_hbm = nc.dram_tensor(pref + "_wprT", [32, 128, 1024], BF16).ap()
    emit_wT_to_hbm(p, ar, psum, cons, w_fc, 4096, 1024, wfcT_hbm, "r",
                   (s_w, s_tmm, s_tev, s_wst, s_wst2), store_eng="sync")
    emit_wT_to_hbm(p, ar, psum, cons, w_proj, 1024, 4096, wprT_hbm, "c",
                   (s_w, s_tmm, s_tev, s_wst, s_wst2), store_eng="sync")
    p.wait("tensor", s_tev)
    return {"wfcT": wfcT_hbm, "wprT": wprT_hbm,
            "s_wst": s_wst, "s_wst2": s_wst2,
            "gates": [(s_wst, p.val(s_wst)), (s_wst2, p.val(s_wst2))]}


def emit_l0f(p, ar, psum, cons, x_sb, gate_w, wp,
             m32, erow, idrow, trasht, pref="m0", nexp=8, stage=9,
             dbg=None):
    nc = p.nc
    s_w = p.sem(pref + "w")
    s_tmm = p.sem(pref + "tmm")
    s_tev = p.sem(pref + "tev")
    s_wst = p.sem(pref + "wst")
    s_cc = wp["s_cc"]
    s_ca = p.sem(pref + "ca")
    s_cg = p.sem(pref + "cg")
    s_pn = p.sem(pref + "pn")
    s_mm = p.sem(pref + "mm")
    s_ev = p.sem(pref + "ev")
    s_hh = p.sem(pref + "hh")
    s_dm = p.sem(pref + "dm")
    s_xe = p.sem(pref + "xe")
    s_kh = p.sem(pref + "kh")
    s_bi = p.sem(pref + "bi")
    s_sca = p.sem(pref + "sca")
    s_add = p.sem(pref + "add")
    s_ws = [p.sem(pref + "ws0"), p.sem(pref + "ws1")]
    s_wd = [p.sem(pref + "wd0"), p.sem(pref + "wd1")]
    s_fin = p.sem(pref + "fin")

    af_in = nc.dram_tensor(pref + "_afin", [8, TPC], F32)
    af_out = nc.dram_tensor(pref + "_afout", [32, TPC], F32)
    compact = wp["compact"]
    moeacc = wp["moeacc"]
    wg_v = wp["wgT_all"].ap().rearrange("(E t p) c -> E t p c", E=8, p=128)
    wu_v = wp["wuT_all"].ap().rearrange("(E t p) c -> E t p c", E=8, p=128)
    wd_v = wp["wdT_all"].ap().rearrange("(E t p) c -> E t p c", E=8, p=128)

    affin = ar.alloc([128, NT, 8], F32)
    scr = ar.alloc([128, 1024], F32)
    stats = ar.alloc([128, 24], F32)
    mr = ar.alloc([128, 2], F32)
    gwT = ar.alloc([128, 8, 8], F32)
    xlt = ar.alloc([128, 8, 128], F32)
    ar.mark()
    comb = ar.alloc([128, NT, 1040], BF16)
    gw_stg = ar.alloc([128, 1024], F32)
    p.op("sync", lambda e: e.dma_start(out=gw_stg[0:8, :], in_=gate_w), sem=s_kh, k=16)
    p.wait("tensor", s_kh)
    for dt in range(8):
        slot = p.val(s_tmm) % 2
        ps = psum.t_u8[:, 12288 + slot * 2048: 12288 + slot * 2048 + 512].bitcast(F32)
        if p.val(s_tmm) >= 2:
            p.wait("tensor", s_tev, p.val(s_tev) - 1)
        p.op("tensor", lambda e, ps=ps, dt=dt: e.transpose(
            ps, gw_stg[:, dt * 128:(dt + 1) * 128], cons["ident_f32"]), sem=s_tmm)
        p.wait("vector", s_tmm)
        p.op("vector", lambda e, ps=ps, dt=dt: e.tensor_copy(
            gwT[:, dt, :], ps[:, 0:8]), sem=s_tev)

    for t in range(NT):
        if t > 0:
            p.wait("vector", s_tmm)
        emit_prenorm_tile(p, ar, x_sb[:, t, :], scr, comb[:, t, 0:1024],
                          stats, mr, s_pn, cons)
        p.wait("tensor", s_pn)
        for dt in range(8):
            slot = p.val(s_tmm) % 2
            ps = psum.t_u8[:, 12288 + slot * 2048: 12288 + slot * 2048 + 512].bitcast(F32)
            p.wait("tensor", s_tev, p.val(s_tev) - 1)
            p.op("tensor", lambda e, ps=ps, dt=dt: e.transpose(
                ps, scr[:, dt * 128:(dt + 1) * 128], cons["ident_f32"]), sem=s_tmm)
            p.wait("vector", s_tmm)
            p.op("vector", lambda e, ps=ps, dt=dt: e.tensor_copy(
                xlt[:, dt, :], ps), sem=s_tev)
        lg = psum.f(2048 + (t % 2) * 512, 512)[:, 0:8]
        p.wait("tensor", s_tev)
        if t >= 2:
            p.wait("tensor", s_ev, p.val(s_ev) - 1)
        for dt in range(8):
            p.op("tensor", lambda e, lg=lg, dt=dt: e.matmul(
                lg, xlt[:, dt, :], gwT[:, dt, :], start=(dt == 0), stop=(dt == 7)),
                sem=(s_mm if dt == 7 else None))
        p.wait("vector", s_mm)
        p.op("vector", lambda e, lg=lg: e.reduce_max(
            stats[:, 16:17], lg, mybir.AxisListType.X), sem=s_bi)
        p.wait("vector", s_bi)
        p.op("vector", lambda e: e.tensor_scalar(
            stats[:, 17:18], stats[:, 16:17], -1.0, None, ALU.mult, ALU.bypass),
            sem=s_bi)
        p.wait("scalar", s_bi)
        p.op("scalar", lambda e, lg=lg, t=t: e.activation(
            affin[:, t, :], lg, AF.Exp, bias=stats[:, 17:18], scale=1.0), sem=s_ev)
        p.wait("vector", s_ev)
        p.op("vector", lambda e, t=t: e.reduce_sum(
            stats[:, 18:19], affin[:, t, :], mybir.AxisListType.X), sem=s_bi)
        p.wait("vector", s_bi)
        p.op("vector", lambda e: e.reciprocal(
            stats[:, 19:20], stats[:, 18:19]), sem=s_bi)
        p.wait("vector", s_bi)
        p.op("vector", lambda e, t=t: e.tensor_scalar(
            affin[:, t, :], affin[:, t, :], stats[:, 19:20], None, ALU.mult,
            ALU.bypass), sem=s_bi)

    if stage < 1:
        return s_bi

    aem = ar.alloc([128, TPC], F32)
    stg = ar.alloc([128, 128], F32)
    p.op("vector", lambda e: e.memset(stg, 0.0), sem=s_bi)
    p.wait("vector", s_ev)
    for t in range(NT):
        p.wait("vector", s_bi)
        p.wait("vector", s_tmm)
        p.op("vector", lambda e, t=t: e.tensor_copy(
            stg[:, 0:8], affin[:, t, :]), sem=s_bi)
        p.wait("tensor", s_bi)
        slot = p.val(s_tmm) % 2
        ps = psum.t_u8[:, 12288 + slot * 2048: 12288 + slot * 2048 + 512].bitcast(F32)
        p.wait("tensor", s_tev, p.val(s_tev) - 1)
        p.op("tensor", lambda e, ps=ps: e.transpose(
            ps, stg, cons["ident_f32"]), sem=s_tmm)
        p.wait("vector", s_tmm)
        p.op("vector", lambda e, ps=ps, t=t: e.tensor_copy(
            aem[0:8, t * 128:(t + 1) * 128], ps[0:8, :]), sem=s_tev)
    p.wait("scalar", s_tev)
    p.op("scalar", lambda e: e.dma_start(out=af_in.ap(), in_=aem[0:8, :]),
         sem=s_ca, k=16)
    p.wait("gpsimd", wp["s_wst"])
    p.wait("gpsimd", wp["s_wst2"])
    p.op("gpsimd", lambda e: e.collective_compute(
        "AllGather", ALU.bypass, replica_groups=[[0, 1, 2, 3, 4, 5, 6, 7]],
        ins=[wp["wgT_loc"].ap().opt()], outs=[wp["wgT_all"].ap().opt()]),
        sem=s_cc, k=1)
    p.wait("gpsimd", s_ca, 16)
    p.op("gpsimd", lambda e: e.collective_compute(
        "AllGather", ALU.bypass, replica_groups=[[0, 1, 2, 3], [4, 5, 6, 7]],
        ins=[af_in.ap().opt()], outs=[af_out.ap().opt()]), sem=s_cg, k=1)
    p.op("gpsimd", lambda e: e.collective_compute(
        "AllGather", ALU.bypass, replica_groups=[[0, 1, 2, 3, 4, 5, 6, 7]],
        ins=[wp["wuT_loc"].ap().opt()], outs=[wp["wuT_all"].ap().opt()]),
        sem=s_cc, k=1)
    p.op("gpsimd", lambda e: e.collective_compute(
        "AllGather", ALU.bypass, replica_groups=[[0, 1, 2, 3, 4, 5, 6, 7]],
        ins=[wp["wdT_loc"].ap().opt()], outs=[wp["wdT_all"].ap().opt()]),
        sem=s_cc, k=1)

    af32 = ar.alloc([32, TPC], F32)
    junk = ar.alloc([32, TPC], F32)
    m32_sb = ar.alloc([32, 32], F32)
    bis = ar.alloc([32, 8], F32)
    selu32 = ar.alloc([32, 1], U8)
    p.wait("sync", s_cg)
    p.op("sync", lambda e: e.dma_start(out=af32, in_=af_out.ap()), sem=s_kh, k=16)
    p.op("sync", lambda e: e.dma_start(out=m32_sb, in_=m32), sem=s_kh, k=16)
    p.wait("vector", s_kh)
    p.wait("tensor", s_kh)
    p.op("vector", lambda e: e.memset(bis[:, 0:1], 0.0), sem=s_bi)
    p.op("vector", lambda e: e.memset(bis[:, 1:2], 1.0), sem=s_bi)
    for it in range(N_BIS):
        p.wait("vector", s_bi)
        p.op("vector", lambda e: e.tensor_tensor(
            bis[:, 2:3], bis[:, 0:1], bis[:, 1:2], ALU.add), sem=s_bi)
        p.wait("vector", s_bi)
        p.op("vector", lambda e: e.tensor_scalar(
            bis[:, 2:3], bis[:, 2:3], 0.5, None, ALU.mult, ALU.bypass), sem=s_bi)
        p.wait("vector", s_bi)
        p.op("vector", lambda e: e.tensor_scalar(
            junk, af32, bis[:, 2:3], 0.0, ALU.is_ge, ALU.add,
            accum_out=bis[:, 4:5]), sem=s_bi)
        ps = psum.f(2048, 512)
        p.wait("tensor", s_bi)
        p.op("tensor", lambda e, ps=ps: e.matmul(
            ps[0:32, 0:1], m32_sb, bis[:, 4:5], start=True, stop=True), sem=s_bi)
        p.wait("vector", s_bi)
        p.op("vector", lambda e, ps=ps: e.tensor_copy(
            bis[:, 3:4], ps[0:32, 0:1]), sem=s_bi)
        p.wait("vector", s_bi)
        p.op("vector", lambda e: e.tensor_scalar(
            selu32, bis[:, 3:4], 2048.0, None, ALU.is_ge, ALU.bypass), sem=s_bi)
        p.wait("vector", s_bi)
        p.op("vector", lambda e: e.select(
            bis[:, 5:6], selu32, bis[:, 2:3], bis[:, 0:1]), sem=s_bi)
        p.op("vector", lambda e: e.select(
            bis[:, 6:7], selu32, bis[:, 1:2], bis[:, 2:3]), sem=s_bi)
        p.wait("vector", s_bi)
        p.op("vector", lambda e: e.tensor_copy(bis[:, 0:1], bis[:, 5:6]), sem=s_bi)
        p.op("vector", lambda e: e.tensor_copy(bis[:, 1:2], bis[:, 6:7]), sem=s_bi)

    if dbg and "bis" in dbg:
        p.wait("scalar", s_bi)
        p.op("scalar", lambda e: e.dma_start(out=dbg["bis"], in_=bis), sem=s_kh, k=16)
    if dbg and "aem" in dbg:
        p.wait("scalar", s_tev)
        p.op("scalar", lambda e: e.dma_start(out=dbg["aem"], in_=aem[0:8, :]), sem=s_kh, k=16)
    if dbg and "af32" in dbg:
        p.wait("scalar", s_kh)
        p.op("scalar", lambda e: e.dma_start(out=dbg["af32"], in_=af32), sem=s_kh, k=16)
    if stage < 2:
        p.wait("scalar", s_kh)
        return s_bi

    idsb = ar.alloc([128, NT, 1], I16)
    p.op("sync", lambda e: e.dma_start(out=idsb, in_=idrow), sem=s_kh, k=16)
    maskf = ar.alloc([8, TPC], F32)
    posem = ar.alloc([8, TPC], F32)
    ones8 = ar.alloc([8, TPC], F32)
    selm = ar.alloc([8, TPC], U8)
    selmf = ar.alloc([8, TPC], F32)
    posx = ar.alloc([8, TPC], F32)
    bigc = ar.alloc([8, TPC], F32)
    erow_sb = ar.alloc([8, 1], F32)
    pos_tm = ar.alloc([128, NT, 8], F32)
    posu = ar.alloc([128, NT, 8], U32)
    p.op("sync", lambda e: e.dma_start(out=erow_sb, in_=erow), sem=s_kh, k=16)
    p.op("vector", lambda e: e.memset(ones8, 1.0), sem=s_bi)
    p.op("vector", lambda e: e.memset(bigc, float(CAP)), sem=s_bi)
    p.wait("vector", s_kh)
    p.wait("vector", s_bi)
    p.op("vector", lambda e: e.tensor_scalar(
        maskf, aem[0:8, :], bis[0:8, 0:1], None, ALU.is_ge, ALU.bypass), sem=s_bi)
    if dbg and "maskf" in dbg:
        p.wait("scalar", s_bi)
        p.op("scalar", lambda e: e.dma_start(out=dbg["maskf"], in_=maskf), sem=s_kh, k=16)
        p.wait("vector", s_kh)
    p.wait("vector", s_bi)
    p.op("vector", lambda e: e.tensor_tensor_scan(
        posem, ones8, maskf, 0.0, ALU.mult, ALU.add), sem=s_bi)
    if dbg and "scan" in dbg:
        p.wait("scalar", s_bi)
        p.op("scalar", lambda e: e.dma_start(out=dbg["scan"], in_=posem), sem=s_kh, k=16)
        p.wait("vector", s_kh)
    p.wait("vector", s_bi)
    p.op("vector", lambda e: e.tensor_scalar(
        posem, posem, -1.0, None, ALU.add, ALU.bypass), sem=s_bi)
    p.wait("vector", s_bi)
    p.op("vector", lambda e: e.tensor_scalar(
        selmf, posem, float(CAP), None, ALU.is_lt, ALU.bypass), sem=s_bi)
    p.wait("vector", s_bi)
    p.op("vector", lambda e: e.tensor_tensor(
        selmf, selmf, maskf, ALU.mult), sem=s_bi)
    p.wait("vector", s_bi)
    p.op("vector", lambda e: e.tensor_scalar(
        selm, selmf, 0.5, None, ALU.is_ge, ALU.bypass), sem=s_bi)
    p.wait("vector", s_bi)
    p.op("vector", lambda e: e.select(posx, selm, posem, bigc), sem=s_bi)
    p.wait("vector", s_bi)
    p.op("vector", lambda e: e.tensor_scalar(
        posem, posx, erow_sb, None, ALU.add, ALU.bypass), sem=s_bi)
    p.op("vector", lambda e: e.memset(stg, 0.0), sem=s_bi)
    for t in range(NT):
        p.wait("vector", s_bi)
        p.wait("vector", s_tmm)
        p.op("vector", lambda e, t=t: e.tensor_copy(
            stg[0:8, :], posem[0:8, t * 128:(t + 1) * 128]), sem=s_bi)
        p.wait("tensor", s_bi)
        slot = p.val(s_tmm) % 2
        ps = psum.t_u8[:, 12288 + slot * 2048: 12288 + slot * 2048 + 512].bitcast(F32)
        p.wait("tensor", s_tev, p.val(s_tev) - 1)
        p.op("tensor", lambda e, ps=ps: e.transpose(
            ps, stg, cons["ident_f32"]), sem=s_tmm)
        p.wait("vector", s_tmm)
        p.op("vector", lambda e, ps=ps, t=t: e.tensor_copy(
            pos_tm[:, t, :], ps[:, 0:8]), sem=s_tev)
    p.wait("vector", s_tev)
    p.op("vector", lambda e: e.tensor_copy(posu, pos_tm), sem=s_bi)
    p.wait("vector", s_ev)
    p.op("vector", lambda e: e.tensor_copy(comb[:, :, 1024:1032], affin), sem=s_bi)
    p.wait("vector", s_kh)
    p.op("vector", lambda e: e.tensor_copy(
        comb[:, :, 1032:1033], idsb.bitcast(BF16)), sem=s_bi)
    p.wait("gpsimd", s_bi)
    p.wait("gpsimd", s_pn)
    p.wait("gpsimd", wp["s_zr"], wp["zr_total"])
    for e_ in range(nexp):
        for t in range(NT):
            p.op("gpsimd", lambda e, e_=e_, t=t: e.indirect_dma_start(
                out=compact, out_offset=IndirectOffsetOnAxis(
                    ap=posu[:, t, e_:e_ + 1], axis=0),
                in_=comb[:, t, :], in_offset=None), sem=s_sca, k=16)

    if dbg and "posu" in dbg:
        p.wait("scalar", s_bi)
        p.op("scalar", lambda e: e.dma_start(out=dbg["posu"], in_=posu), sem=s_kh, k=16)
        p.wait("scalar", s_kh)
    if dbg and "cmp" in dbg:
        p.wait("gpsimd", s_sca)
        p.op("gpsimd", lambda e: e.dma_start(out=dbg["cmp"], in_=compact), sem=s_sca, k=16)
    if stage < 3:
        return s_sca

    ar.release()
    ar.mark()
    rb = ar.alloc([128, 5, 1040], BF16)
    xin_fm = ar.alloc([128, 8, CAP], BF16)
    h2 = ar.alloc([128, 32, CAP], BF16)
    xe_sb = ar.alloc([128, 5, 1024], BF16)
    gat = ar.alloc([128, 5, 1], F32)
    idf = ar.alloc([128, 5, 1], F32)
    idxf = ar.alloc([128, 5, 1], F32)
    idxu = ar.alloc([128, 5, 1], U32)
    selg = ar.alloc([128, 5, 1], U8)
    trash_sb = ar.alloc([128, 5, 1], F32)
    wgb = [ar.alloc([128, 8, 128], BF16) for _ in range(2)]
    wub = [ar.alloc([128, 8, 128], BF16) for _ in range(2)]
    wdb = [ar.alloc([128, 512], BF16) for _ in range(2)]
    p.op("sync", lambda e: e.dma_start(out=trash_sb, in_=trasht), sem=s_kh, k=16)
    p.wait("vector", s_kh)
    p.wait("sync", s_cc, 2)
    for e_ in range(nexp):
        p.wait("sync", s_sca)
        p.wait("sync", s_tev)
        p.wait("sync", s_hh)
        p.op("sync", lambda e, e_=e_: e.dma_start(
            out=rb, in_=compact[e_ * CROW: e_ * CROW + CAP, :].rearrange(
                "(c p) w -> p c w", p=128)), sem=s_kh, k=16)
        p.wait("vector", s_kh)
        if e_ > 0:
            p.wait("vector", s_add)
        p.op("vector", lambda e, e_=e_: e.tensor_copy(
            gat, rb[:, :, 1024 + e_:1025 + e_]), sem=s_ev)
        p.op("vector", lambda e: e.tensor_copy(
            idf, rb[:, :, 1032:1033].bitcast(I16)), sem=s_ev)
        p.wait("vector", s_ev)
        p.op("vector", lambda e: e.tensor_scalar(
            selg, gat, 0.0, None, ALU.is_equal, ALU.bypass), sem=s_ev)
        p.wait("vector", s_ev)
        p.op("vector", lambda e: e.select(idxf, selg, trash_sb, idf), sem=s_ev)
        p.wait("vector", s_ev)
        p.op("vector", lambda e: e.tensor_copy(idxu, idxf), sem=s_ev)
        p.wait("tensor", s_kh)
        for c5 in range(5):
            for dt in range(8):
                slot = p.val(s_tmm) % 2
                ps = psum.t_u8[:, 12288 + slot * 2048:
                               12288 + slot * 2048 + 256].bitcast(BF16)
                p.wait("tensor", s_tev, p.val(s_tev) - 1)
                p.op("tensor", lambda e, ps=ps, c5=c5, dt=dt: e.transpose(
                    ps, rb[:, c5, dt * 128:(dt + 1) * 128], cons["ident_bf"]),
                    sem=s_tmm)
                p.wait("vector", s_tmm)
                p.op("vector", lambda e, ps=ps, c5=c5, dt=dt: e.tensor_copy(
                    xin_fm[:, dt, c5 * 128:(c5 + 1) * 128], ps), sem=s_tev)
        p.wait("tensor", s_tev)
        p.wait("tensor", s_xe)
        for it in range(32):
            for wb, hbm in ((wgb[it % 2], wg_v), (wub[it % 2], wu_v)):
                if p.val(s_mm) >= 2:
                    p.wait("sync", s_mm, p.val(s_mm) - 1)
                p.op("sync", lambda e, wb=wb, hbm=hbm, e_=e_, it=it: e.dma_start(
                    out=wb, in_=hbm[e_, it].rearrange("p (d c) -> p d c", d=8)),
                    sem=s_ws[it % 2], k=16)
            p.wait("tensor", s_ws[it % 2])
            b0 = (it % 2) * 2048
            psg0 = psum.f(b0, 512)
            psg1 = psum.f(b0 + 512, 512)[:, 0:128]
            psu0 = psum.f(b0 + 1024, 512)
            psu1 = psum.f(b0 + 1536, 512)[:, 0:128]
            if it >= 2:
                p.wait("tensor", s_ev, p.val(s_ev) - 2)
                p.wait("tensor", s_hh, p.val(s_hh) - 2)
            for dt in range(8):
                last = dt == 7
                p.op("tensor", lambda e, ps=psg0, wb=wgb[it % 2], dt=dt: e.matmul(
                    ps, wb[:, dt, :], xin_fm[:, dt, 0:512],
                    start=(dt == 0), stop=(dt == 7)))
                p.op("tensor", lambda e, ps=psg1, wb=wgb[it % 2], dt=dt: e.matmul(
                    ps, wb[:, dt, :], xin_fm[:, dt, 512:640],
                    start=(dt == 0), stop=(dt == 7)))
                p.op("tensor", lambda e, ps=psu0, wb=wub[it % 2], dt=dt: e.matmul(
                    ps, wb[:, dt, :], xin_fm[:, dt, 0:512],
                    start=(dt == 0), stop=(dt == 7)))
                p.op("tensor", lambda e, ps=psu1, wb=wub[it % 2], dt=dt: e.matmul(
                    ps, wb[:, dt, :], xin_fm[:, dt, 512:640],
                    start=(dt == 0), stop=(dt == 7)),
                    sem=(s_mm if last else None))
            p.wait("scalar", s_mm)
            p.op("scalar", lambda e, ps=psg0, it=it: e.activation(
                h2[:, it, 0:512], ps, AF.Silu, bias=0.0, scale=1.0), sem=s_ev)
            p.op("scalar", lambda e, ps=psg1, it=it: e.activation(
                h2[:, it, 512:640], ps, AF.Silu, bias=0.0, scale=1.0), sem=s_ev)
            p.wait("vector", s_ev)
            p.op("vector", lambda e, ps=psu0, it=it: e.tensor_tensor(
                h2[:, it, 0:512], h2[:, it, 0:512], ps, ALU.mult), sem=s_hh)
            p.op("vector", lambda e, ps=psu1, it=it: e.tensor_tensor(
                h2[:, it, 512:640], h2[:, it, 512:640], ps, ALU.mult), sem=s_hh)
        if e_ == 0:
            p.wait("sync", s_cc, 3)
        for half in range(2):
            p.wait("tensor", s_ev)
            p.wait("tensor", s_hh)
            if half == 1:
                p.wait("tensor", s_xe)
            for it in range(32):
                wb = wdb[it % 2]
                if p.val(s_dm) >= 2:
                    p.wait("sync", s_dm, p.val(s_dm) - 1)
                p.op("sync", lambda e, wb=wb, e_=e_, it=it, half=half: e.dma_start(
                    out=wb, in_=wd_v[e_, it][:, half * 512:(half + 1) * 512]),
                    sem=s_wd[it % 2], k=16)
                p.wait("tensor", s_wd[it % 2])
                for jc in range(5):
                    p.op("tensor", lambda e, jc=jc, wb=wb, it=it: e.matmul(
                        psum.f(jc * 512, 512), h2[:, it, jc * 128:(jc + 1) * 128],
                        wb, start=(it == 0), stop=(it == 31)),
                        sem=(s_dm if jc == 4 else None))
            p.wait("vector", s_dm)
            for jc in range(5):
                p.op("vector", lambda e, jc=jc, half=half: e.tensor_scalar(
                    xe_sb[:, jc, half * 512:(half + 1) * 512],
                    psum.f(jc * 512, 512), gat[:, jc, :], None, ALU.mult,
                    ALU.bypass), sem=s_xe)
        p.wait("gpsimd", s_xe)
        p.wait("gpsimd", s_add)
        if e_ == 0:
            p.wait("gpsimd", wp["s_zr"], wp["zr_total"])
        for jc in range(5):
            p.op("gpsimd", lambda e, jc=jc: e.indirect_dma_start(
                out=moeacc, out_offset=IndirectOffsetOnAxis(
                    ap=idxu[:, jc, :], axis=0),
                in_=xe_sb[:, jc, :], in_offset=None, compute_op=ALU.add),
                sem=s_add, k=16)

    if stage < 4:
        return s_sca

    if dbg and "acc" in dbg:
        p.wait("gpsimd", s_sca, p.val(s_sca))
        p.op("gpsimd", lambda e: e.dma_start(out=dbg["acc"], in_=moeacc), sem=s_sca, k=16)
    yb = [ar.alloc([128, 1024], F32) for _ in range(2)]
    s_yl = [p.sem(pref + "yl0"), p.sem(pref + "yl1")]
    p.wait("sync", s_sca)
    p.wait("sync", s_add)
    for t in range(NT):
        b = yb[t % 2]
        if t >= 2:
            p.wait("sync", s_fin, p.val(s_fin) - 1)
        p.op("sync", lambda e, b=b, t=t: e.dma_start(
            out=b, in_=moeacc.rearrange("(t p) d -> p t d", p=128)[:, t, :]),
            sem=s_yl[t % 2], k=16)
        p.wait("vector", s_yl[t % 2])
        p.op("vector", lambda e, b=b, t=t: e.tensor_tensor(
            x_sb[:, t, :], x_sb[:, t, :], b, ALU.add), sem=s_fin)
    ar.release()
    return s_fin


def l0f_host_inputs(core_id):
    m32 = np.zeros((32, 32), dtype=np.float32)
    for i in range(32):
        for j in range(32):
            if i % 8 == j % 8:
                m32[i, j] = 1.0
    erow = (np.arange(8, dtype=np.float32) * CROW).reshape(8, 1)
    idrow = np.zeros((128, NT, 1), dtype=np.int16)
    for t in range(NT):
        idrow[:, t, 0] = t * 128 + np.arange(128)
    trasht = np.zeros((128, 5, 1), dtype=np.float32)
    for c5 in range(5):
        trasht[:, c5, 0] = TPC + c5 * 128 + np.arange(128)
    return {"m32": m32, "erow": erow, "idrow": idrow,
            "trasht": trasht}



def build_l0a_launch(nchunks=4, stage=9):
    nc = bass.Bass("TRN2", target_bir_lowering=False)
    xin = nc.dram_tensor("xin", [TPC, D], F32, kind="ExternalInput").ap()
    wq = nc.dram_tensor("wq", [1024, 1024], F32, kind="ExternalInput").ap()
    wk = nc.dram_tensor("wk", [1024, 1024], F32, kind="ExternalInput").ap()
    wv = nc.dram_tensor("wv", [1024, 1024], F32, kind="ExternalInput").ap()
    cin = declare_consts(nc)
    xout = nc.dram_tensor("xout", [TPC, D], F32, kind="ExternalOutput").ap()

    p = Plan(nc)
    ar = Arena(nc, p._stack)
    psum = Psum(nc, p._stack)
    cons = load_consts(p, ar, cin)
    s_ld = p.sem("ldx")
    x_sb = ar.alloc([128, NT, 1024], F32)
    p.op("sync", lambda e: e.dma_start(
        out=x_sb, in_=xin.rearrange("(t p) d -> p t d", p=128)), sem=s_ld, k=16)
    for eng in ("tensor", "vector", "scalar"):
        p.wait(eng, s_ld, 16)
    s_done = emit_l0a(p, ar, psum, cons, x_sb, wq, wk, wv, nchunks=nchunks, stage=stage)
    s_st = p.sem("st")
    p.wait("scalar", s_done)
    p.op("scalar", lambda e: e.dma_start(
        out=xout.rearrange("(t p) d -> p t d", p=128), in_=x_sb), sem=s_st, k=16)
    p.wait("scalar", s_st, 16)
    p.run()
    return nc


def l1a_host_inputs(core_slot):
    inv = 1.0 / 10000.0 ** (np.arange(0, 32, 2, dtype=np.float32) / 32.0)
    t = np.arange(SEQ, dtype=np.float32)
    fr = np.outer(t, inv)
    cos, sin = np.cos(fr).astype(np.float32), np.sin(fr).astype(np.float32)
    own = np.concatenate([np.arange(b * BLK, (b + 1) * BLK) for b in BLOCKS[core_slot]])
    masks = np.zeros((4, 128, 512), dtype=np.float32)
    for j in range(4):
        for r in range(128):
            masks[j, r, :] = np.where(np.arange(512) >= r + 128 * j, 0.0, -1e30)
    masks = masks.astype(bfloat16)
    hg = core_slot
    qidx = np.zeros((128, 4, 4), dtype=np.uint32)
    for hi in range(4):
        for r in range(4):
            for pp in range(64):
                qidx[pp, hi, r] = r * 1024 + hg * 256 + hi * 64 + pp
    attnidx = np.zeros((128, 8), dtype=np.uint32)
    for dt in range(8):
        for pp in range(128):
            attnidx[pp, dt] = ((dt // 2) * 128 + pp) * 4 + hg
    return {
        "cos_own": cos[own], "sin_own": sin[own],
        "cos_all": cos, "sin_all": sin,
        "masks": masks, "qidx": qidx, "attnidx": attnidx,
    }


def build_l1a_launch(nheads=4, ngb=16, stage=9):
    nc = bass.Bass("TRN2", target_bir_lowering=False)
    xin = nc.dram_tensor("xin", [TPC, D], F32, kind="ExternalInput").ap()
    w_kvd = nc.dram_tensor("w_kvd", [1024, 128], F32, kind="ExternalInput").ap()
    w_q = nc.dram_tensor("w_q", [1024, 1024], F32, kind="ExternalInput").ap()
    w_qr = nc.dram_tensor("w_qr", [1024, 32], F32, kind="ExternalInput").ap()
    w_kr = nc.dram_tensor("w_kr", [128, 32], F32, kind="ExternalInput").ap()
    w_kup_loc = nc.dram_tensor("w_kup_loc", [128, 256], F32, kind="ExternalInput").ap()
    w_vup_loc = nc.dram_tensor("w_vup_loc", [128, 256], F32, kind="ExternalInput").ap()
    w_o = nc.dram_tensor("w_o", [1024, 1024], F32, kind="ExternalInput").ap()
    cos_own = nc.dram_tensor("cos_own", [TPC, 16], F32, kind="ExternalInput").ap()
    sin_own = nc.dram_tensor("sin_own", [TPC, 16], F32, kind="ExternalInput").ap()
    cos_all = nc.dram_tensor("cos_all", [SEQ, 16], F32, kind="ExternalInput").ap()
    sin_all = nc.dram_tensor("sin_all", [SEQ, 16], F32, kind="ExternalInput").ap()
    masks = nc.dram_tensor("masks", [4, 128, 512], BF16, kind="ExternalInput").ap()
    qidx = nc.dram_tensor("qidx", [128, 4, 4], U32, kind="ExternalInput").ap()
    attnidx = nc.dram_tensor("attnidx", [128, 8], U32, kind="ExternalInput").ap()
    cin = declare_consts(nc)
    xout = nc.dram_tensor("xout", [TPC, D], F32, kind="ExternalOutput").ap()

    p = Plan(nc)
    ar = Arena(nc, p._stack)
    psum = Psum(nc, p._stack)
    cons = load_consts(p, ar, cin)
    s_ld = p.sem("ldx")
    x_sb = ar.alloc([128, NT, 1024], F32)
    p.op("sync", lambda e: e.dma_start(
        out=x_sb, in_=xin.rearrange("(t p) d -> p t d", p=128)), sem=s_ld, k=16)
    for eng in ("tensor", "vector", "scalar"):
        p.wait(eng, s_ld, 16)
    s_done = emit_l1a(p, ar, psum, cons, x_sb, w_kvd, w_q, w_qr, w_kr,
                      w_kup_loc, w_vup_loc, w_o, cos_own, sin_own, cos_all,
                      sin_all, masks, qidx, attnidx, nheads=nheads, ngb=ngb, stage=stage)
    s_st = p.sem("st")
    p.wait("scalar", s_done)
    p.op("scalar", lambda e: e.dma_start(
        out=xout.rearrange("(t p) d -> p t d", p=128), in_=x_sb), sem=s_st, k=16)
    p.wait("scalar", s_st, 16)
    p.run()
    return nc


def build_l1f_launch(tbs=4, stage=9):
    nc = bass.Bass("TRN2", target_bir_lowering=False)
    xin = nc.dram_tensor("xin", [TPC, D], F32, kind="ExternalInput").ap()
    w_fc = nc.dram_tensor("w_fc", [4096, 1024], F32, kind="ExternalInput").ap()
    w_proj = nc.dram_tensor("w_proj", [1024, 4096], F32, kind="ExternalInput").ap()
    cin = declare_consts(nc)
    xout = nc.dram_tensor("xout", [TPC, D], F32, kind="ExternalOutput").ap()

    p = Plan(nc)
    ar = Arena(nc, p._stack)
    psum = Psum(nc, p._stack)
    cons = load_consts(p, ar, cin)
    s_ld = p.sem("ldx")
    x_sb = ar.alloc([128, NT, 1024], F32)
    p.op("sync", lambda e: e.dma_start(
        out=x_sb, in_=xin.rearrange("(t p) d -> p t d", p=128)), sem=s_ld, k=16)
    for eng in ("tensor", "vector", "scalar"):
        p.wait(eng, s_ld, 16)
    wp1 = emit_l1f_wprep(p, ar, psum, cons, w_fc, w_proj)
    s_done = emit_l1f(p, ar, psum, cons, x_sb, wp1, tbs=tbs, stage=stage)
    s_st = p.sem("st")
    p.wait("scalar", s_done)
    p.op("scalar", lambda e: e.dma_start(
        out=xout.rearrange("(t p) d -> p t d", p=128), in_=x_sb), sem=s_st, k=16)
    p.wait("scalar", s_st, 16)
    p.run()
    return nc


def build_l0f_launch(nexp=8, stage=9):
    nc = bass.Bass("TRN2", target_bir_lowering=False)
    xin = nc.dram_tensor("xin", [TPC, D], F32, kind="ExternalInput").ap()
    gate_w = nc.dram_tensor("gate_w", [8, 1024], F32, kind="ExternalInput").ap()
    wg_loc = nc.dram_tensor("wg_loc", [4096, 1024], F32, kind="ExternalInput").ap()
    wu_loc = nc.dram_tensor("wu_loc", [4096, 1024], F32, kind="ExternalInput").ap()
    wd_loc = nc.dram_tensor("wd_loc", [1024, 4096], F32, kind="ExternalInput").ap()
    m32 = nc.dram_tensor("m32", [32, 32], F32, kind="ExternalInput").ap()
    erow = nc.dram_tensor("erow", [8, 1], F32, kind="ExternalInput").ap()
    idrow = nc.dram_tensor("idrow", [128, NT, 1], I16, kind="ExternalInput").ap()
    trasht = nc.dram_tensor("trasht", [128, 5, 1], F32, kind="ExternalInput").ap()
    cin = declare_consts(nc)
    xout = nc.dram_tensor("xout", [TPC, D], F32, kind="ExternalOutput").ap()

    p = Plan(nc)
    ar = Arena(nc, p._stack)
    psum = Psum(nc, p._stack)
    cons = load_consts(p, ar, cin)
    s_ld = p.sem("ldx")
    x_sb = ar.alloc([128, NT, 1024], F32)
    p.op("sync", lambda e: e.dma_start(
        out=x_sb, in_=xin.rearrange("(t p) d -> p t d", p=128)), sem=s_ld, k=16)
    for eng in ("tensor", "vector", "scalar"):
        p.wait(eng, s_ld, 16)
    wp0 = emit_l0f_wprep(p, ar, psum, cons, wg_loc, wu_loc, wd_loc)
    s_done = emit_l0f(p, ar, psum, cons, x_sb, gate_w, wp0,
                      m32, erow, idrow, trasht, nexp=nexp, stage=stage)
    s_st = p.sem("st")
    p.wait("scalar", s_done)
    p.op("scalar", lambda e: e.dma_start(
        out=xout.rearrange("(t p) d -> p t d", p=128), in_=x_sb), sem=s_st, k=16)
    p.wait("scalar", s_st, 16)
    p.run()
    return nc


def build_fused_launch(stage=9):
    nc = bass.Bass("TRN2", target_bir_lowering=False)
    xin = nc.dram_tensor("xin", [TPC, D], F32, kind="ExternalInput").ap()
    wq = nc.dram_tensor("wq", [1024, 1024], F32, kind="ExternalInput").ap()
    wk = nc.dram_tensor("wk", [1024, 1024], F32, kind="ExternalInput").ap()
    wv = nc.dram_tensor("wv", [1024, 1024], F32, kind="ExternalInput").ap()
    gate_w = nc.dram_tensor("gate_w", [8, 1024], F32, kind="ExternalInput").ap()
    wg_loc = nc.dram_tensor("wg_loc", [4096, 1024], F32, kind="ExternalInput").ap()
    wu_loc = nc.dram_tensor("wu_loc", [4096, 1024], F32, kind="ExternalInput").ap()
    wd_loc = nc.dram_tensor("wd_loc", [1024, 4096], F32, kind="ExternalInput").ap()
    m32 = nc.dram_tensor("m32", [32, 32], F32, kind="ExternalInput").ap()
    erow = nc.dram_tensor("erow", [8, 1], F32, kind="ExternalInput").ap()
    idrow = nc.dram_tensor("idrow", [128, NT, 1], I16, kind="ExternalInput").ap()
    trasht = nc.dram_tensor("trasht", [128, 5, 1], F32, kind="ExternalInput").ap()
    w_kvd = nc.dram_tensor("w_kvd", [1024, 128], F32, kind="ExternalInput").ap()
    w_q = nc.dram_tensor("w_q", [1024, 1024], F32, kind="ExternalInput").ap()
    w_qr = nc.dram_tensor("w_qr", [1024, 32], F32, kind="ExternalInput").ap()
    w_kr = nc.dram_tensor("w_kr", [128, 32], F32, kind="ExternalInput").ap()
    w_kup_loc = nc.dram_tensor("w_kup_loc", [128, 256], F32, kind="ExternalInput").ap()
    w_vup_loc = nc.dram_tensor("w_vup_loc", [128, 256], F32, kind="ExternalInput").ap()
    w_o = nc.dram_tensor("w_o", [1024, 1024], F32, kind="ExternalInput").ap()
    cos_own = nc.dram_tensor("cos_own", [TPC, 16], F32, kind="ExternalInput").ap()
    sin_own = nc.dram_tensor("sin_own", [TPC, 16], F32, kind="ExternalInput").ap()
    cos_all = nc.dram_tensor("cos_all", [SEQ, 16], F32, kind="ExternalInput").ap()
    sin_all = nc.dram_tensor("sin_all", [SEQ, 16], F32, kind="ExternalInput").ap()
    masks = nc.dram_tensor("masks", [4, 128, 512], BF16, kind="ExternalInput").ap()
    qidx = nc.dram_tensor("qidx", [128, 4, 4], U32, kind="ExternalInput").ap()
    attnidx = nc.dram_tensor("attnidx", [128, 8], U32, kind="ExternalInput").ap()
    w_fc = nc.dram_tensor("w_fc", [4096, 1024], F32, kind="ExternalInput").ap()
    w_proj = nc.dram_tensor("w_proj", [1024, 4096], F32, kind="ExternalInput").ap()
    cin = declare_consts(nc)
    xout = nc.dram_tensor("xout", [TPC, D], F32, kind="ExternalOutput").ap()

    p = Plan(nc)
    ar = Arena(nc, p._stack)
    psum = Psum(nc, p._stack)
    cons = load_consts(p, ar, cin)
    s_ld = p.sem("ldx")
    x_sb = ar.alloc([128, NT, 1024], F32)
    p.op("sync", lambda e: e.dma_start(
        out=x_sb, in_=xin.rearrange("(t p) d -> p t d", p=128)), sem=s_ld, k=16)
    for eng in ("tensor", "vector", "scalar", "gpsimd"):
        p.wait(eng, s_ld, 16)

    wps = {}

    def prelude():
        wps["wp0"] = emit_l0f_wprep(p, ar, psum, cons, wg_loc, wu_loc, wd_loc)
        wps["wp1"] = emit_l1f_wprep(p, ar, psum, cons, w_fc, w_proj)
        return wps["wp0"]["gates"] + wps["wp1"]["gates"]

    ar.mark()
    s_done = emit_l0a(p, ar, psum, cons, x_sb, wq, wk, wv, stage=stage,
                      prelude=prelude)
    wp0, wp1 = wps["wp0"], wps["wp1"]
    for eng in ENGS:
        p.wait(eng, s_done)
    ar.release()
    ar.mark()
    s_done = emit_l0f(p, ar, psum, cons, x_sb, gate_w, wp0,
                      m32, erow, idrow, trasht, stage=stage)
    for eng in ENGS:
        p.wait(eng, s_done)
    ar.release()
    ar.mark()
    s_done = emit_l1a(p, ar, psum, cons, x_sb, w_kvd, w_q, w_qr, w_kr,
                      w_kup_loc, w_vup_loc, w_o, cos_own, sin_own, cos_all,
                      sin_all, masks, qidx, attnidx, stage=stage)
    for eng in ENGS:
        p.wait(eng, s_done)
    ar.release()
    ar.mark()
    s_done = emit_l1f(p, ar, psum, cons, x_sb, wp1, stage=stage)
    ar.release()
    s_st = p.sem("st")
    p.wait("scalar", s_done)
    p.op("scalar", lambda e: e.dma_start(
        out=xout.rearrange("(t p) d -> p t d", p=128), in_=x_sb), sem=s_st, k=16)
    p.wait("scalar", s_st, 16)
    p.run()
    return nc


FUSED = True

_NC_CACHE = {}


def _get_nc(name, builder):
    if name not in _NC_CACHE:
        _NC_CACHE[name] = builder()
    return _NC_CACHE[name]


def kernel(**inputs):
    f32 = np.float32
    x = np.ascontiguousarray(np.asarray(inputs["L0_attn_x"], dtype=f32))
    cin = const_inputs()
    shards = shard_rows(x)

    if FUSED:
        nc = _get_nc("fused", build_fused_launch)
        wg = np.asarray(inputs["L0_ffn_w_gate"], f32)
        wu = np.asarray(inputs["L0_ffn_w_up"], f32)
        wd = np.asarray(inputs["L0_ffn_w_down"], f32)
        wkup = np.asarray(inputs["L1_attn_W_k_up"], f32)
        wvup = np.asarray(inputs["L1_attn_W_v_up"], f32)
        ca = np.ascontiguousarray
        in_maps = []
        for c in range(NCORES):
            hg = c % 4
            in_maps.append({
                "xin": shards[c],
                "wq": ca(np.asarray(inputs["L0_attn_Wq"], f32)),
                "wk": ca(np.asarray(inputs["L0_attn_Wk"], f32)),
                "wv": ca(np.asarray(inputs["L0_attn_Wv"], f32)),
                "gate_w": ca(np.asarray(inputs["L0_ffn_gate_w"], f32)),
                "wg_loc": ca(wg[c]), "wu_loc": ca(wu[c]), "wd_loc": ca(wd[c]),
                "w_kvd": ca(np.asarray(inputs["L1_attn_W_kv_down"], f32)),
                "w_q": ca(np.asarray(inputs["L1_attn_W_q"], f32)),
                "w_qr": ca(np.asarray(inputs["L1_attn_W_q_rope"], f32)),
                "w_kr": ca(np.asarray(inputs["L1_attn_W_k_rope"], f32)),
                "w_kup_loc": ca(wkup[:, hg * 256:(hg + 1) * 256]),
                "w_vup_loc": ca(wvup[:, hg * 256:(hg + 1) * 256]),
                "w_o": ca(np.asarray(inputs["L1_attn_W_o"], f32)),
                "w_fc": ca(np.asarray(inputs["L1_ffn_W_fc"], f32)),
                "w_proj": ca(np.asarray(inputs["L1_ffn_W_proj"], f32)),
                **l0f_host_inputs(c), **l1a_host_inputs(hg), **cin})
        res = run_bass_kernel_spmd(nc, in_maps, core_ids=list(range(NCORES)))
        return unshard_rows([res.results[c]["xout"] for c in range(NCORES)]
                            ).astype(np.float32)

    nc = _get_nc("l0a", build_l0a_launch)
    in_maps = [{"xin": shards[c],
                "wq": np.ascontiguousarray(np.asarray(inputs["L0_attn_Wq"], f32)),
                "wk": np.ascontiguousarray(np.asarray(inputs["L0_attn_Wk"], f32)),
                "wv": np.ascontiguousarray(np.asarray(inputs["L0_attn_Wv"], f32)),
                **cin} for c in range(NCORES)]
    res = run_bass_kernel_spmd(nc, in_maps, core_ids=list(range(NCORES)))
    shards = [np.ascontiguousarray(res.results[c]["xout"]) for c in range(NCORES)]

    nc = _get_nc("l0f", build_l0f_launch)
    wg = np.asarray(inputs["L0_ffn_w_gate"], f32)
    wu = np.asarray(inputs["L0_ffn_w_up"], f32)
    wd = np.asarray(inputs["L0_ffn_w_down"], f32)
    in_maps = [{"xin": shards[c],
                "gate_w": np.ascontiguousarray(np.asarray(inputs["L0_ffn_gate_w"], f32)),
                "wg_loc": np.ascontiguousarray(wg[c]),
                "wu_loc": np.ascontiguousarray(wu[c]),
                "wd_loc": np.ascontiguousarray(wd[c]),
                **l0f_host_inputs(c), **cin} for c in range(NCORES)]
    res = run_bass_kernel_spmd(nc, in_maps, core_ids=list(range(NCORES)))
    shards = [np.ascontiguousarray(res.results[c]["xout"]) for c in range(NCORES)]

    nc = _get_nc("l1a", build_l1a_launch)
    wkup = np.asarray(inputs["L1_attn_W_k_up"], f32)
    wvup = np.asarray(inputs["L1_attn_W_v_up"], f32)
    in_maps = []
    for c in range(NCORES):
        hg = c % 4
        in_maps.append({
            "xin": shards[c],
            "w_kvd": np.ascontiguousarray(np.asarray(inputs["L1_attn_W_kv_down"], f32)),
            "w_q": np.ascontiguousarray(np.asarray(inputs["L1_attn_W_q"], f32)),
            "w_qr": np.ascontiguousarray(np.asarray(inputs["L1_attn_W_q_rope"], f32)),
            "w_kr": np.ascontiguousarray(np.asarray(inputs["L1_attn_W_k_rope"], f32)),
            "w_kup_loc": np.ascontiguousarray(wkup[:, hg * 256:(hg + 1) * 256]),
            "w_vup_loc": np.ascontiguousarray(wvup[:, hg * 256:(hg + 1) * 256]),
            "w_o": np.ascontiguousarray(np.asarray(inputs["L1_attn_W_o"], f32)),
            **l1a_host_inputs(hg), **cin})
    res = run_bass_kernel_spmd(nc, in_maps, core_ids=list(range(NCORES)))
    shards = [np.ascontiguousarray(res.results[c]["xout"]) for c in range(NCORES)]

    nc = _get_nc("l1f", build_l1f_launch)
    in_maps = [{"xin": shards[c],
                "w_fc": np.ascontiguousarray(np.asarray(inputs["L1_ffn_W_fc"], f32)),
                "w_proj": np.ascontiguousarray(np.asarray(inputs["L1_ffn_W_proj"], f32)),
                **cin} for c in range(NCORES)]
    res = run_bass_kernel_spmd(nc, in_maps, core_ids=list(range(NCORES)))
    shards = [res.results[c]["xout"] for c in range(NCORES)]

    return unshard_rows(shards).astype(np.float32)


# version 19 (final)
# speedup vs baseline: 1.0439x; 1.0083x over previous
import contextlib

import numpy as np
import ml_dtypes

import concourse.bass as bass
import concourse.mybir as mybir
from concourse.bass_utils import run_bass_kernel_spmd
from concourse.bass import IndirectOffsetOnAxis, ds

F32 = mybir.dt.float32
BF16 = mybir.dt.bfloat16
U32 = mybir.dt.uint32
I32 = mybir.dt.int32
I16 = mybir.dt.int16
U8 = mybir.dt.uint8
AF = mybir.ActivationFunctionType
ALU = mybir.AluOpType
bfloat16 = ml_dtypes.bfloat16

NCORES = 8
D = 1024
SEQ = 8192
B = 2
BLK = 512
TPC = 2048
NT = TPC // 128
BLOCKS = [sorted(s) for s in ([0, 15, 7, 8], [1, 14, 6, 9], [2, 13, 5, 10], [3, 12, 4, 11])]
LN_EPS = 1e-5

ENGS = ("sync", "tensor", "vector", "scalar", "gpsimd")



class Plan:

    def __init__(self, nc):
        self.nc = nc
        self.ops = {e: [] for e in ENGS}
        self.sem_handles = {}
        self.cnt = {}
        self._stack = contextlib.ExitStack()

    def sem(self, name):
        assert name not in self.sem_handles
        self.sem_handles[name] = self._stack.enter_context(self.nc.semaphore(name))
        self.cnt[name] = 0
        return name

    def emit(self, eng, fn):
        self.ops[eng].append(fn)

    def op(self, eng, make, sem=None, k=1):
        h = self.sem_handles[sem] if sem else None
        if sem:
            self.cnt[sem] += k

        def fn(e):
            i = make(e)
            if h is not None:
                i.then_inc(h, k)
        self.emit(eng, fn)
        if sem:
            return self.cnt[sem]

    def wait(self, eng, sem, value=None):
        v = self.cnt[sem] if value is None else value
        if v <= 0:
            return
        h = self.sem_handles[sem]
        self.emit(eng, lambda e: e.wait_ge(h, v))

    def val(self, sem):
        return self.cnt[sem]

    def barrier(self, sem_suffix, producers_done):
        for eng in ENGS:
            for sem, v in producers_done:
                self.wait(eng, sem, v)

    def run(self):
        with self._stack:
            with self.nc.Block() as block:
                for name in ENGS:
                    def body(eng, _n=name):
                        for fn in self.ops[_n]:
                            fn(eng)
                    getattr(block, name)(body)


class Arena:

    def __init__(self, nc, stack, nbytes=206 * 1024, name="arena"):
        self.t = stack.enter_context(nc.sbuf_tensor(name, [128, nbytes], U8))
        self.off = 0
        self.nbytes = nbytes
        self.marks = []
        self.peak = 0

    def alloc(self, shape, dtype, align=128):
        assert shape[0] <= 128
        esz = {F32: 4, BF16: 2, U32: 4, I32: 4, I16: 2, U8: 1}[dtype]
        n = 1
        for s in shape[1:]:
            n *= s
        self.off = (self.off + align - 1) // align * align
        start = self.off
        self.off += n * esz
        self.peak = max(self.peak, self.off)
        limit = self.nbytes - getattr(self, "reserved", 0)
        assert self.off <= limit, f"arena overflow: {self.off} > {limit}"
        v = self.t[0:shape[0], start:start + n * esz]
        if dtype != U8:
            v = v.bitcast(dtype)
        if len(shape) > 2:
            dims = " ".join(f"d{i}" for i in range(1, len(shape)))
            v = v.rearrange(f"p ({dims}) -> p {dims}",
                            **{f"d{i}": shape[i] for i in range(1, len(shape) - 1)})
        return v

    def mark(self):
        self.marks.append(self.off)

    def release(self):
        self.off = self.marks.pop()


class ArenaTop:

    def __init__(self, ar, size):
        ar.reserved = size
        self.par = ar
        self.base = ar.nbytes - size
        self.off = self.base
        self.peak = self.base
        self.nbytes = ar.nbytes
        self.t = ar.t
        self.marks = []

    alloc = Arena.alloc
    mark = Arena.mark
    release = Arena.release



def shard_rows(x_full):
    shards = []
    for c in range(NCORES):
        b = c // 4
        rows = [x_full[b, blk * BLK:(blk + 1) * BLK] for blk in BLOCKS[c % 4]]
        shards.append(np.ascontiguousarray(np.concatenate(rows, axis=0)))
    return shards


def unshard_rows(shards):
    out = np.zeros((B, SEQ, D), dtype=shards[0].dtype)
    for c in range(NCORES):
        b = c // 4
        for i, blk in enumerate(BLOCKS[c % 4]):
            out[b, blk * BLK:(blk + 1) * BLK] = shards[c][i * BLK:(i + 1) * BLK]
    return out


def const_inputs():
    return {
        "ident_bf": np.eye(128, dtype=bfloat16),
        "ident_f32": np.eye(128, dtype=np.float32),
        "ones_f32": np.ones((1, 128), dtype=np.float32),
    }


CONST_SPECS = [("ident_bf", [128, 128], BF16),
               ("ident_f32", [128, 128], F32),
               ("ones_f32", [1, 128], F32)]


def declare_consts(nc):
    return {name: nc.dram_tensor(name, shape, dt, kind="ExternalInput").ap()
            for name, shape, dt in CONST_SPECS}


def load_consts(p, ar, cin):
    out = {}
    s = p.sem("ldc")
    for name, shape, dt in CONST_SPECS:
        out[name] = ar.alloc(shape, dt)
        p.op("sync", lambda e, d=out[name], src=cin[name]: e.dma_start(out=d, in_=src),
             sem=s, k=16)
    out["eps128"] = ar.alloc([128, 1], F32)
    p.op("vector", lambda e: e.memset(out["eps128"], LN_EPS), sem=s, k=1)
    out["onesc_f32"] = ar.alloc([128, 1], F32)
    p.op("vector", lambda e: e.memset(out["onesc_f32"], 1.0), sem=s, k=1)
    for eng in ("tensor", "vector", "scalar", "gpsimd"):
        p.wait(eng, s)
    return out



def emit_transposes_bf(p, psum, cons, src, dst_fn, rtiles, ctiles, s_tmm, s_tev,
                       psum_byte0=2048 * 4):
    nslot = 4
    for rt in range(rtiles):
        for ct in range(ctiles):
            slot = p.val(s_tmm) % nslot
            ps = psum.t_u8[:, psum_byte0 + slot * 2048: psum_byte0 + slot * 2048 + 256].bitcast(BF16)
            if p.val(s_tmm) >= nslot:
                p.wait("tensor", s_tev, p.val(s_tev) - nslot + 1)
            p.op("tensor", lambda e, ps=ps, rt=rt, ct=ct: e.transpose(
                ps, src[:, rt, ct * 128:(ct + 1) * 128], cons["ident_bf"]), sem=s_tmm)
            p.wait("vector", s_tmm)
            p.op("vector", lambda e, ps=ps, d=dst_fn(rt, ct): e.tensor_copy(d, ps),
                 sem=s_tev)


class Psum:

    def __init__(self, nc, stack):
        self.h = stack.enter_context(nc.psum_tensor("ps", [128, 4096], F32))
        self.f32 = self.h.ap()
        self.t_u8 = self.f32.bitcast(U8)

    def f(self, col0, ncols):
        return self.f32[:, col0:col0 + ncols]


def emit_prenorm_tile(p, ar, x_t, scr, scrb, stats, mr, s_ev, cons=None):
    p.op("vector", lambda e: e.bn_stats(stats[:, 0:6], x_t[:, 0:512]), sem=s_ev)
    p.op("vector", lambda e: e.bn_stats(stats[:, 6:12], x_t[:, 512:1024]), sem=s_ev)
    p.wait("vector", s_ev)
    p.op("vector", lambda e: e.bn_aggr(mr[:, 0:2], stats[:, 0:12]), sem=s_ev)
    p.wait("scalar", s_ev)
    p.op("scalar", lambda e: e.activation(
        stats[:, 12:13], mr[:, 1:2], AF.Sqrt, bias=cons["eps128"], scale=1.0), sem=s_ev)
    p.wait("vector", s_ev)
    p.op("vector", lambda e: e.reciprocal(stats[:, 13:14], stats[:, 12:13]), sem=s_ev)
    p.wait("vector", s_ev)
    p.op("vector", lambda e: e.tensor_scalar(
        scr, x_t, mr[:, 0:1], stats[:, 13:14], ALU.subtract, ALU.mult), sem=s_ev)
    p.wait("scalar", s_ev)
    p.op("scalar", lambda e: e.copy(scrb, scr), sem=s_ev)



def emit_wT_to_hbm(p, ar, psum, cons, w, R, C, hbm, group, sems, psum_byte0=8192,
                   load_eng="gpsimd", store_eng="scalar", half_rt_cap=None,
                   nraw=2):
    if len(sems) == 5:
        s_w, s_tmm, s_tev, s_wst, s_wst2 = sems
        wsts = [s_wst, s_wst2]
    else:
        s_w, s_tmm, s_tev, s_wst = sems
        wsts = [s_wst, s_wst]
    RT, CT = R // 128, C // 128
    half_rt = max(1, min(RT, 16384 // C))
    if half_rt_cap is not None:
        half_rt = min(half_rt, half_rt_cap)
    nhalf = (RT + half_rt - 1) // half_rt
    nb = min(nraw, nhalf)
    ar.mark()
    raw = [ar.alloc([128, half_rt, C], BF16) for _ in range(nb)]
    chunk = [ar.alloc([128, CT * 128 if group == "r" else half_rt * 128], BF16)
             for _ in range(2)]
    nch = 0
    tmm_after_half = {}
    w_t = w.rearrange("(t p) c -> p t c", p=128)
    halves = list(range(0, RT, half_rt))
    s_w_base = p.val(s_w)
    for hidx in range(nb):
        h0 = halves[hidx]
        h1 = min(h0 + half_rt, RT)
        p.op(load_eng, lambda e, h0=h0, h1=h1, r=raw[hidx % nb]: e.dma_start(
            out=r[:, 0:h1 - h0, :], in_=w_t[:, h0:h1, :]), sem=s_w, k=16)
    for hidx, h0 in enumerate(halves):
        h1 = min(h0 + half_rt, RT)
        rbuf = raw[hidx % nb]
        p.wait("tensor", s_w, s_w_base + 16 * (hidx + 1))
        outer = range(h0, h1) if group == "r" else range(CT)
        inner = range(CT) if group == "r" else range(h0, h1)
        for g in outer:
            cb = chunk[nch % 2]
            sw = wsts[nch % 2]
            if nch >= 2:
                p.wait("vector", sw, p.val(sw))
            nch += 1
            for i_, it in enumerate(inner):
                rt, ct = (g, it) if group == "r" else (it, g)
                slot = p.val(s_tmm) % 4
                ps = psum.t_u8[:, psum_byte0 + slot * 2048: psum_byte0 + slot * 2048 + 256].bitcast(BF16)
                if p.val(s_tmm) >= 4:
                    p.wait("tensor", s_tev, p.val(s_tev) - 3)
                p.op("tensor", lambda e, ps=ps, rt=rt, ct=ct, h0=h0, rbuf=rbuf: e.transpose(
                    ps, rbuf[:, rt - h0, ct * 128:(ct + 1) * 128], cons["ident_bf"]),
                    sem=s_tmm)
                p.wait("vector", s_tmm)
                p.op("vector", lambda e, ps=ps, cb=cb, i_=i_: e.tensor_copy(
                    cb[:, i_ * 128:(i_ + 1) * 128], ps), sem=s_tev)
            p.wait(store_eng, s_tev)
            if group == "r":
                dst = hbm[g]
            else:
                dst = hbm[g][:, h0 * 128:h1 * 128]
            p.wait(store_eng, sw, p.val(sw))
            p.op(store_eng, lambda e, cb=cb, dst=dst, n=(len(list(inner))) * 128:
                 e.dma_start(out=dst, in_=cb[:, 0:n]), sem=sw, k=16)
        tmm_after_half[hidx] = p.val(s_tmm)
        if hidx + nb < nhalf:
            nh0 = halves[hidx + nb]
            nh1 = min(nh0 + half_rt, RT)
            p.wait(load_eng, s_tmm, tmm_after_half[hidx])
            p.op(load_eng, lambda e, nh0=nh0, nh1=nh1, rbuf=rbuf: e.dma_start(
                out=rbuf[:, 0:nh1 - nh0, :], in_=w_t[:, nh0:nh1, :]), sem=s_w, k=16)
    for sw in set(wsts):
        p.wait(load_eng, sw)
        p.wait("vector", sw)
        p.wait(store_eng, sw)
    ar.release()


def emit_l1f(p, ar, psum, cons, xin_sb, wp, pref="f1", tbs=4, stage=9):
    nc = p.nc
    s_tmm = p.sem(pref + "tmm")
    s_tev = p.sem(pref + "tev")
    s_mm = p.sem(pref + "mm")
    s_ev = p.sem(pref + "ev")
    s_y = p.sem(pref + "y")
    s_lif = p.sem(pref + "lif")
    s_ws = [p.sem(pref + "ws0"), p.sem(pref + "ws1")]

    wfcT_hbm = wp["wfcT"]
    wprT_hbm = wp["wprT"]
    wfcb = [ar.alloc([128, 8, 128], BF16) for _ in range(2)]
    wprb = [ar.alloc([128, 1024], BF16) for _ in range(2)]
    p.wait("sync", wp["s_wst"])
    p.wait("sync", wp["s_wst2"])
    nstep = 5
    for tb in range(tbs):
        ar.mark()
        xln_fm = ar.alloc([128, 8, 512], BF16)
        h2 = ar.alloc([128, 32, 512], BF16)
        scr = ar.alloc([128, 1024], F32)
        scrb = ar.alloc([128, 1024], BF16)
        stats = ar.alloc([128, 16], F32)
        mr = ar.alloc([128, 2], F32)
        ysb = ar.alloc([128, 1024], F32)
        memt = ar.alloc([128, 1024], F32)
        acct = ar.alloc([128, 1024], F32)
        spkt = ar.alloc([128, 1024], F32)

        for ti in range(4):
            t = tb * 4 + ti
            if ti > 0 or tb > 0:
                p.wait("scalar", s_tmm)
            emit_prenorm_tile(p, ar, xin_sb[:, t, :], scr, scrb, stats, mr, s_ev, cons)
            p.wait("tensor", s_ev)
            if tb > 0 and ti == 0:
                p.wait("tensor", s_y, p.val(s_y))
            for dt in range(8):
                slot = dt % 2
                ps = psum.t_u8[:, 12288 + slot * 2048: 12288 + slot * 2048 + 256].bitcast(BF16)
                if ti * 8 + dt >= 2:
                    p.wait("tensor", s_tev, p.val(s_tev) - 1)
                p.op("tensor", lambda e, ps=ps, dt=dt: e.transpose(
                    ps, scrb[:, dt * 128:(dt + 1) * 128], cons["ident_bf"]), sem=s_tmm)
                p.wait("vector", s_tmm)
                p.op("vector", lambda e, ps=ps, dt=dt, ti=ti: e.tensor_copy(
                    xln_fm[:, dt, ti * 128:(ti + 1) * 128], ps), sem=s_tev)

        p.wait("tensor", s_tev)
        p.wait("sync", s_mm)
        if stage < 2:
            ar.release()
            continue
        for ft in range(32):
            wb = wfcb[ft % 2]
            if ft >= 2:
                p.wait("sync", s_mm, p.val(s_mm) - 1)
            p.op("sync", lambda e, wb=wb, ft=ft: e.dma_start(
                out=wb, in_=wfcT_hbm[ft].rearrange("p (t c) -> p t c", t=8)),
                sem=s_ws[ft % 2], k=16)
            p.wait("tensor", s_ws[ft % 2])
            ps = psum.f((ft % 2) * 512, 512)
            if ft >= 2:
                p.wait("tensor", s_ev, p.val(s_ev) - 3)
            for dt in range(8):
                p.op("tensor", lambda e, ps=ps, wb=wb, dt=dt: e.matmul(
                    ps, wb[:, dt, :], xln_fm[:, dt, :],
                    start=(dt == 0), stop=(dt == 7)),
                    sem=(s_mm if dt == 7 else None))
            p.wait("vector", s_mm)
            p.op("vector", lambda e, ps=ps, ft=ft: e.tensor_scalar(
                h2[:, ft, :], ps, 0.0, None, ALU.max, ALU.bypass), sem=s_ev)
            p.wait("vector", s_ev)
            p.op("vector", lambda e, ft=ft: e.tensor_tensor(
                h2[:, ft, :], h2[:, ft, :], h2[:, ft, :], ALU.mult), sem=s_ev)

        if stage < 3:
            ar.release()
            continue
        p.wait("tensor", s_ev)
        p.wait("sync", s_mm)
        for ft in range(32):
            wb = wprb[ft % 2]
            if ft >= 2:
                p.wait("sync", s_mm, p.val(s_mm) - 8)
            p.op("sync", lambda e, wb=wb, ft=ft: e.dma_start(
                out=wb, in_=wprT_hbm[ft]), sem=s_ws[ft % 2], k=16)
            p.wait("tensor", s_ws[ft % 2])
            for ti in range(4):
                for nchk in range(2):
                    ps = psum.f((ti * 2 + nchk) * 512, 512)
                    p.op("tensor", lambda e, ps=ps, wb=wb, ft=ft, ti=ti, nchk=nchk: e.matmul(
                        ps, h2[:, ft, ti * 128:(ti + 1) * 128],
                        wb[:, nchk * 512:(nchk + 1) * 512],
                        start=(ft == 0), stop=(ft == 31)),
                        sem=s_mm)
        if stage < 4:
            ar.release()
            continue
        for ti in range(4):
            t = tb * 4 + ti
            p.wait("scalar", s_mm)
            p.wait("scalar", s_lif)
            for nchk in range(2):
                ps = psum.f((ti * 2 + nchk) * 512, 512)
                p.op("scalar", lambda e, ps=ps, nchk=nchk: e.copy(
                    ysb[:, nchk * 512:(nchk + 1) * 512], ps), sem=s_y)
            p.wait("vector", s_y)
            p.op("vector", lambda e: e.tensor_scalar(
                acct, ysb, 1.0, None, ALU.is_gt, ALU.bypass), sem=s_lif)
            p.wait("vector", s_lif)
            p.op("vector", lambda e: e.tensor_copy(spkt, acct), sem=s_lif)
            p.op("vector", lambda e: e.tensor_copy(memt, ysb), sem=s_lif)
            for st in range(1, nstep):
                p.wait("vector", s_lif)
                p.op("vector", lambda e: e.scalar_tensor_tensor(
                    memt, memt, 0.9, ysb, ALU.mult, ALU.add), sem=s_lif)
                p.wait("vector", s_lif)
                p.op("vector", lambda e: e.tensor_tensor(
                    memt, memt, spkt, ALU.subtract), sem=s_lif)
                p.wait("vector", s_lif)
                p.op("vector", lambda e: e.tensor_scalar(
                    spkt, memt, 1.0, None, ALU.is_gt, ALU.bypass), sem=s_lif)
                p.wait("vector", s_lif)
                p.op("vector", lambda e: e.tensor_tensor(
                    acct, acct, spkt, ALU.add), sem=s_lif)
            p.wait("vector", s_lif)
            p.op("vector", lambda e, t=t: e.scalar_tensor_tensor(
                xin_sb[:, t, :], acct, 0.2, xin_sb[:, t, :], ALU.mult, ALU.add),
                sem=s_lif)
        ar.release()
    return s_lif



def emit_l0a(p, ar, psum, cons, x_sb, wq, wk, wv, pref="a0", nchunks=4, stage=9,
             prelude=None):
    nc = p.nc
    s_w = p.sem(pref + "w")
    s_tmm = p.sem(pref + "tmm")
    s_tev = p.sem(pref + "tev")
    s_pn = p.sem(pref + "pn")
    s_qkv = p.sem(pref + "qkv")
    s_qev = p.sem(pref + "qev")
    s_sc = p.sem(pref + "sc")
    s_ex = p.sem(pref + "ex")
    s_av = p.sem(pref + "av")
    s_oev = p.sem(pref + "oev")
    s_oc = p.sem(pref + "oc")
    s_ot = p.sem(pref + "ot")
    s_ov = p.sem(pref + "ov")
    s_nrm = p.sem(pref + "nrm")
    s_yst = p.sem(pref + "yst")
    s_cc = p.sem(pref + "cc")
    s_fin = p.sem(pref + "fin")

    y_hbm = nc.dram_tensor(pref + "_y", [TPC, D], F32).ap()
    cc_in = nc.dram_tensor(pref + "_ccin", [1, 16], F32)
    cc_out = nc.dram_tensor(pref + "_ccout", [1, 16], F32)
    groups = [[0, 1, 2, 3], [4, 5, 6, 7]]

    wT = {}
    for name in ("q", "k", "v"):
        wT[name] = ar.alloc([128, 8, 1024], BF16)
    ar.mark()
    raw = ar.alloc([128, 8, 1024], BF16)
    for name, w in (("q", wq), ("k", wk), ("v", wv)):
        if name != "q":
            p.wait("gpsimd", s_tmm)
        p.op("gpsimd", lambda e, w=w: e.dma_start(
            out=raw, in_=w.rearrange("(t p) c -> p t c", p=128)), sem=s_w, k=16)
        p.wait("tensor", s_w)
        emit_transposes_bf(p, psum, cons, raw,
                           lambda rt, ct, name=name: wT[name][:, ct, rt * 128:(rt + 1) * 128],
                           8, 8, s_tmm, s_tev)
    p.wait("gpsimd", s_tmm)
    ar.release()
    if prelude is not None:
        p.wait("tensor", s_tev)
        gates = prelude()
        p.barrier("wp", gates)

    bnstash = ar.alloc([128, 192], F32)
    mv2 = ar.alloc([128, 2], F32)
    msq = ar.alloc([128, 1], F32)
    red_sb = ar.alloc([1, 16], F32)
    bcast_sb = ar.alloc([128, 2], F32)
    onescol_bf = ar.alloc([128, 1], BF16)
    p.op("vector", lambda e: e.memset(onescol_bf, 1.0), sem=s_pn)
    p.op("vector", lambda e: e.memset(red_sb, 0.0), sem=s_pn)

    kbs = [list(range(i, min(i + 3, 64))) for i in range(0, 64, 3)]
    ptb_last_read = {0: 0, 1: 0}
    exp_read = {0: 0, 1: 0}

    for c in range(nchunks):
        ar.mark()
        xln_fm = ar.alloc([128, 8, 512], BF16)
        qc = ar.alloc([128, 8192], BF16)
        kc = ar.alloc([128, 8192], BF16)
        vbuf = ar.alloc([128, 4, 16, 96], BF16)
        ptb = [ar.alloc([128, 1536], BF16) for _ in range(2)]
        stag = ar.alloc([128, 512], F32)
        ych = ar.alloc([128, 4, 1024], F32)
        scr = ar.alloc([128, 1024], F32)
        scrb = ar.alloc([128, 1024], BF16)
        stats = ar.alloc([128, 16], F32)
        mr = ar.alloc([128, 2], F32)
        mrstash = ar.alloc([128, 8], F32)
        sq = ar.alloc([64, 512], BF16)
        umax = ar.alloc([1, 8], F32)
        negu = ar.alloc([128, 1], F32)
        rc = ar.alloc([128, 1], F32)

        p.wait("tensor", s_ex)
        p.wait("tensor", s_oc)
        p.wait("vector", s_qkv)
        p.wait("vector", s_sc)
        p.wait("vector", s_av)
        p.wait("vector", s_ot)
        p.wait("vector", s_ov)
        p.wait("scalar", s_av)
        p.op("vector", lambda e: e.memset(stag, 0.0), sem=s_pn)
        p.op("vector", lambda e: e.memset(vbuf, 1.0), sem=s_pn)
        p.op("vector", lambda e: e.memset(qc[64:128, :], 0.0), sem=s_pn)
        p.op("vector", lambda e: e.memset(kc[64:128, :], 0.0), sem=s_pn)

        for ti in range(4):
            t = c * 4 + ti
            if t > 0:
                p.wait("scalar", s_tmm)
            emit_prenorm_tile(p, ar, x_sb[:, t, :], scr, scrb, stats, mr, s_pn, cons)
            p.wait("vector", s_pn)
            p.op("vector", lambda e, ti=ti: e.tensor_copy(
                mrstash[:, 2 * ti:2 * ti + 1], mr[:, 0:1]), sem=s_pn)
            p.op("vector", lambda e, ti=ti: e.tensor_copy(
                mrstash[:, 2 * ti + 1:2 * ti + 2], stats[:, 13:14]), sem=s_pn)
            p.wait("tensor", s_pn)
            for dt in range(8):
                slot = dt % 2
                ps = psum.t_u8[:, 12288 + slot * 2048: 12288 + slot * 2048 + 256].bitcast(BF16)
                if p.val(s_tmm) >= 2:
                    p.wait("tensor", s_tev, p.val(s_tev) - 1)
                p.op("tensor", lambda e, ps=ps, dt=dt: e.transpose(
                    ps, scrb[:, dt * 128:(dt + 1) * 128], cons["ident_bf"]), sem=s_tmm)
                p.wait("vector", s_tmm)
                p.op("vector", lambda e, ps=ps, dt=dt, ti=ti: e.tensor_copy(
                    xln_fm[:, dt, ti * 128:(ti + 1) * 128], ps), sem=s_tev)

        p.wait("tensor", s_tev)
        nstrip = 0
        for name, dstbuf in (("q", qc), ("k", kc)):
            for g in range(16):
                ps = psum.f((nstrip % 2) * 512, 512)[0:64, :]
                if nstrip >= 2:
                    p.wait("tensor", s_qev, p.val(s_qev) - 1)
                for dt in range(8):
                    p.op("tensor", lambda e, ps=ps, name=name, g=g, dt=dt: e.matmul(
                        ps, wT[name][:, dt, g * 64:(g + 1) * 64], xln_fm[:, dt, :],
                        start=(dt == 0), stop=(dt == 7)),
                        sem=(s_qkv if dt == 7 else None))
                p.wait("vector", s_qkv)
                p.op("vector", lambda e, ps=ps, dstbuf=dstbuf, g=g: e.tensor_copy(
                    dstbuf[0:64, g * 512:(g + 1) * 512], ps), sem=s_qev)
                nstrip += 1
        for tb in range(4):
            for dvc in range(2):
                ps = psum.f((nstrip % 2) * 512, 512)
                if nstrip >= 2:
                    p.wait("tensor", s_qev, p.val(s_qev) - 1)
                for dt in range(8):
                    p.op("tensor", lambda e, ps=ps, tb=tb, dvc=dvc, dt=dt: e.matmul(
                        ps, xln_fm[:, dt, tb * 128:(tb + 1) * 128],
                        wT["v"][:, dt, dvc * 512:(dvc + 1) * 512],
                        start=(dt == 0), stop=(dt == 7)),
                        sem=(s_qkv if dt == 7 else None))
                p.wait("vector", s_qkv)
                dst = vbuf[:, tb, dvc * 8:(dvc + 1) * 8, 0:64]
                p.op("vector", lambda e, ps=ps, dst=dst: e.tensor_copy(dst, ps), sem=s_qev)
                nstrip += 1

        if stage < 2:
            ar.release()
            continue

        p.op("vector", lambda e: e.memset(umax, 0.0), sem=s_nrm)
        p.wait("vector", s_qev)
        p.op("vector", lambda e: e.memset(kc[64:65, :], 1.0), sem=s_nrm)
        p.wait("scalar", s_qev)
        for blk in range(16):
            p.wait("scalar", s_nrm)
            p.op("scalar", lambda e, blk=blk: e.activation(
                sq, kc[0:64, blk * 512:(blk + 1) * 512], AF.Square,
                bias=0.0, scale=1.0), sem=s_nrm)
            ps = psum.f(1024, 512)[0:1, :]
            p.wait("tensor", s_nrm)
            p.op("tensor", lambda e, ps=ps: e.matmul(
                ps, onescol_bf[0:64, :], sq[0:64, :], start=True, stop=True),
                sem=s_nrm)
            p.wait("vector", s_nrm)
            p.op("vector", lambda e, ps=ps: e.tensor_reduce(
                umax[0:1, 2:3], ps, mybir.AxisListType.X, ALU.max), sem=s_nrm)
            p.wait("vector", s_nrm)
            p.op("vector", lambda e: e.tensor_tensor(
                umax[0:1, 0:1], umax[0:1, 0:1], umax[0:1, 2:3], ALU.max), sem=s_nrm)
        p.wait("scalar", s_nrm)
        p.op("scalar", lambda e: e.activation(
            umax[0:1, 4:5], umax[0:1, 0:1], AF.Sqrt, bias=0.0, scale=1.0), sem=s_nrm)
        ps_nb = psum.f(1024, 128)[:, 0:1]
        p.wait("tensor", s_nrm)
        p.op("tensor", lambda e, ps=ps_nb: e.matmul(
            ps, cons["ones_f32"], umax[0:1, 4:5], start=True, stop=True), sem=s_nrm)
        p.wait("vector", s_nrm)
        p.op("vector", lambda e, ps=ps_nb: e.tensor_copy(negu, ps), sem=s_nrm)
        for g in range(16):
            p.wait("scalar", s_nrm)
            p.op("scalar", lambda e, g=g: e.activation(
                sq, qc[0:64, g * 512:(g + 1) * 512], AF.Square,
                bias=0.0, scale=1.0), sem=s_nrm)
            psq = psum.f(1536, 512)[64:65, :]
            p.wait("tensor", s_nrm)
            p.op("tensor", lambda e, psq=psq: e.matmul(
                psq, onescol_bf[0:64, :], sq[0:64, :], start=True, stop=True,
                tile_position=(0, 64)), sem=s_nrm)
            p.wait("scalar", s_nrm)
            p.op("scalar", lambda e, psq=psq: e.activation(
                scr[64:65, 0:512], psq, AF.Sqrt, bias=0.0, scale=1.0), sem=s_nrm)
            p.wait("vector", s_nrm)
            p.op("vector", lambda e, g=g: e.tensor_scalar(
                qc[64:65, g * 512:(g + 1) * 512], scr[64:65, 0:512],
                negu[64:65, 0:1], -1.0, ALU.mult, ALU.subtract), sem=s_nrm)
        p.wait("vector", s_nrm)
        p.op("vector", lambda e: e.tensor_scalar(
            qc[64:65, :], qc[64:65, :], -1.0, None, ALU.mult, ALU.bypass), sem=s_nrm)

        if stage < 3:
            ar.release()
            continue

        p.wait("tensor", s_nrm)
        def mk_drain(qg, otps):
            def drain():
                p.wait("vector", s_av)
                p.wait("vector", s_ot)
                p.op("vector", lambda e, otps=otps: e.tensor_copy(
                    stag[0:65, :], otps[0:65, :]), sem=s_oc)
                p.wait("tensor", s_oc)
                p.wait("tensor", s_ov)
                for tbq in range(4):
                    px = psum.f(3584, 512)[:, tbq * 128:(tbq + 1) * 128]
                    p.op("tensor", lambda e, px=px, tbq=tbq: e.transpose(
                        px, stag[:, tbq * 128:(tbq + 1) * 128], cons["ident_f32"]),
                        sem=s_ot)
                p.wait("vector", s_ot)
                for tbq in range(4):
                    px = psum.f(3584, 512)[:, tbq * 128:(tbq + 1) * 128]
                    p.op("vector", lambda e, px=px: e.tensor_scalar(
                        rc, px[:, 64:65], 1e-35, None, ALU.add, ALU.bypass),
                        sem=s_oev)
                    p.wait("vector", s_oev)
                    p.op("vector", lambda e: e.reciprocal(rc, rc), sem=s_oev)
                    p.wait("vector", s_oev)
                    p.op("vector", lambda e, px=px, tbq=tbq, qg=qg: e.tensor_scalar(
                        ych[:, tbq, qg * 64:(qg + 1) * 64], px[:, 0:64], rc, None,
                        ALU.mult, ALU.bypass), sem=s_ov)
            return drain

        pending_drain = None
        for qg in range(16):
            otps = psum.f(0, 512)
            nb = len(kbs)
            sc_done = {}
            for it in range(nb + 1):
                if it < nb:
                    bi, batch = it, kbs[it]
                    stps = psum.f(512 + (bi % 2) * 1536, 1536)
                    p.wait("tensor", s_ex, exp_read[bi % 2])
                    for j, kb in enumerate(batch):
                        p.op("tensor", lambda e, stps=stps, j=j, kb=kb, qg=qg: e.matmul(
                            stps[:, j * 512:(j + 1) * 512],
                            kc[:, kb * 128:(kb + 1) * 128],
                            qc[:, qg * 512:(qg + 1) * 512],
                            start=True, stop=True),
                            sem=(s_sc if j == len(batch) - 1 else None))
                    sc_done[bi] = p.val(s_sc)
                if it == 1 and pending_drain is not None:
                    pending_drain()
                    pending_drain = None
                if it >= 1:
                    bi, batch = it - 1, kbs[it - 1]
                    stps = psum.f(512 + (bi % 2) * 1536, 1536)
                    ptbuf = ptb[bi % 2]
                    p.wait("scalar", s_sc, sc_done[bi])
                    p.wait("scalar", s_nrm)
                    p.wait("scalar", s_av, ptb_last_read[bi % 2])
                    n = len(batch) * 512
                    p.op("scalar", lambda e, stps=stps, ptbuf=ptbuf, n=n: e.activation(
                        ptbuf[:, 0:n], stps[:, 0:n], AF.Exp, bias=0.0, scale=1.0),
                        sem=s_ex)
                    exp_read[bi % 2] = p.val(s_ex)
                    if bi == 0:
                        p.wait("tensor", s_oc)
                    p.wait("tensor", s_ex)
                    for j, kb in enumerate(batch):
                        gk, tbk = kb // 4, kb % 4
                        vsl = vbuf[:, tbk, gk, :]
                        first = (bi == 0 and j == 0)
                        last = (bi == nb - 1 and j == len(batch) - 1)
                        p.op("tensor", lambda e, vsl=vsl, otps=otps, ptbuf=ptbuf, j=j,
                             first=first, last=last: e.matmul(
                                 otps[0:96, :], vsl, ptbuf[:, j * 512:(j + 1) * 512],
                                 start=first, stop=last), sem=s_av)
                    ptb_last_read[bi % 2] = p.val(s_av)
            pending_drain = mk_drain(qg, otps)
        pending_drain()

        for ti in range(4):
            t = c * 4 + ti
            p.wait("vector", s_ov)
            p.op("vector", lambda e, t=t, ti=ti: e.tensor_scalar(
                scr, x_sb[:, t, :], mrstash[:, 2 * ti:2 * ti + 1],
                mrstash[:, 2 * ti + 1:2 * ti + 2], ALU.subtract, ALU.mult), sem=s_fin)
            p.wait("vector", s_fin)
            p.op("vector", lambda e, ti=ti: e.tensor_tensor(
                ych[:, ti, :], ych[:, ti, :], scr, ALU.add), sem=s_fin)
            p.wait("vector", s_fin)
            for hh in range(2):
                p.op("vector", lambda e, t=t, ti=ti, hh=hh: e.bn_stats(
                    bnstash[:, (t * 2 + hh) * 6:(t * 2 + hh) * 6 + 6],
                    ych[:, ti, hh * 512:(hh + 1) * 512]), sem=s_fin)
            p.wait("scalar", s_fin)
            p.wait("scalar", s_yst, p.val(s_yst))
            p.op("scalar", lambda e, t=t, ti=ti: e.dma_start(
                out=y_hbm.rearrange("(t p) d -> p t d", p=128)[:, t, :],
                in_=ych[:, ti, :]), sem=s_yst, k=16)
        ar.release()

    if stage < 4:
        return s_fin

    p.wait("vector", s_fin)
    p.op("vector", lambda e: e.bn_aggr(mv2, bnstash), sem=s_fin)
    p.wait("vector", s_fin)
    p.op("vector", lambda e: e.tensor_scalar(
        msq, mv2[:, 0:1], mv2[:, 0:1], None, ALU.mult, ALU.bypass), sem=s_fin)
    p.wait("vector", s_fin)
    p.op("vector", lambda e: e.tensor_tensor(
        mv2[:, 1:2], mv2[:, 1:2], msq, ALU.add), sem=s_fin)
    psr = psum.f(1024, 512)[0:1, 0:2]
    p.wait("tensor", s_fin)
    p.op("tensor", lambda e, psr=psr: e.matmul(
        psr, cons["onesc_f32"], mv2, start=True, stop=True), sem=s_fin)
    p.wait("vector", s_fin)
    p.op("vector", lambda e, psr=psr: e.tensor_copy(red_sb[0:1, 0:2], psr), sem=s_fin)
    p.wait("gpsimd", s_fin)
    p.op("gpsimd", lambda e: e.dma_start(out=cc_in[0:1, :], in_=red_sb), sem=s_cc, k=16)
    p.wait("gpsimd", s_cc)
    p.op("gpsimd", lambda e: e.collective_compute(
        "AllReduce", ALU.add, replica_groups=groups,
        ins=[cc_in.ap().opt()], outs=[cc_out.ap().opt()]), sem=s_cc, k=1)
    p.wait("gpsimd", s_cc)
    p.op("gpsimd", lambda e: e.dma_start(out=red_sb, in_=cc_out[0:1, :]), sem=s_cc, k=16)
    for eng in ("vector", "scalar", "tensor"):
        p.wait(eng, s_cc)
    p.op("vector", lambda e: e.tensor_scalar(
        red_sb[0:1, 2:4], red_sb[0:1, 0:2], 1.0 / (4 * 128), None, ALU.mult,
        ALU.bypass), sem=s_fin)
    p.wait("vector", s_fin)
    p.op("vector", lambda e: e.tensor_tensor(
        red_sb[0:1, 4:5], red_sb[0:1, 2:3], red_sb[0:1, 2:3], ALU.mult), sem=s_fin)
    p.wait("vector", s_fin)
    p.op("vector", lambda e: e.tensor_tensor(
        red_sb[0:1, 5:6], red_sb[0:1, 3:4], red_sb[0:1, 4:5], ALU.subtract), sem=s_fin)
    p.wait("scalar", s_fin)
    p.op("scalar", lambda e: e.activation(
        red_sb[0:1, 6:7], red_sb[0:1, 5:6], AF.Sqrt, bias=cons["eps128"][0:1, :],
        scale=1.0), sem=s_fin)
    p.wait("vector", s_fin)
    p.op("vector", lambda e: e.reciprocal(red_sb[0:1, 7:8], red_sb[0:1, 6:7]), sem=s_fin)
    p.wait("vector", s_fin)
    p.op("vector", lambda e: e.tensor_copy(red_sb[0:1, 8:9], red_sb[0:1, 7:8]), sem=s_fin)
    p.wait("vector", s_fin)
    p.op("vector", lambda e: e.tensor_tensor(
        red_sb[0:1, 9:10], red_sb[0:1, 2:3], red_sb[0:1, 7:8], ALU.mult), sem=s_fin)
    p.wait("vector", s_fin)
    p.op("vector", lambda e: e.tensor_scalar(
        red_sb[0:1, 9:10], red_sb[0:1, 9:10], -1.0, None, ALU.mult, ALU.bypass), sem=s_fin)
    psb = psum.f(1024, 512)[:, 0:2]
    p.wait("tensor", s_fin)
    p.op("tensor", lambda e, psb=psb: e.matmul(
        psb, cons["ones_f32"], red_sb[0:1, 8:10], start=True, stop=True), sem=s_fin)
    p.wait("vector", s_fin)
    p.op("vector", lambda e, psb=psb: e.tensor_copy(bcast_sb, psb), sem=s_fin)
    ar.mark()
    yb = [ar.alloc([128, 1024], F32) for _ in range(2)]
    s_yl = [p.sem(pref + "yl0"), p.sem(pref + "yl1")]
    p.wait("sync", s_yst)
    p.wait("sync", s_cc)
    for t in range(NT):
        b = yb[t % 2]
        if t >= 2:
            p.wait("sync", s_fin, p.val(s_fin) - 2)
        p.op("sync", lambda e, b=b, t=t: e.dma_start(
            out=b, in_=y_hbm.rearrange("(t p) d -> p t d", p=128)[:, t, :]),
            sem=s_yl[t % 2], k=16)
        p.wait("vector", s_yl[t % 2])
        p.op("vector", lambda e, b=b, t=t: e.scalar_tensor_tensor(
            b, b, bcast_sb[:, 0:1], x_sb[:, t, :], ALU.mult, ALU.add), sem=s_fin)
        p.wait("vector", s_fin)
        p.op("vector", lambda e, b=b, t=t: e.tensor_scalar(
            x_sb[:, t, :], b, bcast_sb[:, 1:2], None, ALU.add, ALU.bypass), sem=s_fin)
    ar.release()
    return s_fin



L1_SCALE = 1.0 / (96.0 ** 0.5)


def emit_l1a(p, ar, psum, cons, x_sb, w_kvd, w_q, w_qr, w_kr, w_kup_loc, w_vup_loc,
             w_o, cos_own, sin_own, cos_all, sin_all, masks, qidx, attnidx, pref="b1",
             nheads=4, ngb=16, stage=9):
    nc = p.nc
    s_w = p.sem(pref + "w")
    s_tmm = p.sem(pref + "tmm")
    s_tev = p.sem(pref + "tev")
    s_pn = p.sem(pref + "pn")
    s_qkv = p.sem(pref + "qkv")
    s_qev = p.sem(pref + "qev")
    s_cc = p.sem(pref + "cc")
    s_kst = p.sem(pref + "kst")
    s_kh = p.sem(pref + "kh")
    s_sc = p.sem(pref + "sc")
    s_ex = p.sem(pref + "ex")
    s_av = p.sem(pref + "av")
    s_oev = p.sem(pref + "oev")
    s_nrm = p.sem(pref + "nrm")
    s_fin = p.sem(pref + "fin")
    s_dl = [p.sem(pref + "dl0"), p.sem(pref + "dl1")]

    agq_in = nc.dram_tensor(pref + "_aqin", [1024, 2048], BF16)
    agq_out = nc.dram_tensor(pref + "_aqout", [4096, 2048], BF16)
    ag1_in = nc.dram_tensor(pref + "_ag1in", [160, 2048], BF16)
    ag1_out = nc.dram_tensor(pref + "_ag1out", [640, 2048], BF16)
    ag2_in = [nc.dram_tensor(pref + "_ag2i%d" % i, [128, 8192], BF16)
              for i in range(2)]
    ag2_out = [nc.dram_tensor(pref + "_ag2o%d" % i, [512, 8192], BF16)
               for i in range(2)]
    groups = [[0, 1, 2, 3], [4, 5, 6, 7]]

    ar.mark()
    xln_fm = ar.alloc([128, 8, 2048], BF16)
    scr = ar.alloc([128, 1024], F32)
    scrb = ar.alloc([128, 1024], BF16)
    stats = ar.alloc([128, 16], F32)
    mr = ar.alloc([128, 2], F32)
    for t in range(NT):
        if t > 0:
            p.wait("scalar", s_tmm)
        emit_prenorm_tile(p, ar, x_sb[:, t, :], scr, scrb, stats, mr, s_pn, cons)
        p.wait("tensor", s_pn)
        for dt in range(8):
            slot = p.val(s_tmm) % 2
            ps = psum.t_u8[:, 12288 + slot * 2048: 12288 + slot * 2048 + 256].bitcast(BF16)
            if p.val(s_tmm) >= 2:
                p.wait("tensor", s_tev, p.val(s_tev) - 1)
            p.op("tensor", lambda e, ps=ps, dt=dt: e.transpose(
                ps, scrb[:, dt * 128:(dt + 1) * 128], cons["ident_bf"]), sem=s_tmm)
            p.wait("vector", s_tmm)
            p.op("vector", lambda e, ps=ps, dt=dt, t=t: e.tensor_copy(
                xln_fm[:, dt, t * 128:(t + 1) * 128], ps), sem=s_tev)

    wq_sb = ar.alloc([128, 8, 1024], BF16)
    wkvd_sb = ar.alloc([128, 8, 128], BF16)
    wqr_sb = ar.alloc([128, 8, 32], BF16)
    p.op("gpsimd", lambda e: e.dma_start(
        out=wq_sb, in_=w_q.rearrange("(t p) c -> p t c", p=128)), sem=s_w, k=16)
    p.op("gpsimd", lambda e: e.dma_start(
        out=wkvd_sb, in_=w_kvd.rearrange("(t p) c -> p t c", p=128)), sem=s_w, k=16)
    p.op("gpsimd", lambda e: e.dma_start(
        out=wqr_sb, in_=w_qr.rearrange("(t p) c -> p t c", p=128)), sem=s_w, k=16)
    costile = ar.alloc([128, 16, 32], F32)
    p.op("sync", lambda e: e.dma_start(
        out=costile[:, :, 0:16], in_=cos_own.rearrange("(t p) f -> p t f", p=128)),
        sem=s_kh, k=16)
    p.op("sync", lambda e: e.dma_start(
        out=costile[:, :, 16:32], in_=sin_own.rearrange("(t p) f -> p t f", p=128)),
        sem=s_kh, k=16)
    p.wait("tensor", s_w)
    p.wait("tensor", s_kh)
    p.wait("vector", s_kh)
    p.wait("tensor", s_tev)

    qloc = ar.alloc([128, 8, 2048], BF16)
    ckv = ar.alloc([128, 2048], BF16)
    qrt = ar.alloc([128, 16, 32], F32)
    nmm = 0
    for mt in range(8):
        for nt4 in range(4):
            ps = psum.f((nmm % 2) * 512, 512)
            if nmm >= 2:
                p.wait("tensor", s_qev, p.val(s_qev) - 1)
            for dt in range(8):
                p.op("tensor", lambda e, ps=ps, mt=mt, nt4=nt4, dt=dt: e.matmul(
                    ps, wq_sb[:, dt, mt * 128:(mt + 1) * 128],
                    xln_fm[:, dt, nt4 * 512:(nt4 + 1) * 512],
                    start=(dt == 0), stop=(dt == 7)),
                    sem=(s_qkv if dt == 7 else None))
            p.wait("vector", s_qkv)
            p.op("vector", lambda e, ps=ps, mt=mt, nt4=nt4: e.tensor_copy(
                qloc[:, mt, nt4 * 512:(nt4 + 1) * 512], ps), sem=s_qev)
            nmm += 1
    for nt4 in range(4):
        ps = psum.f((nmm % 2) * 512, 512)
        if nmm >= 2:
            p.wait("tensor", s_qev, p.val(s_qev) - 1)
        for dt in range(8):
            p.op("tensor", lambda e, ps=ps, nt4=nt4, dt=dt: e.matmul(
                ps, wkvd_sb[:, dt, :], xln_fm[:, dt, nt4 * 512:(nt4 + 1) * 512],
                start=(dt == 0), stop=(dt == 7)),
                sem=(s_qkv if dt == 7 else None))
        p.wait("vector", s_qkv)
        p.op("vector", lambda e, ps=ps, nt4=nt4: e.tensor_copy(
            ckv[:, nt4 * 512:(nt4 + 1) * 512], ps), sem=s_qev)
        nmm += 1
    for t in range(NT):
        ps = psum.f(1024 + (t % 2) * 512, 512)[:, 0:32]
        if t >= 2:
            p.wait("tensor", s_qev, p.val(s_qev) - 1)
        for dt in range(8):
            p.op("tensor", lambda e, ps=ps, t=t, dt=dt: e.matmul(
                ps, xln_fm[:, dt, t * 128:(t + 1) * 128], wqr_sb[:, dt, :],
                start=(dt == 0), stop=(dt == 7)),
                sem=(s_qkv if dt == 7 else None))
        p.wait("vector", s_qkv)
        p.op("vector", lambda e, ps=ps, t=t: e.tensor_copy(qrt[:, t, :], ps), sem=s_qev)
    rot0 = ar.alloc([128, 16, 32], F32)
    emit_rotary(p, qrt, costile, NT, s_qev, rot0)
    stg = ar.alloc([128, 128], BF16)
    qrf = ar.alloc([96, 2048], BF16)
    p.op("vector", lambda e: e.memset(stg, 0.0), sem=s_qev)
    for t in range(NT):
        p.wait("vector", s_qev)
        p.wait("vector", s_tmm)
        p.op("vector", lambda e, t=t: e.tensor_copy(
            stg[:, 64:96], qrt[:, t, :]), sem=s_qev)
        p.wait("tensor", s_qev)
        p.wait("tensor", s_tev)
        ps = psum.t_u8[:, 12288:12288 + 256].bitcast(BF16)
        p.op("tensor", lambda e, ps=ps: e.transpose(
            ps, stg, cons["ident_bf"]), sem=s_tmm)
        p.wait("vector", s_tmm)
        p.op("vector", lambda e, ps=ps, t=t: e.tensor_copy(
            qrf[64:96, t * 128:(t + 1) * 128], ps[64:96, :]), sem=s_tev)

    p.wait("scalar", s_qev)
    p.wait("scalar", s_tev)
    for mt in range(8):
        p.op("scalar", lambda e, mt=mt: e.dma_start(
            out=agq_in[mt * 128:(mt + 1) * 128, :], in_=qloc[:, mt, :]), sem=s_cc, k=16)
    p.op("scalar", lambda e: e.dma_start(
        out=ag1_in[0:32, :], in_=qrf[64:96, :]), sem=s_cc, k=16)
    p.op("scalar", lambda e: e.dma_start(
        out=ag1_in[32:160, :], in_=ckv), sem=s_cc, k=16)
    cc_stores = p.val(s_cc)
    p.wait("gpsimd", s_cc)
    p.op("gpsimd", lambda e: e.collective_compute(
        "AllGather", ALU.bypass, replica_groups=groups,
        ins=[ag1_in.ap().opt()], outs=[ag1_out.ap().opt()]), sem=s_cc, k=1)
    cc_ag1 = p.val(s_cc)
    p.op("gpsimd", lambda e: e.collective_compute(
        "AllGather", ALU.bypass, replica_groups=groups,
        ins=[agq_in.ap().opt()], outs=[agq_out.ap().opt()]), sem=s_cc, k=1)
    cc_agq = p.val(s_cc)
    ar.release()

    if stage < 2:
        for eng in ENGS:
            p.wait(eng, s_cc)
        return s_cc

    for eng in ("tensor", "vector", "scalar"):
        p.wait(eng, s_cc, cc_stores)
    ar.mark()
    ckv_g = ar.alloc([128, 4, 2048], BF16)
    kstack = ar.alloc([128, 8192], BF16)
    qstack = ar.alloc([128, 8192], BF16)
    vform = ar.alloc([128, 64, 130], BF16)
    attn = ar.alloc([128, 2, 8192], BF16)
    ptb = [ar.alloc([128, 1536], BF16) for _ in range(2)]
    wkup_sb = ar.alloc([128, 256], BF16)
    wvup_sb = ar.alloc([128, 256], BF16)
    masks_sb = ar.alloc([128, 4, 512], BF16)
    p.wait("sync", s_cc, cc_ag1)
    for r in range(4):
        p.op("sync", lambda e, r=r: e.dma_start(
            out=ckv_g[:, r, :], in_=ag1_out.ap()[r * 160 + 32: r * 160 + 160, :]),
            sem=s_kh, k=16)
    ar.mark()
    cosall = ar.alloc([128, 64, 32], BF16)
    p.wait("gpsimd", s_cc, cc_stores)
    p.op("gpsimd", lambda e: e.dma_start(
        out=cosall[:, :, 0:16], in_=cos_all.rearrange("(t p) f -> p t f", p=128)),
        sem=s_kst, k=16)
    p.op("gpsimd", lambda e: e.dma_start(
        out=cosall[:, :, 16:32], in_=sin_all.rearrange("(t p) f -> p t f", p=128)),
        sem=s_kst, k=16)
    wkr_sb = ar.alloc([128, 32], BF16)
    p.op("gpsimd", lambda e: e.dma_start(out=wkr_sb, in_=w_kr), sem=s_kst, k=16)
    p.op("gpsimd", lambda e: e.dma_start(out=wkup_sb, in_=w_kup_loc), sem=s_kst, k=16)
    p.op("gpsimd", lambda e: e.dma_start(out=wvup_sb, in_=w_vup_loc), sem=s_kst, k=16)
    p.op("sync", lambda e: e.dma_start(out=masks_sb, in_=masks.rearrange("j p q -> p j q")),
         sem=s_kh, k=16)
    p.wait("tensor", s_kst)
    p.wait("vector", s_kst)
    p.wait("tensor", s_kh)
    p.wait("vector", s_kh)

    g2c = []
    for r in range(4):
        for bi, blk in enumerate(BLOCKS[r]):
            g2c.append((blk, r, bi))
    g2c.sort()
    gcol = {blk: (r, bi) for blk, r, bi in g2c}
    xq = {blk: (r * 4 + bi) for blk, r, bi in g2c}

    krt = ar.alloc([128, 64, 32], BF16)
    nrp = 0
    for gb in range(16):
        r, bi = gcol[gb]
        for tt in range(4):
            gt = gb * 4 + tt
            ps = psum.f(1024 + (nrp % 2) * 512, 512)[:, 0:32]
            if nrp >= 2:
                p.wait("tensor", s_qev, p.val(s_qev) - 1)
            p.op("tensor", lambda e, ps=ps, r=r, bi=bi, tt=tt: e.matmul(
                ps, ckv_g[:, r, bi * 512 + tt * 128: bi * 512 + (tt + 1) * 128],
                wkr_sb[:, 0:32], start=True, stop=True),
                sem=s_qkv)
            p.wait("vector", s_qkv)
            p.op("vector", lambda e, ps=ps, gt=gt: e.tensor_copy(krt[:, gt, :], ps),
                 sem=s_qev)
            nrp += 1
    rot0c = ar.alloc([128, 64, 32], BF16)
    emit_rotary(p, krt, cosall, 64, s_qev, rot0c)
    stgc = ar.alloc([128, 128], BF16)
    p.op("vector", lambda e: e.memset(stgc, 0.0), sem=s_qev)
    for gt in range(64):
        p.wait("vector", s_qev)
        p.wait("vector", s_tmm)
        p.op("vector", lambda e, gt=gt: e.tensor_copy(
            stgc[:, 64:96], krt[:, gt, :]), sem=s_qev)
        p.wait("tensor", s_qev)
        p.wait("tensor", s_tev)
        ps = psum.t_u8[:, 12288:12288 + 256].bitcast(BF16)
        p.op("tensor", lambda e, ps=ps: e.transpose(
            ps, stgc, cons["ident_bf"]), sem=s_tmm)
        p.wait("vector", s_tmm)
        p.op("vector", lambda e, ps=ps, gt=gt: e.tensor_copy(
            kstack[64:96, gt * 128:(gt + 1) * 128], ps[64:96, :]), sem=s_tev)
    p.wait("vector", s_tev)
    p.op("vector", lambda e: e.memset(kstack[96:128, :], 0.0), sem=s_qev)
    ar.release()

    p.op("vector", lambda e: e.memset(qstack[96:128, :], 0.0), sem=s_qev)
    onesq = ar.alloc([128, 64], BF16)
    p.op("vector", lambda e: e.memset(onesq, 1.0), sem=s_qev)
    p.op("vector", lambda e: e.memset(attn, 0.0), sem=s_qev)
    onesn = ar.alloc([96, 1], BF16)
    p.op("vector", lambda e: e.memset(onesn, 1.0), sem=s_qev)
    rrow = ar.alloc([1, 512], F32)
    qgstg = ar.alloc([128, 4, 2048], BF16)
    qidx_sb = ar.alloc([128, 4, 4], U32)
    p.wait("sync", s_kh)
    p.op("sync", lambda e: e.dma_start(out=qidx_sb, in_=qidx), sem=s_kh, k=16)
    Rsb = ar.alloc([128, 512], F32)
    umax = ar.alloc([1, 8], F32)
    negu = ar.alloc([128, 1], F32)
    sq2 = [ar.alloc([96, 512], BF16) for _ in range(2)]
    unorm = ar.alloc([1, 32], F32)
    ostag = ar.alloc([128, 512], F32)
    s_sq = p.sem(pref + "usq")
    s_nq = p.sem(pref + "unq")
    s_nr = p.sem(pref + "unr")
    s_oc2 = p.sem(pref + "oc2")
    p.wait("sync", s_kh)
    for r in range(4):
        p.op("sync", lambda e, r=r: e.dma_start(
            out=qstack[64:96, r * 2048:(r + 1) * 2048],
            in_=ag1_out.ap()[r * 160: r * 160 + 32, :]), sem=s_kh, k=16)

    kbs3 = {}
    for gb in range(ngb):
        ext = (gb + 1) * 4
        kbs3[gb] = [list(range(i, min(i + 3, ext))) for i in range(0, ext, 3)]

    ptb_last_read = {0: 0, 1: 0}
    exp_read = {0: 0, 1: 0}
    for hi in range(nheads):
        par = hi % 2
        if hi == 0:
            p.wait("gpsimd", s_cc, cc_agq)
        p.wait("gpsimd", s_av)
        p.wait("gpsimd", s_kh)
        p.wait("gpsimd", s_qev)
        for r in range(4):
            p.op("gpsimd", lambda e, r=r, hi=hi: e.indirect_dma_start(
                out=qgstg[:, r, :], out_offset=None, in_=agq_out.ap(),
                in_offset=IndirectOffsetOnAxis(ap=qidx_sb[:, hi, r:r + 1], axis=0)),
                sem=s_dl[par], k=16)
        p.wait("vector", s_dl[par])
        for r in range(4):
            p.op("vector", lambda e, r=r: e.tensor_copy(
                qstack[0:64, r * 2048:(r + 1) * 2048], qgstg[0:64, r, :]), sem=s_qev)
        p.wait("tensor", s_ex)
        p.wait("tensor", s_oev)
        p.wait("vector", s_sc)
        p.wait("vector", s_av)
        for gb in range(16):
            r, bi = gcol[gb]
            ps = psum.f((gb % 2) * 512, 512)[0:64, :]
            if gb >= 2:
                p.wait("tensor", s_qev, p.val(s_qev) - 1)
            p.op("tensor", lambda e, ps=ps, r=r, bi=bi, hi=hi: e.matmul(
                ps, wkup_sb[:, hi * 64:(hi + 1) * 64],
                ckv_g[:, r, bi * 512:(bi + 1) * 512], start=True, stop=True),
                sem=s_qkv)
            p.wait("vector", s_qkv)
            p.op("vector", lambda e, ps=ps, gb=gb: e.tensor_copy(
                kstack[0:64, gb * 512:(gb + 1) * 512], ps), sem=s_qev)
        p.wait("vector", s_qev)
        p.op("vector", lambda e: e.memset(vform, 0.0), sem=s_qev)
        p.wait("vector", s_qev)
        oc = 0 if par == 1 else 64
        p.op("vector", lambda e, oc=oc: e.tensor_copy(
            vform[:, :, oc:oc + 1], onesq.unsqueeze(2)), sem=s_qev)
        vcol = 0 if par == 0 else 64
        for gt in range(64):
            gb4 = gt // 4
            r, bi = gcol[gb4]
            tt = gt % 4
            ps = psum.f(1024 + (gt % 2) * 512, 512)[:, 0:64]
            if gt >= 2:
                p.wait("tensor", s_qev, p.val(s_qev) - 1)
            p.op("tensor", lambda e, ps=ps, r=r, bi=bi, tt=tt, hi=hi: e.matmul(
                ps, ckv_g[:, r, bi * 512 + tt * 128: bi * 512 + (tt + 1) * 128],
                wvup_sb[:, hi * 64:(hi + 1) * 64], start=True, stop=True),
                sem=s_qkv)
            p.wait("vector", s_qkv)
            p.op("vector", lambda e, ps=ps, gt=gt, vcol=vcol: e.tensor_copy(
                vform[:, gt, vcol:vcol + 64], ps), sem=s_qev)
        p.wait("vector", s_dl[par])
        p.wait("scalar", s_qev)
        for idx in range(32):
            ui, blk = idx // 16, idx % 16
            buf = qstack if ui == 0 else kstack
            sq_b = sq2[idx % 2]
            psu = psum.f(3072 + (idx % 2) * 512, 512)[0:1, :]
            p.wait("scalar", s_nq, p.val(s_nq) - 1)
            p.op("scalar", lambda e, buf=buf, blk=blk, sq_b=sq_b: e.activation(
                sq_b, buf[0:96, blk * 512:(blk + 1) * 512], AF.Square,
                bias=0.0, scale=1.0), sem=s_sq)
            p.wait("tensor", s_sq)
            p.wait("tensor", s_nr, p.val(s_nr) - 1)
            p.op("tensor", lambda e, psu=psu, sq_b=sq_b: e.matmul(
                psu, onesn, sq_b, start=True, stop=True), sem=s_nq)
            p.wait("vector", s_nq)
            p.op("vector", lambda e, psu=psu, idx=idx: e.tensor_reduce(
                unorm[0:1, idx:idx + 1], psu, mybir.AxisListType.X, ALU.max),
                sem=s_nr)
        p.wait("vector", s_nr)
        p.op("vector", lambda e: e.tensor_reduce(
            umax[0:1, 0:1], unorm[0:1, 0:16], mybir.AxisListType.X, ALU.max),
            sem=s_nrm)
        p.op("vector", lambda e: e.tensor_reduce(
            umax[0:1, 1:2], unorm[0:1, 16:32], mybir.AxisListType.X, ALU.max),
            sem=s_nrm)
        p.wait("vector", s_nrm)
        p.op("vector", lambda e: e.tensor_tensor(
            umax[0:1, 3:4], umax[0:1, 0:1], umax[0:1, 1:2], ALU.mult), sem=s_nrm)
        p.wait("scalar", s_nrm)
        p.op("scalar", lambda e: e.activation(
            umax[0:1, 4:5], umax[0:1, 3:4], AF.Sqrt, bias=0.0, scale=1.0), sem=s_nrm)
        p.wait("scalar", s_nrm)
        p.op("scalar", lambda e: e.mul(umax[0:1, 5:6], umax[0:1, 4:5], -L1_SCALE), sem=s_nrm)
        psn = psum.f(3584, 512)[:, 0:1]
        p.wait("tensor", s_nrm)
        p.op("tensor", lambda e, ps=psn: e.matmul(
            ps, cons["ones_f32"], umax[0:1, 5:6], start=True, stop=True), sem=s_nrm)
        p.wait("vector", s_nrm)
        p.op("vector", lambda e, ps=psn: e.tensor_copy(negu, ps), sem=s_nrm)

        if stage < 3:
            continue

        p.wait("tensor", s_nrm)
        p.wait("tensor", s_nr)

        def mk_drain1(gb, otps, hi=hi, par=par):
            def drain():
                p.wait("vector", s_av)
                p.wait("vector", s_oev)
                p.op("vector", lambda e, otps=otps: e.tensor_copy(ostag, otps),
                     sem=s_oc2)
                sumrow = ostag[64:65, :] if par == 0 else ostag[0:1, :]
                p.op("vector", lambda e, sumrow=sumrow: e.tensor_scalar(
                    rrow, sumrow, 1e-35, None, ALU.add, ALU.bypass), sem=s_oev)
                p.wait("vector", s_oev)
                p.op("vector", lambda e: e.reciprocal(rrow, rrow), sem=s_oev)
                psR = psum.f(3584, 512)
                p.wait("tensor", s_oev)
                p.op("tensor", lambda e, ps=psR: e.matmul(
                    ps, cons["ones_f32"], rrow, start=True, stop=True), sem=s_oev)
                p.wait("vector", s_oev)
                p.op("vector", lambda e, ps=psR: e.tensor_copy(Rsb, ps), sem=s_oev)
                p.wait("vector", s_oev)
                vr = (0, 64) if par == 0 else (64, 128)
                qcol2 = xq[gb] * 512
                p.op("vector", lambda e, vr=vr, hi=hi, qcol2=qcol2: e.tensor_tensor(
                    attn[vr[0]:vr[1], hi // 2, qcol2:qcol2 + 512],
                    ostag[vr[0]:vr[1], :], Rsb[vr[0]:vr[1], :], ALU.mult), sem=s_oev)
            return drain

        pending1 = None
        for gb in range(ngb):
            otps = psum.f(0, 512)
            batches = kbs3[gb]
            nb = len(batches)
            sc_done = {}
            qcol = xq[gb] * 512
            for it in range(nb + 1):
                if it < nb:
                    bi2, batch = it, batches[it]
                    stps = psum.f(512 + (bi2 % 2) * 1536, 1536)
                    p.wait("tensor", s_ex, exp_read[bi2 % 2])
                    for j, kt in enumerate(batch):
                        p.op("tensor", lambda e, stps=stps, j=j, kt=kt, qcol=qcol: e.matmul(
                            stps[:, j * 512:(j + 1) * 512],
                            kstack[:, kt * 128:(kt + 1) * 128],
                            qstack[:, qcol:qcol + 512],
                            start=True, stop=True),
                            sem=(s_sc if j == len(batch) - 1 else None))
                    if any(kt - gb * 4 >= 0 for kt in batch):
                        p.wait("vector", s_sc)
                        for j, kt in enumerate(batch):
                            jj = kt - gb * 4
                            if jj >= 0:
                                p.op("vector", lambda e, stps=stps, j=j, jj=jj: e.tensor_tensor(
                                    stps[:, j * 512:(j + 1) * 512],
                                    stps[:, j * 512:(j + 1) * 512],
                                    masks_sb[:, jj, :], ALU.add), sem=s_sc)
                    sc_done[bi2] = p.val(s_sc)
                if it == 1 and pending1 is not None:
                    pending1()
                    pending1 = None
                if it >= 1:
                    bi2, batch = it - 1, batches[it - 1]
                    stps = psum.f(512 + (bi2 % 2) * 1536, 1536)
                    ptbuf = ptb[bi2 % 2]
                    p.wait("scalar", s_sc, sc_done[bi2])
                    p.wait("scalar", s_nrm)
                    p.wait("scalar", s_av, ptb_last_read[bi2 % 2])
                    n = len(batch) * 512
                    p.op("scalar", lambda e, stps=stps, ptbuf=ptbuf, n=n: e.activation(
                        ptbuf[:, 0:n], stps[:, 0:n], AF.Exp, bias=negu, scale=L1_SCALE),
                        sem=s_ex)
                    exp_read[bi2 % 2] = p.val(s_ex)
                    if bi2 == 0:
                        p.wait("tensor", s_oc2)
                    p.wait("tensor", s_ex)
                    for j, kt in enumerate(batch):
                        first = (bi2 == 0 and j == 0)
                        last = (bi2 == nb - 1 and j == len(batch) - 1)
                        vap = vform[:, kt, 0:65] if par == 0 else vform[:, kt, 0:128]
                        oap = otps[0:65, :] if par == 0 else otps
                        p.op("tensor", lambda e, vap=vap, oap=oap, ptbuf=ptbuf, j=j,
                             first=first, last=last: e.matmul(
                                 oap, vap, ptbuf[:, j * 512:(j + 1) * 512],
                                 start=first, stop=last), sem=s_av)
                    ptb_last_read[bi2 % 2] = p.val(s_av)
            pending1 = mk_drain1(gb, otps)
        pending1()

        if stage >= 4 and hi in (1, 3):
            pr = hi // 2
            p.wait("scalar", s_oev)
            p.op("scalar", lambda e, pr=pr: e.dma_start(
                out=ag2_in[pr].ap(), in_=attn[:, pr, :]), sem=s_cc, k=16)
            p.wait("gpsimd", s_cc)
            p.op("gpsimd", lambda e, pr=pr: e.collective_compute(
                "AllGather", ALU.bypass, replica_groups=groups,
                ins=[ag2_in[pr].ap().opt()], outs=[ag2_out[pr].ap().opt()]),
                sem=s_cc, k=1)

    if stage < 4:
        return s_oev

    for eng in ENGS:
        p.wait(eng, s_cc)
    ar.release()

    ar.mark()
    attn_own = ar.alloc([128, 8, 2048], BF16)
    wo_sb = ar.alloc([128, 8, 1024], BF16)
    aidx_sb = ar.alloc([128, 8], U32)
    p.wait("sync", s_kh)
    p.op("sync", lambda e: e.dma_start(out=aidx_sb, in_=attnidx), sem=s_kh, k=16)
    p.op("gpsimd", lambda e: e.dma_start(
        out=wo_sb, in_=w_o.rearrange("(t p) c -> p t c", p=128)), sem=s_w, k=16)
    p.wait("gpsimd", s_kh)
    ag2v = [t.ap().rearrange("r (g c) -> (r g) c", g=4) for t in ag2_out]
    for dt in range(8):
        p.op("gpsimd", lambda e, dt=dt: e.indirect_dma_start(
            out=attn_own[:, dt, :], out_offset=None, in_=ag2v[dt % 2],
            in_offset=IndirectOffsetOnAxis(ap=aidx_sb[:, dt:dt + 1], axis=0)),
            sem=s_kst, k=16)
    p.wait("tensor", s_kst)
    p.wait("tensor", s_kh)
    p.wait("tensor", s_w)
    ystg = ar.alloc([128, 512], F32)
    for mt in range(8):
        for nt4 in range(4):
            ps = psum.f(((mt * 4 + nt4) % 2) * 512, 512)
            if mt * 4 + nt4 >= 2:
                p.wait("tensor", s_qev, p.val(s_qev) - 1)
            for dt in range(8):
                p.op("tensor", lambda e, ps=ps, mt=mt, nt4=nt4, dt=dt: e.matmul(
                    ps, wo_sb[:, dt, mt * 128:(mt + 1) * 128],
                    attn_own[:, dt, nt4 * 512:(nt4 + 1) * 512],
                    start=(dt == 0), stop=(dt == 7)),
                    sem=(s_qkv if dt == 7 else None))
            p.wait("vector", s_qkv)
            p.wait("vector", s_tmm)
            p.op("vector", lambda e, ps=ps: e.tensor_copy(ystg, ps), sem=s_qev)
            for sub in range(4):
                t = nt4 * 4 + sub
                px = psum.f(1536, 128)
                p.wait("tensor", s_qev)
                p.wait("tensor", s_fin)
                p.op("tensor", lambda e, px=px, sub=sub: e.transpose(
                    px, ystg[:, sub * 128:(sub + 1) * 128], cons["ident_f32"]),
                    sem=s_tmm)
                p.wait("vector", s_tmm)
                p.op("vector", lambda e, px=px, t=t, mt=mt: e.tensor_tensor(
                    x_sb[:, t, mt * 128:(mt + 1) * 128],
                    x_sb[:, t, mt * 128:(mt + 1) * 128], px, ALU.add), sem=s_fin)
    ar.release()
    return s_fin




def emit_rotary(p, buf, cs, nt, s_ev, r0):
    v1 = buf[:, 0:nt, 0:16]
    v2 = buf[:, 0:nt, 16:32]
    co = cs[:, 0:nt, 0:16]
    si = cs[:, 0:nt, 16:32]
    sA = r0[:, 0:nt, 0:16]
    sB = r0[:, 0:nt, 16:32]
    for f in [
        lambda e: e.tensor_tensor(sA, v1, co, ALU.mult),
        lambda e: e.tensor_tensor(sB, v1, si, ALU.mult),
        lambda e: e.tensor_tensor(v1, v2, si, ALU.mult),
        lambda e: e.tensor_tensor(v1, sA, v1, ALU.subtract),
        lambda e: e.tensor_tensor(v2, v2, co, ALU.mult),
        lambda e: e.tensor_tensor(v2, v2, sB, ALU.add),
    ]:
        p.wait("vector", s_ev)
        p.op("vector", f, sem=s_ev)



CAP = 640
CROW = CAP + 1
MROWS = TPC + CAP
N_BIS = 30


def emit_l0f_wprep(p, ar, psum, cons, wg_loc, wu_loc, wd_loc, pref="m0p"):
    nc = p.nc
    s_w = p.sem(pref + "w")
    s_tmm = p.sem(pref + "tmm")
    s_tev = p.sem(pref + "tev")
    s_wst = p.sem(pref + "wst")
    s_wst2 = p.sem(pref + "wst2")
    s_cc = p.sem(pref + "cc")
    s_zr = p.sem(pref + "zr")
    s_pn = p.sem(pref + "pn")
    wgT_loc = nc.dram_tensor(pref + "_wgTl", [4096, 1024], BF16)
    wuT_loc = nc.dram_tensor(pref + "_wuTl", [4096, 1024], BF16)
    wdT_loc = nc.dram_tensor(pref + "_wdTl", [4096, 1024], BF16)
    wgT_all = nc.dram_tensor(pref + "_wgTa", [8 * 4096, 1024], BF16, addr_space="Shared")
    wuT_all = nc.dram_tensor(pref + "_wuTa", [8 * 4096, 1024], BF16, addr_space="Shared")
    wdT_all = nc.dram_tensor(pref + "_wdTa", [8 * 4096, 1024], BF16, addr_space="Shared")
    compact = nc.dram_tensor(pref + "_cmp", [8 * CROW, 1040], BF16).ap()
    moeacc = nc.dram_tensor(pref + "_acc", [MROWS, 1024], F32).ap()
    zt = ar.alloc([128, 1040], BF16)

    p.op("vector", lambda e: e.memset(zt, 0.0), sem=s_pn)
    p.wait("gpsimd", s_pn)
    for r0 in range(0, 8 * CROW, 128):
        r1 = min(r0 + 128, 8 * CROW)
        p.op("gpsimd", lambda e, r0=r0, r1=r1: e.dma_start(
            out=compact[r0:r1, :], in_=zt[0:r1 - r0, :]), sem=s_zr, k=16)
    ztf = zt.bitcast(F32)
    for r0 in range(0, MROWS, 128):
        r1 = min(r0 + 128, MROWS)
        for c0 in (0, 512):
            p.op("gpsimd", lambda e, r0=r0, r1=r1, c0=c0: e.dma_start(
                out=moeacc[r0:r1, c0:c0 + 512], in_=ztf[0:r1 - r0, 0:512]),
                sem=s_zr, k=16)

    for w_loc, wT_loc, R, C, grp in ((wg_loc, wgT_loc, 4096, 1024, "r"),
                                     (wu_loc, wuT_loc, 4096, 1024, "r"),
                                     (wd_loc, wdT_loc, 1024, 4096, "c")):
        emit_wT_to_hbm(p, ar, psum, cons, w_loc, R, C,
                       wT_loc.ap().rearrange("(t p) c -> t p c", p=128),
                       grp, (s_w, s_tmm, s_tev, s_wst, s_wst2),
                       store_eng="sync")
    p.wait("tensor", s_tev)
    return {"wgT_all": wgT_all, "wuT_all": wuT_all, "wdT_all": wdT_all,
            "wgT_loc": wgT_loc, "wuT_loc": wuT_loc, "wdT_loc": wdT_loc,
            "compact": compact, "moeacc": moeacc,
            "s_cc": s_cc, "s_zr": s_zr, "zr_total": p.val(s_zr),
            "s_wst": s_wst, "s_wst2": s_wst2,
            "gates": [(s_wst, p.val(s_wst)), (s_wst2, p.val(s_wst2))]}


def emit_l1f_wprep(p, ar, psum, cons, w_fc, w_proj, pref="f1p"):
    nc = p.nc
    s_w = p.sem(pref + "w")
    s_tmm = p.sem(pref + "tmm")
    s_tev = p.sem(pref + "tev")
    s_wst = p.sem(pref + "wst")
    s_wst2 = p.sem(pref + "wst2")
    wfcT_hbm = nc.dram_tensor(pref + "_wfcT", [32, 128, 1024], BF16).ap()
    wprT_hbm = nc.dram_tensor(pref + "_wprT", [32, 128, 1024], BF16).ap()
    emit_wT_to_hbm(p, ar, psum, cons, w_fc, 4096, 1024, wfcT_hbm, "r",
                   (s_w, s_tmm, s_tev, s_wst, s_wst2), store_eng="sync")
    emit_wT_to_hbm(p, ar, psum, cons, w_proj, 1024, 4096, wprT_hbm, "c",
                   (s_w, s_tmm, s_tev, s_wst, s_wst2), store_eng="sync")
    p.wait("tensor", s_tev)
    return {"wfcT": wfcT_hbm, "wprT": wprT_hbm,
            "s_wst": s_wst, "s_wst2": s_wst2,
            "gates": [(s_wst, p.val(s_wst)), (s_wst2, p.val(s_wst2))]}


def emit_l0f(p, ar, psum, cons, x_sb, gate_w, wp,
             m32, erow, idrow, trasht, pref="m0", nexp=8, stage=9,
             dbg=None):
    nc = p.nc
    s_w = p.sem(pref + "w")
    s_tmm = p.sem(pref + "tmm")
    s_tev = p.sem(pref + "tev")
    s_wst = p.sem(pref + "wst")
    s_cc = wp["s_cc"]
    s_ca = p.sem(pref + "ca")
    s_cg = p.sem(pref + "cg")
    s_pn = p.sem(pref + "pn")
    s_mm = p.sem(pref + "mm")
    s_ev = p.sem(pref + "ev")
    s_hh = p.sem(pref + "hh")
    s_dm = p.sem(pref + "dm")
    s_xe = p.sem(pref + "xe")
    s_kh = p.sem(pref + "kh")
    s_bi = p.sem(pref + "bi")
    s_sca = p.sem(pref + "sca")
    s_add = p.sem(pref + "add")
    s_ws = [p.sem(pref + "ws0"), p.sem(pref + "ws1")]
    s_wd = [p.sem(pref + "wd0"), p.sem(pref + "wd1")]
    s_fin = p.sem(pref + "fin")

    af_in = nc.dram_tensor(pref + "_afin", [8, TPC], F32)
    af_out = nc.dram_tensor(pref + "_afout", [32, TPC], F32)
    compact = wp["compact"]
    moeacc = wp["moeacc"]
    wg_v = wp["wgT_all"].ap().rearrange("(E t p) c -> E t p c", E=8, p=128)
    wu_v = wp["wuT_all"].ap().rearrange("(E t p) c -> E t p c", E=8, p=128)
    wd_v = wp["wdT_all"].ap().rearrange("(E t p) c -> E t p c", E=8, p=128)

    affin = ar.alloc([128, NT, 8], F32)
    scr = ar.alloc([128, 1024], F32)
    stats = ar.alloc([128, 24], F32)
    mr = ar.alloc([128, 2], F32)
    gwT = ar.alloc([128, 8, 8], F32)
    xlt = ar.alloc([128, 8, 128], F32)
    ar.mark()
    comb = ar.alloc([128, NT, 1040], BF16)
    gw_stg = ar.alloc([128, 1024], F32)
    p.op("sync", lambda e: e.dma_start(out=gw_stg[0:8, :], in_=gate_w), sem=s_kh, k=16)
    p.wait("tensor", s_kh)
    for dt in range(8):
        slot = p.val(s_tmm) % 2
        ps = psum.t_u8[:, 12288 + slot * 2048: 12288 + slot * 2048 + 512].bitcast(F32)
        if p.val(s_tmm) >= 2:
            p.wait("tensor", s_tev, p.val(s_tev) - 1)
        p.op("tensor", lambda e, ps=ps, dt=dt: e.transpose(
            ps, gw_stg[:, dt * 128:(dt + 1) * 128], cons["ident_f32"]), sem=s_tmm)
        p.wait("vector", s_tmm)
        p.op("vector", lambda e, ps=ps, dt=dt: e.tensor_copy(
            gwT[:, dt, :], ps[:, 0:8]), sem=s_tev)

    for t in range(NT):
        if t > 0:
            p.wait("vector", s_tmm)
        emit_prenorm_tile(p, ar, x_sb[:, t, :], scr, comb[:, t, 0:1024],
                          stats, mr, s_pn, cons)
        p.wait("tensor", s_pn)
        for dt in range(8):
            slot = p.val(s_tmm) % 2
            ps = psum.t_u8[:, 12288 + slot * 2048: 12288 + slot * 2048 + 512].bitcast(F32)
            p.wait("tensor", s_tev, p.val(s_tev) - 1)
            p.op("tensor", lambda e, ps=ps, dt=dt: e.transpose(
                ps, scr[:, dt * 128:(dt + 1) * 128], cons["ident_f32"]), sem=s_tmm)
            p.wait("vector", s_tmm)
            p.op("vector", lambda e, ps=ps, dt=dt: e.tensor_copy(
                xlt[:, dt, :], ps), sem=s_tev)
        lg = psum.f(2048 + (t % 2) * 512, 512)[:, 0:8]
        p.wait("tensor", s_tev)
        if t >= 2:
            p.wait("tensor", s_ev, p.val(s_ev) - 1)
        for dt in range(8):
            p.op("tensor", lambda e, lg=lg, dt=dt: e.matmul(
                lg, xlt[:, dt, :], gwT[:, dt, :], start=(dt == 0), stop=(dt == 7)),
                sem=(s_mm if dt == 7 else None))
        p.wait("vector", s_mm)
        p.op("vector", lambda e, lg=lg: e.reduce_max(
            stats[:, 16:17], lg, mybir.AxisListType.X), sem=s_bi)
        p.wait("vector", s_bi)
        p.op("vector", lambda e: e.tensor_scalar(
            stats[:, 17:18], stats[:, 16:17], -1.0, None, ALU.mult, ALU.bypass),
            sem=s_bi)
        p.wait("scalar", s_bi)
        p.op("scalar", lambda e, lg=lg, t=t: e.activation(
            affin[:, t, :], lg, AF.Exp, bias=stats[:, 17:18], scale=1.0), sem=s_ev)
        p.wait("vector", s_ev)
        p.op("vector", lambda e, t=t: e.reduce_sum(
            stats[:, 18:19], affin[:, t, :], mybir.AxisListType.X), sem=s_bi)
        p.wait("vector", s_bi)
        p.op("vector", lambda e: e.reciprocal(
            stats[:, 19:20], stats[:, 18:19]), sem=s_bi)
        p.wait("vector", s_bi)
        p.op("vector", lambda e, t=t: e.tensor_scalar(
            affin[:, t, :], affin[:, t, :], stats[:, 19:20], None, ALU.mult,
            ALU.bypass), sem=s_bi)

    if stage < 1:
        return s_bi

    aem = ar.alloc([128, TPC], F32)
    stg = ar.alloc([128, 128], F32)
    p.op("vector", lambda e: e.memset(stg, 0.0), sem=s_bi)
    p.wait("vector", s_ev)
    for t in range(NT):
        p.wait("vector", s_bi)
        p.wait("vector", s_tmm)
        p.op("vector", lambda e, t=t: e.tensor_copy(
            stg[:, 0:8], affin[:, t, :]), sem=s_bi)
        p.wait("tensor", s_bi)
        slot = p.val(s_tmm) % 2
        ps = psum.t_u8[:, 12288 + slot * 2048: 12288 + slot * 2048 + 512].bitcast(F32)
        p.wait("tensor", s_tev, p.val(s_tev) - 1)
        p.op("tensor", lambda e, ps=ps: e.transpose(
            ps, stg, cons["ident_f32"]), sem=s_tmm)
        p.wait("vector", s_tmm)
        p.op("vector", lambda e, ps=ps, t=t: e.tensor_copy(
            aem[0:8, t * 128:(t + 1) * 128], ps[0:8, :]), sem=s_tev)
    p.wait("scalar", s_tev)
    p.op("scalar", lambda e: e.dma_start(out=af_in.ap(), in_=aem[0:8, :]),
         sem=s_ca, k=16)
    p.wait("gpsimd", wp["s_wst"])
    p.wait("gpsimd", wp["s_wst2"])
    p.op("gpsimd", lambda e: e.collective_compute(
        "AllGather", ALU.bypass, replica_groups=[[0, 1, 2, 3, 4, 5, 6, 7]],
        ins=[wp["wgT_loc"].ap().opt()], outs=[wp["wgT_all"].ap().opt()]),
        sem=s_cc, k=1)
    p.wait("gpsimd", s_ca, 16)
    p.op("gpsimd", lambda e: e.collective_compute(
        "AllGather", ALU.bypass, replica_groups=[[0, 1, 2, 3], [4, 5, 6, 7]],
        ins=[af_in.ap().opt()], outs=[af_out.ap().opt()]), sem=s_cg, k=1)
    p.op("gpsimd", lambda e: e.collective_compute(
        "AllGather", ALU.bypass, replica_groups=[[0, 1, 2, 3, 4, 5, 6, 7]],
        ins=[wp["wuT_loc"].ap().opt()], outs=[wp["wuT_all"].ap().opt()]),
        sem=s_cc, k=1)
    p.op("gpsimd", lambda e: e.collective_compute(
        "AllGather", ALU.bypass, replica_groups=[[0, 1, 2, 3, 4, 5, 6, 7]],
        ins=[wp["wdT_loc"].ap().opt()], outs=[wp["wdT_all"].ap().opt()]),
        sem=s_cc, k=1)

    af32 = ar.alloc([32, TPC], F32)
    junk = ar.alloc([32, TPC], F32)
    m32_sb = ar.alloc([32, 32], F32)
    bis = ar.alloc([32, 8], F32)
    selu32 = ar.alloc([32, 1], U8)
    p.wait("sync", s_cg)
    p.op("sync", lambda e: e.dma_start(out=af32, in_=af_out.ap()), sem=s_kh, k=16)
    p.op("sync", lambda e: e.dma_start(out=m32_sb, in_=m32), sem=s_kh, k=16)
    p.wait("vector", s_kh)
    p.wait("tensor", s_kh)
    p.op("vector", lambda e: e.memset(bis[:, 0:1], 0.0), sem=s_bi)
    p.op("vector", lambda e: e.memset(bis[:, 1:2], 1.0), sem=s_bi)
    for it in range(N_BIS):
        p.wait("vector", s_bi)
        p.op("vector", lambda e: e.tensor_tensor(
            bis[:, 2:3], bis[:, 0:1], bis[:, 1:2], ALU.add), sem=s_bi)
        p.wait("vector", s_bi)
        p.op("vector", lambda e: e.tensor_scalar(
            bis[:, 2:3], bis[:, 2:3], 0.5, None, ALU.mult, ALU.bypass), sem=s_bi)
        p.wait("vector", s_bi)
        p.op("vector", lambda e: e.tensor_scalar(
            junk, af32, bis[:, 2:3], 0.0, ALU.is_ge, ALU.add,
            accum_out=bis[:, 4:5]), sem=s_bi)
        ps = psum.f(2048, 512)
        p.wait("tensor", s_bi)
        p.op("tensor", lambda e, ps=ps: e.matmul(
            ps[0:32, 0:1], m32_sb, bis[:, 4:5], start=True, stop=True), sem=s_bi)
        p.wait("vector", s_bi)
        p.op("vector", lambda e, ps=ps: e.tensor_copy(
            bis[:, 3:4], ps[0:32, 0:1]), sem=s_bi)
        p.wait("vector", s_bi)
        p.op("vector", lambda e: e.tensor_scalar(
            selu32, bis[:, 3:4], 2048.0, None, ALU.is_ge, ALU.bypass), sem=s_bi)
        p.wait("vector", s_bi)
        p.op("vector", lambda e: e.select(
            bis[:, 5:6], selu32, bis[:, 2:3], bis[:, 0:1]), sem=s_bi)
        p.op("vector", lambda e: e.select(
            bis[:, 6:7], selu32, bis[:, 1:2], bis[:, 2:3]), sem=s_bi)
        p.wait("vector", s_bi)
        p.op("vector", lambda e: e.tensor_copy(bis[:, 0:1], bis[:, 5:6]), sem=s_bi)
        p.op("vector", lambda e: e.tensor_copy(bis[:, 1:2], bis[:, 6:7]), sem=s_bi)

    if dbg and "bis" in dbg:
        p.wait("scalar", s_bi)
        p.op("scalar", lambda e: e.dma_start(out=dbg["bis"], in_=bis), sem=s_kh, k=16)
    if dbg and "aem" in dbg:
        p.wait("scalar", s_tev)
        p.op("scalar", lambda e: e.dma_start(out=dbg["aem"], in_=aem[0:8, :]), sem=s_kh, k=16)
    if dbg and "af32" in dbg:
        p.wait("scalar", s_kh)
        p.op("scalar", lambda e: e.dma_start(out=dbg["af32"], in_=af32), sem=s_kh, k=16)
    if stage < 2:
        p.wait("scalar", s_kh)
        return s_bi

    idsb = ar.alloc([128, NT, 1], I16)
    p.op("sync", lambda e: e.dma_start(out=idsb, in_=idrow), sem=s_kh, k=16)
    maskf = ar.alloc([8, TPC], F32)
    posem = ar.alloc([8, TPC], F32)
    ones8 = ar.alloc([8, TPC], F32)
    selm = ar.alloc([8, TPC], U8)
    selmf = ar.alloc([8, TPC], F32)
    posx = ar.alloc([8, TPC], F32)
    bigc = ar.alloc([8, TPC], F32)
    erow_sb = ar.alloc([8, 1], F32)
    pos_tm = ar.alloc([128, NT, 8], F32)
    posu = ar.alloc([128, NT, 8], U32)
    p.op("sync", lambda e: e.dma_start(out=erow_sb, in_=erow), sem=s_kh, k=16)
    p.op("vector", lambda e: e.memset(ones8, 1.0), sem=s_bi)
    p.op("vector", lambda e: e.memset(bigc, float(CAP)), sem=s_bi)
    p.wait("vector", s_kh)
    p.wait("vector", s_bi)
    p.op("vector", lambda e: e.tensor_scalar(
        maskf, aem[0:8, :], bis[0:8, 0:1], None, ALU.is_ge, ALU.bypass), sem=s_bi)
    if dbg and "maskf" in dbg:
        p.wait("scalar", s_bi)
        p.op("scalar", lambda e: e.dma_start(out=dbg["maskf"], in_=maskf), sem=s_kh, k=16)
        p.wait("vector", s_kh)
    p.wait("vector", s_bi)
    p.op("vector", lambda e: e.tensor_tensor_scan(
        posem, ones8, maskf, 0.0, ALU.mult, ALU.add), sem=s_bi)
    if dbg and "scan" in dbg:
        p.wait("scalar", s_bi)
        p.op("scalar", lambda e: e.dma_start(out=dbg["scan"], in_=posem), sem=s_kh, k=16)
        p.wait("vector", s_kh)
    p.wait("vector", s_bi)
    p.op("vector", lambda e: e.tensor_scalar(
        posem, posem, -1.0, None, ALU.add, ALU.bypass), sem=s_bi)
    p.wait("vector", s_bi)
    p.op("vector", lambda e: e.tensor_scalar(
        selmf, posem, float(CAP), None, ALU.is_lt, ALU.bypass), sem=s_bi)
    p.wait("vector", s_bi)
    p.op("vector", lambda e: e.tensor_tensor(
        selmf, selmf, maskf, ALU.mult), sem=s_bi)
    p.wait("vector", s_bi)
    p.op("vector", lambda e: e.tensor_scalar(
        selm, selmf, 0.5, None, ALU.is_ge, ALU.bypass), sem=s_bi)
    p.wait("vector", s_bi)
    p.op("vector", lambda e: e.select(posx, selm, posem, bigc), sem=s_bi)
    p.wait("vector", s_bi)
    p.op("vector", lambda e: e.tensor_scalar(
        posem, posx, erow_sb, None, ALU.add, ALU.bypass), sem=s_bi)
    p.op("vector", lambda e: e.memset(stg, 0.0), sem=s_bi)
    for t in range(NT):
        p.wait("vector", s_bi)
        p.wait("vector", s_tmm)
        p.op("vector", lambda e, t=t: e.tensor_copy(
            stg[0:8, :], posem[0:8, t * 128:(t + 1) * 128]), sem=s_bi)
        p.wait("tensor", s_bi)
        slot = p.val(s_tmm) % 2
        ps = psum.t_u8[:, 12288 + slot * 2048: 12288 + slot * 2048 + 512].bitcast(F32)
        p.wait("tensor", s_tev, p.val(s_tev) - 1)
        p.op("tensor", lambda e, ps=ps: e.transpose(
            ps, stg, cons["ident_f32"]), sem=s_tmm)
        p.wait("vector", s_tmm)
        p.op("vector", lambda e, ps=ps, t=t: e.tensor_copy(
            pos_tm[:, t, :], ps[:, 0:8]), sem=s_tev)
    p.wait("vector", s_tev)
    p.op("vector", lambda e: e.tensor_copy(posu, pos_tm), sem=s_bi)
    p.wait("vector", s_ev)
    p.op("vector", lambda e: e.tensor_copy(comb[:, :, 1024:1032], affin), sem=s_bi)
    p.wait("vector", s_kh)
    p.op("vector", lambda e: e.tensor_copy(
        comb[:, :, 1032:1033], idsb.bitcast(BF16)), sem=s_bi)
    p.wait("gpsimd", s_bi)
    p.wait("gpsimd", s_pn)
    p.wait("gpsimd", wp["s_zr"], wp["zr_total"])
    for e_ in range(nexp):
        for t in range(NT):
            p.op("gpsimd", lambda e, e_=e_, t=t: e.indirect_dma_start(
                out=compact, out_offset=IndirectOffsetOnAxis(
                    ap=posu[:, t, e_:e_ + 1], axis=0),
                in_=comb[:, t, :], in_offset=None), sem=s_sca, k=16)

    if dbg and "posu" in dbg:
        p.wait("scalar", s_bi)
        p.op("scalar", lambda e: e.dma_start(out=dbg["posu"], in_=posu), sem=s_kh, k=16)
        p.wait("scalar", s_kh)
    if dbg and "cmp" in dbg:
        p.wait("gpsimd", s_sca)
        p.op("gpsimd", lambda e: e.dma_start(out=dbg["cmp"], in_=compact), sem=s_sca, k=16)
    if stage < 3:
        return s_sca

    ar.release()
    ar.mark()
    rb = ar.alloc([128, 5, 1040], BF16)
    xin_fm = ar.alloc([128, 8, CAP], BF16)
    h2 = ar.alloc([128, 32, CAP], BF16)
    xe_sb = ar.alloc([128, 5, 1024], BF16)
    gat = ar.alloc([128, 5, 1], F32)
    idf = ar.alloc([128, 5, 1], F32)
    idxf = ar.alloc([128, 5, 1], F32)
    idxu = ar.alloc([128, 5, 1], U32)
    selg = ar.alloc([128, 5, 1], U8)
    trash_sb = ar.alloc([128, 5, 1], F32)
    wgb = [ar.alloc([128, 8, 128], BF16) for _ in range(2)]
    wub = [ar.alloc([128, 8, 128], BF16) for _ in range(2)]
    wdb = [ar.alloc([128, 512], BF16) for _ in range(2)]
    p.op("sync", lambda e: e.dma_start(out=trash_sb, in_=trasht), sem=s_kh, k=16)
    p.wait("vector", s_kh)
    p.wait("sync", s_cc, 2)
    for e_ in range(nexp):
        p.wait("sync", s_sca)
        p.wait("sync", s_tev)
        p.wait("sync", s_hh)
        p.op("sync", lambda e, e_=e_: e.dma_start(
            out=rb, in_=compact[e_ * CROW: e_ * CROW + CAP, :].rearrange(
                "(c p) w -> p c w", p=128)), sem=s_kh, k=16)
        p.wait("vector", s_kh)
        if e_ > 0:
            p.wait("vector", s_add)
        p.op("vector", lambda e, e_=e_: e.tensor_copy(
            gat, rb[:, :, 1024 + e_:1025 + e_]), sem=s_ev)
        p.op("vector", lambda e: e.tensor_copy(
            idf, rb[:, :, 1032:1033].bitcast(I16)), sem=s_ev)
        p.wait("vector", s_ev)
        p.op("vector", lambda e: e.tensor_scalar(
            selg, gat, 0.0, None, ALU.is_equal, ALU.bypass), sem=s_ev)
        p.wait("vector", s_ev)
        p.op("vector", lambda e: e.select(idxf, selg, trash_sb, idf), sem=s_ev)
        p.wait("vector", s_ev)
        p.op("vector", lambda e: e.tensor_copy(idxu, idxf), sem=s_ev)
        p.wait("tensor", s_kh)
        for c5 in range(5):
            for dt in range(8):
                slot = p.val(s_tmm) % 2
                ps = psum.t_u8[:, 12288 + slot * 2048:
                               12288 + slot * 2048 + 256].bitcast(BF16)
                p.wait("tensor", s_tev, p.val(s_tev) - 1)
                p.op("tensor", lambda e, ps=ps, c5=c5, dt=dt: e.transpose(
                    ps, rb[:, c5, dt * 128:(dt + 1) * 128], cons["ident_bf"]),
                    sem=s_tmm)
                p.wait("vector", s_tmm)
                p.op("vector", lambda e, ps=ps, c5=c5, dt=dt: e.tensor_copy(
                    xin_fm[:, dt, c5 * 128:(c5 + 1) * 128], ps), sem=s_tev)
        p.wait("tensor", s_tev)
        p.wait("tensor", s_xe)
        for it in range(32):
            for wb, hbm in ((wgb[it % 2], wg_v), (wub[it % 2], wu_v)):
                if p.val(s_mm) >= 2:
                    p.wait("sync", s_mm, p.val(s_mm) - 1)
                p.op("sync", lambda e, wb=wb, hbm=hbm, e_=e_, it=it: e.dma_start(
                    out=wb, in_=hbm[e_, it].rearrange("p (d c) -> p d c", d=8)),
                    sem=s_ws[it % 2], k=16)
            p.wait("tensor", s_ws[it % 2])
            b0 = (it % 2) * 2048
            psg0 = psum.f(b0, 512)
            psg1 = psum.f(b0 + 512, 512)[:, 0:128]
            psu0 = psum.f(b0 + 1024, 512)
            psu1 = psum.f(b0 + 1536, 512)[:, 0:128]
            if it >= 2:
                p.wait("tensor", s_ev, p.val(s_ev) - 2)
                p.wait("tensor", s_hh, p.val(s_hh) - 2)
            for dt in range(8):
                last = dt == 7
                p.op("tensor", lambda e, ps=psg0, wb=wgb[it % 2], dt=dt: e.matmul(
                    ps, wb[:, dt, :], xin_fm[:, dt, 0:512],
                    start=(dt == 0), stop=(dt == 7)))
                p.op("tensor", lambda e, ps=psg1, wb=wgb[it % 2], dt=dt: e.matmul(
                    ps, wb[:, dt, :], xin_fm[:, dt, 512:640],
                    start=(dt == 0), stop=(dt == 7)))
                p.op("tensor", lambda e, ps=psu0, wb=wub[it % 2], dt=dt: e.matmul(
                    ps, wb[:, dt, :], xin_fm[:, dt, 0:512],
                    start=(dt == 0), stop=(dt == 7)))
                p.op("tensor", lambda e, ps=psu1, wb=wub[it % 2], dt=dt: e.matmul(
                    ps, wb[:, dt, :], xin_fm[:, dt, 512:640],
                    start=(dt == 0), stop=(dt == 7)),
                    sem=(s_mm if last else None))
            p.wait("scalar", s_mm)
            p.op("scalar", lambda e, ps=psg0, it=it: e.activation(
                h2[:, it, 0:512], ps, AF.Silu, bias=0.0, scale=1.0), sem=s_ev)
            p.op("scalar", lambda e, ps=psg1, it=it: e.activation(
                h2[:, it, 512:640], ps, AF.Silu, bias=0.0, scale=1.0), sem=s_ev)
            p.wait("vector", s_ev)
            p.op("vector", lambda e, ps=psu0, it=it: e.tensor_tensor(
                h2[:, it, 0:512], h2[:, it, 0:512], ps, ALU.mult), sem=s_hh)
            p.op("vector", lambda e, ps=psu1, it=it: e.tensor_tensor(
                h2[:, it, 512:640], h2[:, it, 512:640], ps, ALU.mult), sem=s_hh)
        if e_ == 0:
            p.wait("sync", s_cc, 3)
        for half in range(2):
            p.wait("tensor", s_ev)
            p.wait("tensor", s_hh)
            if half == 1:
                p.wait("tensor", s_xe)
            for it in range(32):
                wb = wdb[it % 2]
                if p.val(s_dm) >= 2:
                    p.wait("sync", s_dm, p.val(s_dm) - 1)
                p.op("sync", lambda e, wb=wb, e_=e_, it=it, half=half: e.dma_start(
                    out=wb, in_=wd_v[e_, it][:, half * 512:(half + 1) * 512]),
                    sem=s_wd[it % 2], k=16)
                p.wait("tensor", s_wd[it % 2])
                for jc in range(5):
                    p.op("tensor", lambda e, jc=jc, wb=wb, it=it: e.matmul(
                        psum.f(jc * 512, 512), h2[:, it, jc * 128:(jc + 1) * 128],
                        wb, start=(it == 0), stop=(it == 31)),
                        sem=(s_dm if jc == 4 else None))
            p.wait("vector", s_dm)
            for jc in range(5):
                p.op("vector", lambda e, jc=jc, half=half: e.tensor_scalar(
                    xe_sb[:, jc, half * 512:(half + 1) * 512],
                    psum.f(jc * 512, 512), gat[:, jc, :], None, ALU.mult,
                    ALU.bypass), sem=s_xe)
        p.wait("gpsimd", s_xe)
        p.wait("gpsimd", s_add)
        if e_ == 0:
            p.wait("gpsimd", wp["s_zr"], wp["zr_total"])
        for jc in range(5):
            p.op("gpsimd", lambda e, jc=jc: e.indirect_dma_start(
                out=moeacc, out_offset=IndirectOffsetOnAxis(
                    ap=idxu[:, jc, :], axis=0),
                in_=xe_sb[:, jc, :], in_offset=None, compute_op=ALU.add),
                sem=s_add, k=16)

    if stage < 4:
        return s_sca

    if dbg and "acc" in dbg:
        p.wait("gpsimd", s_sca, p.val(s_sca))
        p.op("gpsimd", lambda e: e.dma_start(out=dbg["acc"], in_=moeacc), sem=s_sca, k=16)
    yb = [ar.alloc([128, 1024], F32) for _ in range(2)]
    s_yl = [p.sem(pref + "yl0"), p.sem(pref + "yl1")]
    p.wait("sync", s_sca)
    p.wait("sync", s_add)
    for t in range(NT):
        b = yb[t % 2]
        if t >= 2:
            p.wait("sync", s_fin, p.val(s_fin) - 1)
        p.op("sync", lambda e, b=b, t=t: e.dma_start(
            out=b, in_=moeacc.rearrange("(t p) d -> p t d", p=128)[:, t, :]),
            sem=s_yl[t % 2], k=16)
        p.wait("vector", s_yl[t % 2])
        p.op("vector", lambda e, b=b, t=t: e.tensor_tensor(
            x_sb[:, t, :], x_sb[:, t, :], b, ALU.add), sem=s_fin)
    ar.release()
    return s_fin


def l0f_host_inputs(core_id):
    m32 = np.zeros((32, 32), dtype=np.float32)
    for i in range(32):
        for j in range(32):
            if i % 8 == j % 8:
                m32[i, j] = 1.0
    erow = (np.arange(8, dtype=np.float32) * CROW).reshape(8, 1)
    idrow = np.zeros((128, NT, 1), dtype=np.int16)
    for t in range(NT):
        idrow[:, t, 0] = t * 128 + np.arange(128)
    trasht = np.zeros((128, 5, 1), dtype=np.float32)
    for c5 in range(5):
        trasht[:, c5, 0] = TPC + c5 * 128 + np.arange(128)
    return {"m32": m32, "erow": erow, "idrow": idrow,
            "trasht": trasht}



def build_l0a_launch(nchunks=4, stage=9):
    nc = bass.Bass("TRN2", target_bir_lowering=False)
    xin = nc.dram_tensor("xin", [TPC, D], F32, kind="ExternalInput").ap()
    wq = nc.dram_tensor("wq", [1024, 1024], F32, kind="ExternalInput").ap()
    wk = nc.dram_tensor("wk", [1024, 1024], F32, kind="ExternalInput").ap()
    wv = nc.dram_tensor("wv", [1024, 1024], F32, kind="ExternalInput").ap()
    cin = declare_consts(nc)
    xout = nc.dram_tensor("xout", [TPC, D], F32, kind="ExternalOutput").ap()

    p = Plan(nc)
    ar = Arena(nc, p._stack)
    psum = Psum(nc, p._stack)
    cons = load_consts(p, ar, cin)
    s_ld = p.sem("ldx")
    x_sb = ar.alloc([128, NT, 1024], F32)
    p.op("sync", lambda e: e.dma_start(
        out=x_sb, in_=xin.rearrange("(t p) d -> p t d", p=128)), sem=s_ld, k=16)
    for eng in ("tensor", "vector", "scalar"):
        p.wait(eng, s_ld, 16)
    s_done = emit_l0a(p, ar, psum, cons, x_sb, wq, wk, wv, nchunks=nchunks, stage=stage)
    s_st = p.sem("st")
    p.wait("scalar", s_done)
    p.op("scalar", lambda e: e.dma_start(
        out=xout.rearrange("(t p) d -> p t d", p=128), in_=x_sb), sem=s_st, k=16)
    p.wait("scalar", s_st, 16)
    p.run()
    return nc


def l1a_host_inputs(core_slot):
    inv = 1.0 / 10000.0 ** (np.arange(0, 32, 2, dtype=np.float32) / 32.0)
    t = np.arange(SEQ, dtype=np.float32)
    fr = np.outer(t, inv)
    cos, sin = np.cos(fr).astype(np.float32), np.sin(fr).astype(np.float32)
    own = np.concatenate([np.arange(b * BLK, (b + 1) * BLK) for b in BLOCKS[core_slot]])
    masks = np.zeros((4, 128, 512), dtype=np.float32)
    for j in range(4):
        for r in range(128):
            masks[j, r, :] = np.where(np.arange(512) >= r + 128 * j, 0.0, -1e30)
    masks = masks.astype(bfloat16)
    hg = core_slot
    qidx = np.zeros((128, 4, 4), dtype=np.uint32)
    for hi in range(4):
        for r in range(4):
            for pp in range(64):
                qidx[pp, hi, r] = r * 1024 + hg * 256 + hi * 64 + pp
    attnidx = np.zeros((128, 8), dtype=np.uint32)
    for dt in range(8):
        for pp in range(128):
            attnidx[pp, dt] = ((dt // 2) * 128 + pp) * 4 + hg
    return {
        "cos_own": cos[own], "sin_own": sin[own],
        "cos_all": cos, "sin_all": sin,
        "masks": masks, "qidx": qidx, "attnidx": attnidx,
    }


def build_l1a_launch(nheads=4, ngb=16, stage=9):
    nc = bass.Bass("TRN2", target_bir_lowering=False)
    xin = nc.dram_tensor("xin", [TPC, D], F32, kind="ExternalInput").ap()
    w_kvd = nc.dram_tensor("w_kvd", [1024, 128], F32, kind="ExternalInput").ap()
    w_q = nc.dram_tensor("w_q", [1024, 1024], F32, kind="ExternalInput").ap()
    w_qr = nc.dram_tensor("w_qr", [1024, 32], F32, kind="ExternalInput").ap()
    w_kr = nc.dram_tensor("w_kr", [128, 32], F32, kind="ExternalInput").ap()
    w_kup_loc = nc.dram_tensor("w_kup_loc", [128, 256], F32, kind="ExternalInput").ap()
    w_vup_loc = nc.dram_tensor("w_vup_loc", [128, 256], F32, kind="ExternalInput").ap()
    w_o = nc.dram_tensor("w_o", [1024, 1024], F32, kind="ExternalInput").ap()
    cos_own = nc.dram_tensor("cos_own", [TPC, 16], F32, kind="ExternalInput").ap()
    sin_own = nc.dram_tensor("sin_own", [TPC, 16], F32, kind="ExternalInput").ap()
    cos_all = nc.dram_tensor("cos_all", [SEQ, 16], F32, kind="ExternalInput").ap()
    sin_all = nc.dram_tensor("sin_all", [SEQ, 16], F32, kind="ExternalInput").ap()
    masks = nc.dram_tensor("masks", [4, 128, 512], BF16, kind="ExternalInput").ap()
    qidx = nc.dram_tensor("qidx", [128, 4, 4], U32, kind="ExternalInput").ap()
    attnidx = nc.dram_tensor("attnidx", [128, 8], U32, kind="ExternalInput").ap()
    cin = declare_consts(nc)
    xout = nc.dram_tensor("xout", [TPC, D], F32, kind="ExternalOutput").ap()

    p = Plan(nc)
    ar = Arena(nc, p._stack)
    psum = Psum(nc, p._stack)
    cons = load_consts(p, ar, cin)
    s_ld = p.sem("ldx")
    x_sb = ar.alloc([128, NT, 1024], F32)
    p.op("sync", lambda e: e.dma_start(
        out=x_sb, in_=xin.rearrange("(t p) d -> p t d", p=128)), sem=s_ld, k=16)
    for eng in ("tensor", "vector", "scalar"):
        p.wait(eng, s_ld, 16)
    s_done = emit_l1a(p, ar, psum, cons, x_sb, w_kvd, w_q, w_qr, w_kr,
                      w_kup_loc, w_vup_loc, w_o, cos_own, sin_own, cos_all,
                      sin_all, masks, qidx, attnidx, nheads=nheads, ngb=ngb, stage=stage)
    s_st = p.sem("st")
    p.wait("scalar", s_done)
    p.op("scalar", lambda e: e.dma_start(
        out=xout.rearrange("(t p) d -> p t d", p=128), in_=x_sb), sem=s_st, k=16)
    p.wait("scalar", s_st, 16)
    p.run()
    return nc


def build_l1f_launch(tbs=4, stage=9):
    nc = bass.Bass("TRN2", target_bir_lowering=False)
    xin = nc.dram_tensor("xin", [TPC, D], F32, kind="ExternalInput").ap()
    w_fc = nc.dram_tensor("w_fc", [4096, 1024], F32, kind="ExternalInput").ap()
    w_proj = nc.dram_tensor("w_proj", [1024, 4096], F32, kind="ExternalInput").ap()
    cin = declare_consts(nc)
    xout = nc.dram_tensor("xout", [TPC, D], F32, kind="ExternalOutput").ap()

    p = Plan(nc)
    ar = Arena(nc, p._stack)
    psum = Psum(nc, p._stack)
    cons = load_consts(p, ar, cin)
    s_ld = p.sem("ldx")
    x_sb = ar.alloc([128, NT, 1024], F32)
    p.op("sync", lambda e: e.dma_start(
        out=x_sb, in_=xin.rearrange("(t p) d -> p t d", p=128)), sem=s_ld, k=16)
    for eng in ("tensor", "vector", "scalar"):
        p.wait(eng, s_ld, 16)
    wp1 = emit_l1f_wprep(p, ar, psum, cons, w_fc, w_proj)
    s_done = emit_l1f(p, ar, psum, cons, x_sb, wp1, tbs=tbs, stage=stage)
    s_st = p.sem("st")
    p.wait("scalar", s_done)
    p.op("scalar", lambda e: e.dma_start(
        out=xout.rearrange("(t p) d -> p t d", p=128), in_=x_sb), sem=s_st, k=16)
    p.wait("scalar", s_st, 16)
    p.run()
    return nc


def build_l0f_launch(nexp=8, stage=9):
    nc = bass.Bass("TRN2", target_bir_lowering=False)
    xin = nc.dram_tensor("xin", [TPC, D], F32, kind="ExternalInput").ap()
    gate_w = nc.dram_tensor("gate_w", [8, 1024], F32, kind="ExternalInput").ap()
    wg_loc = nc.dram_tensor("wg_loc", [4096, 1024], F32, kind="ExternalInput").ap()
    wu_loc = nc.dram_tensor("wu_loc", [4096, 1024], F32, kind="ExternalInput").ap()
    wd_loc = nc.dram_tensor("wd_loc", [1024, 4096], F32, kind="ExternalInput").ap()
    m32 = nc.dram_tensor("m32", [32, 32], F32, kind="ExternalInput").ap()
    erow = nc.dram_tensor("erow", [8, 1], F32, kind="ExternalInput").ap()
    idrow = nc.dram_tensor("idrow", [128, NT, 1], I16, kind="ExternalInput").ap()
    trasht = nc.dram_tensor("trasht", [128, 5, 1], F32, kind="ExternalInput").ap()
    cin = declare_consts(nc)
    xout = nc.dram_tensor("xout", [TPC, D], F32, kind="ExternalOutput").ap()

    p = Plan(nc)
    ar = Arena(nc, p._stack)
    psum = Psum(nc, p._stack)
    cons = load_consts(p, ar, cin)
    s_ld = p.sem("ldx")
    x_sb = ar.alloc([128, NT, 1024], F32)
    p.op("sync", lambda e: e.dma_start(
        out=x_sb, in_=xin.rearrange("(t p) d -> p t d", p=128)), sem=s_ld, k=16)
    for eng in ("tensor", "vector", "scalar"):
        p.wait(eng, s_ld, 16)
    wp0 = emit_l0f_wprep(p, ar, psum, cons, wg_loc, wu_loc, wd_loc)
    s_done = emit_l0f(p, ar, psum, cons, x_sb, gate_w, wp0,
                      m32, erow, idrow, trasht, nexp=nexp, stage=stage)
    s_st = p.sem("st")
    p.wait("scalar", s_done)
    p.op("scalar", lambda e: e.dma_start(
        out=xout.rearrange("(t p) d -> p t d", p=128), in_=x_sb), sem=s_st, k=16)
    p.wait("scalar", s_st, 16)
    p.run()
    return nc


def build_fused_launch(stage=9):
    nc = bass.Bass("TRN2", target_bir_lowering=False)
    xin = nc.dram_tensor("xin", [TPC, D], F32, kind="ExternalInput").ap()
    wq = nc.dram_tensor("wq", [1024, 1024], F32, kind="ExternalInput").ap()
    wk = nc.dram_tensor("wk", [1024, 1024], F32, kind="ExternalInput").ap()
    wv = nc.dram_tensor("wv", [1024, 1024], F32, kind="ExternalInput").ap()
    gate_w = nc.dram_tensor("gate_w", [8, 1024], F32, kind="ExternalInput").ap()
    wg_loc = nc.dram_tensor("wg_loc", [4096, 1024], F32, kind="ExternalInput").ap()
    wu_loc = nc.dram_tensor("wu_loc", [4096, 1024], F32, kind="ExternalInput").ap()
    wd_loc = nc.dram_tensor("wd_loc", [1024, 4096], F32, kind="ExternalInput").ap()
    m32 = nc.dram_tensor("m32", [32, 32], F32, kind="ExternalInput").ap()
    erow = nc.dram_tensor("erow", [8, 1], F32, kind="ExternalInput").ap()
    idrow = nc.dram_tensor("idrow", [128, NT, 1], I16, kind="ExternalInput").ap()
    trasht = nc.dram_tensor("trasht", [128, 5, 1], F32, kind="ExternalInput").ap()
    w_kvd = nc.dram_tensor("w_kvd", [1024, 128], F32, kind="ExternalInput").ap()
    w_q = nc.dram_tensor("w_q", [1024, 1024], F32, kind="ExternalInput").ap()
    w_qr = nc.dram_tensor("w_qr", [1024, 32], F32, kind="ExternalInput").ap()
    w_kr = nc.dram_tensor("w_kr", [128, 32], F32, kind="ExternalInput").ap()
    w_kup_loc = nc.dram_tensor("w_kup_loc", [128, 256], F32, kind="ExternalInput").ap()
    w_vup_loc = nc.dram_tensor("w_vup_loc", [128, 256], F32, kind="ExternalInput").ap()
    w_o = nc.dram_tensor("w_o", [1024, 1024], F32, kind="ExternalInput").ap()
    cos_own = nc.dram_tensor("cos_own", [TPC, 16], F32, kind="ExternalInput").ap()
    sin_own = nc.dram_tensor("sin_own", [TPC, 16], F32, kind="ExternalInput").ap()
    cos_all = nc.dram_tensor("cos_all", [SEQ, 16], F32, kind="ExternalInput").ap()
    sin_all = nc.dram_tensor("sin_all", [SEQ, 16], F32, kind="ExternalInput").ap()
    masks = nc.dram_tensor("masks", [4, 128, 512], BF16, kind="ExternalInput").ap()
    qidx = nc.dram_tensor("qidx", [128, 4, 4], U32, kind="ExternalInput").ap()
    attnidx = nc.dram_tensor("attnidx", [128, 8], U32, kind="ExternalInput").ap()
    w_fc = nc.dram_tensor("w_fc", [4096, 1024], F32, kind="ExternalInput").ap()
    w_proj = nc.dram_tensor("w_proj", [1024, 4096], F32, kind="ExternalInput").ap()
    cin = declare_consts(nc)
    xout = nc.dram_tensor("xout", [TPC, D], F32, kind="ExternalOutput").ap()

    p = Plan(nc)
    ar = Arena(nc, p._stack)
    psum = Psum(nc, p._stack)
    cons = load_consts(p, ar, cin)
    s_ld = p.sem("ldx")
    x_sb = ar.alloc([128, NT, 1024], F32)
    p.op("sync", lambda e: e.dma_start(
        out=x_sb, in_=xin.rearrange("(t p) d -> p t d", p=128)), sem=s_ld, k=16)
    for eng in ("tensor", "vector", "scalar", "gpsimd"):
        p.wait(eng, s_ld, 16)

    wps = {}

    def prelude():
        wps["wp0"] = emit_l0f_wprep(p, ar, psum, cons, wg_loc, wu_loc, wd_loc)
        wps["wp1"] = emit_l1f_wprep(p, ar, psum, cons, w_fc, w_proj)
        return wps["wp0"]["gates"] + wps["wp1"]["gates"]

    ar.mark()
    s_done = emit_l0a(p, ar, psum, cons, x_sb, wq, wk, wv, stage=stage,
                      prelude=prelude)
    wp0, wp1 = wps["wp0"], wps["wp1"]
    for eng in ENGS:
        p.wait(eng, s_done)
    ar.release()
    ar.mark()
    s_done = emit_l0f(p, ar, psum, cons, x_sb, gate_w, wp0,
                      m32, erow, idrow, trasht, stage=stage)
    for eng in ENGS:
        p.wait(eng, s_done)
    ar.release()
    ar.mark()
    s_done = emit_l1a(p, ar, psum, cons, x_sb, w_kvd, w_q, w_qr, w_kr,
                      w_kup_loc, w_vup_loc, w_o, cos_own, sin_own, cos_all,
                      sin_all, masks, qidx, attnidx, stage=stage)
    for eng in ENGS:
        p.wait(eng, s_done)
    ar.release()
    ar.mark()
    s_done = emit_l1f(p, ar, psum, cons, x_sb, wp1, stage=stage)
    ar.release()
    s_st = p.sem("st")
    p.wait("scalar", s_done)
    p.op("scalar", lambda e: e.dma_start(
        out=xout.rearrange("(t p) d -> p t d", p=128), in_=x_sb), sem=s_st, k=16)
    p.wait("scalar", s_st, 16)
    p.run()
    return nc


FUSED = True

_NC_CACHE = {}


def _get_nc(name, builder):
    if name not in _NC_CACHE:
        _NC_CACHE[name] = builder()
    return _NC_CACHE[name]


def kernel(**inputs):
    f32 = np.float32
    x = np.ascontiguousarray(np.asarray(inputs["L0_attn_x"], dtype=f32))
    cin = const_inputs()
    shards = shard_rows(x)

    if FUSED:
        nc = _get_nc("fused", build_fused_launch)
        wg = np.asarray(inputs["L0_ffn_w_gate"], f32)
        wu = np.asarray(inputs["L0_ffn_w_up"], f32)
        wd = np.asarray(inputs["L0_ffn_w_down"], f32)
        wkup = np.asarray(inputs["L1_attn_W_k_up"], f32)
        wvup = np.asarray(inputs["L1_attn_W_v_up"], f32)
        ca = np.ascontiguousarray
        in_maps = []
        for c in range(NCORES):
            hg = c % 4
            in_maps.append({
                "xin": shards[c],
                "wq": ca(np.asarray(inputs["L0_attn_Wq"], f32)),
                "wk": ca(np.asarray(inputs["L0_attn_Wk"], f32)),
                "wv": ca(np.asarray(inputs["L0_attn_Wv"], f32)),
                "gate_w": ca(np.asarray(inputs["L0_ffn_gate_w"], f32)),
                "wg_loc": ca(wg[c]), "wu_loc": ca(wu[c]), "wd_loc": ca(wd[c]),
                "w_kvd": ca(np.asarray(inputs["L1_attn_W_kv_down"], f32)),
                "w_q": ca(np.asarray(inputs["L1_attn_W_q"], f32)),
                "w_qr": ca(np.asarray(inputs["L1_attn_W_q_rope"], f32)),
                "w_kr": ca(np.asarray(inputs["L1_attn_W_k_rope"], f32)),
                "w_kup_loc": ca(wkup[:, hg * 256:(hg + 1) * 256]),
                "w_vup_loc": ca(wvup[:, hg * 256:(hg + 1) * 256]),
                "w_o": ca(np.asarray(inputs["L1_attn_W_o"], f32)),
                "w_fc": ca(np.asarray(inputs["L1_ffn_W_fc"], f32)),
                "w_proj": ca(np.asarray(inputs["L1_ffn_W_proj"], f32)),
                **l0f_host_inputs(c), **l1a_host_inputs(hg), **cin})
        res = run_bass_kernel_spmd(nc, in_maps, core_ids=list(range(NCORES)))
        return unshard_rows([res.results[c]["xout"] for c in range(NCORES)]
                            ).astype(np.float32)

    nc = _get_nc("l0a", build_l0a_launch)
    in_maps = [{"xin": shards[c],
                "wq": np.ascontiguousarray(np.asarray(inputs["L0_attn_Wq"], f32)),
                "wk": np.ascontiguousarray(np.asarray(inputs["L0_attn_Wk"], f32)),
                "wv": np.ascontiguousarray(np.asarray(inputs["L0_attn_Wv"], f32)),
                **cin} for c in range(NCORES)]
    res = run_bass_kernel_spmd(nc, in_maps, core_ids=list(range(NCORES)))
    shards = [np.ascontiguousarray(res.results[c]["xout"]) for c in range(NCORES)]

    nc = _get_nc("l0f", build_l0f_launch)
    wg = np.asarray(inputs["L0_ffn_w_gate"], f32)
    wu = np.asarray(inputs["L0_ffn_w_up"], f32)
    wd = np.asarray(inputs["L0_ffn_w_down"], f32)
    in_maps = [{"xin": shards[c],
                "gate_w": np.ascontiguousarray(np.asarray(inputs["L0_ffn_gate_w"], f32)),
                "wg_loc": np.ascontiguousarray(wg[c]),
                "wu_loc": np.ascontiguousarray(wu[c]),
                "wd_loc": np.ascontiguousarray(wd[c]),
                **l0f_host_inputs(c), **cin} for c in range(NCORES)]
    res = run_bass_kernel_spmd(nc, in_maps, core_ids=list(range(NCORES)))
    shards = [np.ascontiguousarray(res.results[c]["xout"]) for c in range(NCORES)]

    nc = _get_nc("l1a", build_l1a_launch)
    wkup = np.asarray(inputs["L1_attn_W_k_up"], f32)
    wvup = np.asarray(inputs["L1_attn_W_v_up"], f32)
    in_maps = []
    for c in range(NCORES):
        hg = c % 4
        in_maps.append({
            "xin": shards[c],
            "w_kvd": np.ascontiguousarray(np.asarray(inputs["L1_attn_W_kv_down"], f32)),
            "w_q": np.ascontiguousarray(np.asarray(inputs["L1_attn_W_q"], f32)),
            "w_qr": np.ascontiguousarray(np.asarray(inputs["L1_attn_W_q_rope"], f32)),
            "w_kr": np.ascontiguousarray(np.asarray(inputs["L1_attn_W_k_rope"], f32)),
            "w_kup_loc": np.ascontiguousarray(wkup[:, hg * 256:(hg + 1) * 256]),
            "w_vup_loc": np.ascontiguousarray(wvup[:, hg * 256:(hg + 1) * 256]),
            "w_o": np.ascontiguousarray(np.asarray(inputs["L1_attn_W_o"], f32)),
            **l1a_host_inputs(hg), **cin})
    res = run_bass_kernel_spmd(nc, in_maps, core_ids=list(range(NCORES)))
    shards = [np.ascontiguousarray(res.results[c]["xout"]) for c in range(NCORES)]

    nc = _get_nc("l1f", build_l1f_launch)
    in_maps = [{"xin": shards[c],
                "w_fc": np.ascontiguousarray(np.asarray(inputs["L1_ffn_W_fc"], f32)),
                "w_proj": np.ascontiguousarray(np.asarray(inputs["L1_ffn_W_proj"], f32)),
                **cin} for c in range(NCORES)]
    res = run_bass_kernel_spmd(nc, in_maps, core_ids=list(range(NCORES)))
    shards = [res.results[c]["xout"] for c in range(NCORES)]

    return unshard_rows(shards).astype(np.float32)
